# version 13
# baseline (speedup 1.0000x reference)
import numpy as np
from contextlib import ExitStack

import concourse.bass as bass
import concourse.mybir as mybir
from concourse.bass_utils import run_bass_kernel_spmd

F32 = mybir.dt.float32
F32R = mybir.dt.float32r
BF16 = mybir.dt.bfloat16
AF = mybir.ActivationFunctionType
Al = mybir.AluOpType
AX = mybir.AxisListType.X

T = 2048
D = 1024
NCH = 8
HE = 1408
HCH = 11
CH = 128
NCK = T // CH
HEADS, DK, DV = 16, 32, 64
FI = 4096
EPS = 1e-5
MAGIC = float(np.float32(1.5 * 2 ** 23))
LN_DKS = float(np.log(DK ** -0.5))

TAPS = ()
PHASES = 99
DBGOUT = False



def _pack_cols(*arrs):
    cols = []
    for a in arrs:
        a = np.ascontiguousarray(np.asarray(a, np.float32))
        if a.ndim == 2 and a.shape[0] == 128 and a.shape[1] <= 16:
            cols.append(np.ascontiguousarray(a))
            continue
        a = a.reshape(-1)
        if a.size <= 128:
            c = np.zeros(128, np.float32)
            c[: a.size] = a
            cols.append(c[:, None])
        else:
            assert a.size % 128 == 0
            cols.append(a.reshape(-1, 128).T)
    return np.ascontiguousarray(np.concatenate(cols, axis=1))


def _cpack_index():
    sizes = {
        "b1": 8 * HCH, "b3": 8 * HCH, "b2": 8 * NCH,
        "sb1": HCH, "sb3": HCH, "sb2": NCH,
        "lnw": NCH, "lnb": NCH, "gb": 1, "gbrow": 8,
        "l1b1": 16, "l1b2": NCH, "flag": 1,
    }
    off, idx = 0, {}
    for k, s in sizes.items():
        idx[k] = off
        off += s
    return idx, off


CPI, CP_COLS = _cpack_index()


def _ctile(w, nk, nm):
    return np.ascontiguousarray(
        w.reshape(nk, 128, nm, 128).transpose(2, 1, 0, 3).reshape(nm, 128, nk * 128))


def _rtile(w, nk, nn):
    return np.ascontiguousarray(
        w.reshape(nk, 128, nn, 512).transpose(0, 2, 1, 3))


def _host_layout(inputs):
    inp = {k: np.ascontiguousarray(np.asarray(v, np.float32)) for k, v in inputs.items()}
    filt = inp["L0_attn_filt"]
    fpad = np.zeros(128 * 17 + 127, np.float32)
    fpad[127:127 + 2048] = filt
    d_i = np.arange(17)[:, None, None]
    q_i = np.arange(128)[None, :, None]
    p_i = np.arange(128)[None, None, :]
    toep = fpad[127 + 128 * d_i + q_i - p_i]
    toep_t = np.ascontiguousarray(toep.transpose(1, 0, 2).reshape(128, 17 * 128))

    gkw2e = np.ascontiguousarray(np.concatenate(
        [inp["L1_attn_gk_w2"], inp["L1_attn_gk_b2"][None, :]], axis=0))
    gate_wT = np.ascontiguousarray(
        inp["L0_ffn_gate_w"].T.reshape(8, 128, 8).transpose(1, 0, 2).reshape(128, 64))
    gkw1t = np.ascontiguousarray(
        inp["L1_attn_gk_w1"].reshape(8, 128, 16).transpose(1, 0, 2).reshape(128, 128))
    rms_wb = np.ascontiguousarray(
        np.broadcast_to(inp["L1_attn_rms_w"][None, :], (128, DV)))

    w1t = np.concatenate([_ctile(inp["L0_ffn_w1"][e], 8, HCH) for e in range(8)]
                         + [_ctile(inp["L0_ffn_sw1"], 8, HCH)])
    w3t = np.concatenate([_ctile(inp["L0_ffn_w3"][e], 8, HCH) for e in range(8)]
                         + [_ctile(inp["L0_ffn_sw3"], 8, HCH)])
    w2t = np.concatenate([_ctile(inp["L0_ffn_w2"][e], HCH, 8) for e in range(8)]
                         + [_ctile(inp["L0_ffn_sw2"], HCH, 8)])

    common = {
        "toep": toep_t, "gkw2e": gkw2e, "gate_wT": gate_wT, "rms_wb": rms_wb,
        "w1": w1t, "w3": w3t, "w2": w2t,
        "Wq": _ctile(inp["L1_attn_Wq"], 8, 4), "Wk": _ctile(inp["L1_attn_Wk"], 8, 4),
        "Wv": _rtile(inp["L1_attn_Wv"], 8, 2), "Wg": _rtile(inp["L1_attn_Wg"], 8, 2),
        "Wo": _ctile(inp["L1_attn_Wo"], 8, 8),
        "gkw1": gkw1t,
        "W1": _ctile(inp["L1_attn_W1"], 8, 16), "W2": _ctile(inp["L1_attn_W2"], 16, 8),
        "wgateT": np.ascontiguousarray(inp["L1_ffn_w_gate"].T),
        "wupT": np.ascontiguousarray(inp["L1_ffn_w_up"].T),
        "wdownT": np.ascontiguousarray(inp["L1_ffn_w_down"].T),
    }
    x = inp["L0_attn_x"]
    in_maps = []
    for core in range(8):
        b, half = core // 2, core % 2
        x_ext = np.zeros((4096, D), np.float32)
        avail = 4096 - half * 2048
        x_ext[:avail] = x[b, half * 2048: half * 2048 + avail]
        cpack = _pack_cols(
            inp["L0_ffn_b1"], inp["L0_ffn_b3"], inp["L0_ffn_b2"],
            inp["L0_ffn_sb1"], inp["L0_ffn_sb3"], inp["L0_ffn_sb2"],
            inp["L1_attn_ln_w"], inp["L1_attn_ln_b"],
            inp["L0_ffn_gate_b"],
            np.broadcast_to(inp["L0_ffn_gate_b"][None, :], (128, 8)),
            inp["L1_attn_b1"], inp["L1_attn_b2"],
            np.full(128, float(half), np.float32),
        )
        m = dict(common)
        m["x_ext"] = x_ext
        m["cpack"] = cpack
        in_maps.append(m)
    return in_maps



ENGINES = ("sync", "gpsimd", "tensor", "vector", "scalar")


class SemT:
    def __init__(self, ctx, nc, name):
        self.h = ctx.enter_context(nc.semaphore(name))
        self.v = 0


class Prog:
    def __init__(self, nc):
        self.nc = nc
        self.streams = {e: [] for e in ENGINES}

    def em(self, eng, fn):
        self.streams[eng].append(fn)

    def wait(self, eng, sem, val=None):
        v = sem.v if val is None else val
        if v > 0:
            self.streams[eng].append(lambda q, s=sem, v=v: q.wait_ge(s.h, v))

    def drain(self, eng):
        self.streams[eng].append(lambda q: q.drain())

    def ei(self, eng, sem, fn, k=1):
        if sem is None:
            self.streams[eng].append(fn)
            return None
        sem.v += k
        self.streams[eng].append(lambda q, fn=fn, sem=sem, k=k: fn(q).then_inc(sem.h, k))
        return sem.v

    def run(self):
        with self.nc.Block() as block:
            for eng in ENGINES:
                fns = self.streams[eng]
                if not fns:
                    continue
                getattr(block, eng)(lambda q, fns=fns: [f(q) for f in fns])



class Builder:
    def __init__(self):
        self.nc = bass.Bass("TRN2", target_bir_lowering=False)
        nc = self.nc
        self.dram = {}

        def din(name, shape, dtype=F32R):
            self.dram[name] = nc.dram_tensor(name, list(shape), dtype,
                                             kind="ExternalInput").ap()
        din("x_ext", (4096, D))
        din("toep", (128, 17 * 128))
        din("gate_wT", (128, 64))
        din("w1", (99, 128, 1024)); din("w3", (99, 128, 1024))
        din("w2", (72, 128, HE))
        din("Wq", (4, 128, 1024)); din("Wk", (4, 128, 1024))
        din("Wv", (8, 2, 128, 512)); din("Wg", (8, 2, 128, 512))
        din("Wo", (8, 128, 1024)); din("gkw1", (128, 128)); din("gkw2e", (17, 512))
        din("W1", (16, 128, 1024)); din("W2", (8, 128, 2048))
        din("wgateT", (D, FI)); din("wupT", (D, FI)); din("wdownT", (FI, D))
        din("cpack", (128, CP_COLS), F32)
        din("rms_wb", (128, DV), F32)
        self.out = nc.dram_tensor("out", [T, D], F32, kind="ExternalOutput").ap()
        self.taps = {t: nc.dram_tensor("tap_" + t, [T, D], F32,
                                       kind="ExternalOutput").ap() for t in TAPS}
        self.x2_spill = nc.dram_tensor("x2_spill", [128, NCH * T], F32R).ap()
        self.cc_in = nc.dram_tensor("cc_in", [128, 1024], F32).ap()
        self.cc_out = nc.dram_tensor("cc_out", [128, 1024], F32).ap()
        self.qst_d = nc.dram_tensor("qst_d", [128, NCK * 512], BF16).ap()
        qk = "ExternalOutput" if DBGOUT else "Internal"
        self.qwg_t = nc.dram_tensor("qwg_t", [32, 128, 1024], BF16, kind=qk).ap()
        self.qwu_t = nc.dram_tensor("qwu_t", [32, 128, 1024], BF16, kind=qk).ap()
        self.qwd_t = nc.dram_tensor("qwd_t", [NCH, 128, FI], F32R, kind=qk).ap()
        if DBGOUT:
            self.dbg_swc = nc.dram_tensor("dbg_swc", [128, 16], F32, kind=qk).ap()
            self.dbg_mc = nc.dram_tensor("dbg_mc", [128, 16], F32, kind=qk).ap()
            self.dbg_tern = nc.dram_tensor("dbg_tern", [128, 2048], F32, kind=qk).ap()
            self.dbg_wt = nc.dram_tensor("dbg_wt", [128, 8192], F32, kind=qk).ap()

        self.ctx = ExitStack()
        self.sb = {}

    def sbuf(self, name, shape, dtype):
        t = self.ctx.enter_context(self.nc.sbuf_tensor(name, list(shape), dtype))
        self.sb[name] = t
        return t

    def sbuf_local(self, name, shape, dtype):
        t = self.phase_ctx.enter_context(self.nc.sbuf_tensor(name, list(shape), dtype))
        self.sb[name] = t
        return t

    def cpc(self, key, j=0):
        return self.CP[:, CPI[key] + j: CPI[key] + j + 1]

    def build(self):
        nc = self.nc
        with self.ctx, nc.allow_low_precision(
                "f32r (12-bit significand) stream accumulation is intentional; "
                "precision exceeds the sanctioned bf16+fp32 bar"):
            self.X = self.sbuf("X", [128, NCH * T], F32R)
            self.ACC = self.sbuf("ACC", [128, NCH * T], F32R)
            save_base = nc.sbuf_base
            nc.sbuf_base = save_base - NCH * T * 4
            self.ACCB = nc.sbuf_tensor("ACCB", [128, 2 * NCH * T], BF16).__enter__()
            assert nc.sbuf_base == save_base
            self.CP = self.sbuf("CP", [128, CP_COLS], F32)
            self.IDR = self.sbuf("IDR", [128, 128], F32R)
            self.IDB = self.sbuf("IDB", [128, 128], BF16)
            self.MASKB = self.sbuf("MASKB", [128, 128], BF16)
            self.ONECOL = self.sbuf("ONECOL", [128, 1], F32R)
            self.ONEROW = self.sbuf("ONEROW", [1, 128], F32R)
            self.ROWA = self.sbuf("ROWA", [1, T], F32R)
            self.ROWB = self.sbuf("ROWB", [1, T], F32R)
            self.XSQ = self.sbuf("XSQ", [128, 512], F32R)
            self.NTMP = self.sbuf("NTMP", [128, 512], F32R)
            self.PS = [self.ctx.enter_context(nc.psum_tensor(f"PS{i}", [128, 512], F32))
                       for i in range(8)]
            self.sem = {n: SemT(self.ctx, nc, n) for n in
                        ["sld", "sldB", "sld2", "sld2B", "gld", "gldB", "gq", "gqB",
                         "gk2", "guc", "srw", "sst", "mmA", "mmB", "vA", "vB", "scA",
                         "cc", "ginit"]}
            phases = [self.phase_init]
            if PHASES >= 1:
                phases += [self.phase_conv, lambda: self.tap("x1")]
            if PHASES >= 2:
                phases += [self.phase_moe, lambda: self.tap("x2")]
            if PHASES >= 3:
                phases += [self.phase_gla_d0, self.phase_gla_main,
                           self.phase_gla_post, lambda: self.gla_ctx.close(),
                           lambda: self.tap("xp")]
            if PHASES >= 4:
                phases += [self.phase_mlp, lambda: self.tap("x3")]
            if PHASES >= 5:
                phases += [self.phase_bitffn_prep, self.phase_bitffn,
                           lambda: self.bit_ctx.close(), lambda: self.tap("fin2")]
            if PHASES >= 6:
                phases += [self.phase_final]
            for ph in phases:
                with ExitStack() as pc:
                    self.phase_ctx = pc
                    name = getattr(ph, "__name__", "misc")
                    with nc.named_scope(name):
                        ph()
        return nc

    def phase_init(self):
        p = Prog(self.nc)
        s = self.sem
        IOTA = self.sbuf_local("IOTA", [128, 128], F32)
        p.ei("sync", s["sld"], lambda q: q.dma_start(out=self.CP[:, :], in_=self.dram["cpack"]), 16)

        gi = p.ei("gpsimd", s["ginit"], lambda q: q.iota(
            IOTA[:, :], [[1, 128]], base=0, channel_multiplier=-1,
            allow_small_or_imprecise_dtypes=True))
        p.wait("vector", s["ginit"], gi)
        p.ei("vector", s["vA"], lambda q: q.tensor_single_scalar(self.IDR[:, :], IOTA[:, :], 0.0, Al.is_equal))
        p.ei("vector", s["vA"], lambda q: q.tensor_single_scalar(self.IDB[:, :], IOTA[:, :], 0.0, Al.is_equal))
        p.ei("vector", s["vA"], lambda q: q.tensor_single_scalar(self.MASKB[:, :], IOTA[:, :], 0.0, Al.is_ge))
        p.ei("vector", s["vA"], lambda q: q.tensor_single_scalar(
            self.ONECOL[:, :], IOTA[:, 0:1], -1e30, Al.is_ge))
        p.ei("vector", s["vA"], lambda q: q.tensor_single_scalar(
            self.ONEROW[:, :], IOTA[0:1, 0:128], -1e30, Al.is_ge))
        p.run()

    def ln_stats2(self, p, buf, row_a, row_b, sq=None):
        s = self.sem
        sq = self.ACC if sq is None else sq
        mmB_prev = 0
        for t in range(8):
            tof = t * 256
            p.wait("tensor", s["vA"])
            def mksum(q, tof=tof):
                for c in range(NCH):
                    ins = q.matmul(self.PS[6][0:1, 0:256], self.ONECOL[:, :],
                                   buf[:, c * T + tof: c * T + tof + 256],
                                   start=(c == 0), stop=(c == NCH - 1))
                return ins
            mv = p.ei("tensor", s["mmA"], mksum)
            p.wait("vector", s["mmA"], mv)
            p.drain("vector")
            p.ei("vector", s["vA"], lambda q, tof=tof: q.tensor_copy(
                row_a[0:1, tof: tof + 256], self.PS[6][0:1, 0:256]))
            p.wait("scalar", s["mmB"], mmB_prev)
            p.drain("scalar")
            for c in range(NCH):
                scv = p.ei("scalar", s["scA"], lambda q, c=c, tof=tof: q.activation(
                    sq[:, c * 256:(c + 1) * 256],
                    buf[:, c * T + tof: c * T + tof + 256], AF.Square))
            p.wait("tensor", s["scA"], scv)
            def mksq(q):
                for c in range(NCH):
                    ins = q.matmul(self.PS[7][0:1, 0:256], self.ONECOL[:, :],
                                   sq[:, c * 256:(c + 1) * 256],
                                   start=(c == 0), stop=(c == NCH - 1))
                return ins
            mmB_prev = p.ei("tensor", s["mmB"], mksq)
            p.wait("vector", s["mmB"], mmB_prev)
            p.drain("vector")
            p.ei("vector", s["vA"], lambda q, tof=tof: q.tensor_copy(
                row_b[0:1, tof: tof + 256], self.PS[7][0:1, 0:256]))

    def ln_rows_finalize(self, p, row_a, row_b, n=D):
        s = self.sem
        p.drain("vector")
        p.ei("vector", s["vA"], lambda q: q.tensor_scalar(
            row_a[0:1, :], row_a[0:1, :], 1.0 / n, None, Al.mult))
        p.drain("vector")
        p.ei("vector", s["vA"], lambda q: q.tensor_scalar(
            row_b[0:1, :], row_b[0:1, :], 1.0 / n, EPS, Al.mult, Al.add))
        for seg in range(4):
            sl = slice(seg * 512, seg * 512 + 512)
            p.drain("vector")
            p.ei("vector", s["vA"], lambda q, sl=sl: q.scalar_tensor_tensor(
                self.XSQ[0:1, 0:512], row_a[0:1, sl], -1.0, row_a[0:1, sl],
                Al.mult, Al.mult))
            p.drain("vector")
            p.ei("vector", s["vA"], lambda q, sl=sl: q.tensor_tensor(
                row_b[0:1, sl], row_b[0:1, sl], self.XSQ[0:1, 0:512], Al.add))
        p.drain("vector")
        vv = p.ei("vector", s["vA"], lambda q: q.reciprocal(
            row_b[0:1, :], row_b[0:1, :]))
        p.wait("scalar", s["vA"], vv)
        p.drain("scalar")
        scv = p.ei("scalar", s["scA"], lambda q: q.activation(
            row_b[0:1, :], row_b[0:1, :], AF.Sqrt))
        p.wait("vector", s["scA"], scv)
        p.drain("vector")
        p.ei("vector", s["vA"], lambda q: q.scalar_tensor_tensor(
            row_a[0:1, :], row_a[0:1, :], -1.0, row_b[0:1, :], Al.mult, Al.mult))

    def ln_norm(self, p, src, dst, row_rstd, row_mrstd, affine=None, dst_dtype=None):
        s = self.sem
        p.wait("tensor", s["vA"], s["vA"].v)
        p.wait("tensor", s["scA"], s["scA"].v)
        for t in range(4):
            tof = t * 512
            def mkb(q, tof=tof):
                q.matmul(self.PS[4][:, :], self.ONEROW[0:1, :],
                         row_rstd[0:1, tof: tof + 512].bitcast(F32R), start=True, stop=True)
                return q.matmul(self.PS[5][:, :], self.ONEROW[0:1, :],
                                row_mrstd[0:1, tof: tof + 512].bitcast(F32R), start=True, stop=True)
            mv = p.ei("tensor", s["mmA"], mkb)
            p.wait("vector", s["mmA"], mv)
            for c in range(NCH):
                p.drain("vector")
                p.ei("vector", s["vA"], lambda q, c=c, tof=tof: q.tensor_tensor(
                    self.NTMP[:, :], src[:, c * T + tof: c * T + tof + 512],
                    self.PS[4][:, :], Al.mult))
                p.drain("vector")
                if affine is None:
                    vv = p.ei("vector", s["vA"], lambda q, c=c, tof=tof: q.tensor_tensor(
                        dst[:, c * T + tof: c * T + tof + 512], self.NTMP[:, :],
                        self.PS[5][:, :], Al.add))
                else:
                    wk, bk = affine
                    p.ei("vector", s["vA"], lambda q, c=c, tof=tof: q.tensor_tensor(
                        self.NTMP[:, :], self.NTMP[:, :], self.PS[5][:, :], Al.add))
                    p.drain("vector")
                    vv = p.ei("vector", s["vA"], lambda q, c=c, tof=tof, wk=wk, bk=bk: q.tensor_scalar(
                        dst[:, c * T + tof: c * T + tof + 512], self.NTMP[:, :],
                        self.cpc(wk, c), self.cpc(bk, c), Al.mult, Al.add))
            p.wait("tensor", s["vA"], s["vA"].v)

    def phase_conv(self):
        nc, s = self.nc, self.sem
        p = Prog(nc)
        RING = 18
        xr0 = self.sbuf_local("xre0", [128, D], F32R)
        xr1 = self.sbuf_local("xre1", [128, D], F32R)
        TOEP = self.sbuf_local("TOEP", [128, 17 * 128], F32R)
        xtm = self.sbuf_local("xtm_b", [128, D], F32R)
        stt = self.sbuf_local("stt_b", [128, 4], F32)
        sq1 = self.sbuf_local("sq1_b", [128, D], F32)
        cvE = SemT(self.ctx, nc, "cvE")

        def ring(j):
            sl = j % RING
            if sl < 16:
                return self.ACC[:, sl * D:(sl + 1) * D]
            return (xr0 if sl == 16 else xr1)[:, :]

        def emit_load(j):
            if j >= RING:
                p.wait("sync", cvE, j - RING + 1)
            p.ei("sync", s["sld"], lambda q, j=j: q.dma_start(
                out=ring(j), in_=self.dram["x_ext"][j * 128:(j + 1) * 128, :]), 16)

        p.ei("sync", s["sld"], lambda q: q.dma_start(
            out=TOEP[:, :], in_=self.dram["toep"]), 16)
        for j in range(17):
            emit_load(j)

        mmA = s["mmA"]
        for i in range(16):
            if i + 17 < 32:
                emit_load(i + 17)
            b0, b1 = (i % 2) * 2, (i % 2) * 2 + 1
            if i >= 2:
                p.wait("tensor", s["vA"], self.conv_ep_marks[i - 2])
            p.wait("tensor", s["sld"])
            def mkconv(q, i=i, b0=b0, b1=b1):
                for d in range(17):
                    st, sp = (d == 0), (d == 16)
                    lhsT = TOEP[:, d * 128:(d + 1) * 128]
                    src = ring(i + d)
                    q.matmul(self.PS[b0][:, :], lhsT, src[:, 0:512], start=st, stop=sp)
                    ins = q.matmul(self.PS[b1][:, :], lhsT, src[:, 512:1024], start=st, stop=sp)
                return ins
            mv = p.ei("tensor", mmA, mkconv)
            p.wait("vector", mmA, mv)
            p.drain("vector")
            p.ei("vector", s["vA"], lambda q, i=i, b0=b0: q.tensor_tensor(
                xtm[:, 0:512], ring(i)[:, 0:512], self.PS[b0][:, :], Al.add))
            p.drain("vector")
            vv = p.ei("vector", cvE, lambda q, i=i, b1=b1: q.tensor_tensor(
                xtm[:, 512:1024], ring(i)[:, 512:1024], self.PS[b1][:, :], Al.add))
            p.wait("scalar", cvE, vv)
            p.drain("scalar")
            scv = p.ei("scalar", s["scA"], lambda q: q.activation(sq1[:, :], xtm[:, :], AF.Square))
            p.wait("vector", s["scA"], scv)
            p.drain("vector")
            p.ei("vector", s["vA"], lambda q: q.reduce_sum(stt[:, 0:1], xtm[:, :], AX))
            p.drain("vector")
            p.ei("vector", s["vA"], lambda q: q.reduce_sum(stt[:, 1:2], sq1[:, :], AX))
            p.drain("vector")
            p.ei("vector", s["vA"], lambda q: q.tensor_scalar(
                stt[:, 0:1], stt[:, 0:1], 1.0 / D, None, Al.mult))
            p.drain("vector")
            p.ei("vector", s["vA"], lambda q: q.scalar_tensor_tensor(
                stt[:, 2:3], stt[:, 0:1], -1.0, stt[:, 0:1], Al.mult, Al.mult))
            p.drain("vector")
            p.ei("vector", s["vA"], lambda q: q.tensor_scalar(
                stt[:, 1:2], stt[:, 1:2], 1.0 / D, EPS, Al.mult, Al.add))
            p.drain("vector")
            p.ei("vector", s["vA"], lambda q: q.tensor_tensor(
                stt[:, 1:2], stt[:, 1:2], stt[:, 2:3], Al.add))
            p.drain("vector")
            vv = p.ei("vector", s["vA"], lambda q: q.reciprocal(stt[:, 2:3], stt[:, 1:2]))
            p.wait("scalar", s["vA"], vv)
            p.drain("scalar")
            scv = p.ei("scalar", s["scA"], lambda q: q.activation(
                stt[:, 3:4], stt[:, 2:3], AF.Sqrt))
            p.wait("vector", s["scA"], scv)
            p.drain("vector")
            vv = p.ei("vector", s["vA"], lambda q: q.tensor_scalar(
                xtm[:, :], xtm[:, :], stt[:, 0:1], stt[:, 3:4], Al.subtract, Al.mult))
            p.wait("tensor", s["vA"], vv)
            for c in range(NCH):
                bk = 4 + (c % 2)
                p.wait("tensor", s["vB"], s["vB"].v - 1)
                mv2 = p.ei("tensor", s["mmB"], lambda q, c=c, bk=bk: q.transpose(
                    self.PS[bk][:, 0:128].bitcast(F32R), xtm[:, c * 128:(c + 1) * 128],
                    self.IDR[:, :]))
                p.wait("vector", s["mmB"], mv2)
                p.drain("vector")
                p.ei("vector", s["vB"], lambda q, c=c, bk=bk, i=i: q.tensor_copy(
                    self.X[:, c * T + i * 128: c * T + (i + 1) * 128], self.PS[bk][:, 0:128]))
            self.conv_ep_marks[i] = s["vA"].v
        p.wait("tensor", s["vB"], s["vB"].v)
        p.run()

    def tap(self, name):
        if name not in self.taps:
            return
        nc, s = self.nc, self.sem
        p = Prog(nc)
        ttile = self.sbuf_local("tap_tm_" + name, [128, D], F32)
        for i in range(16):
            p.wait("vector", s["sst"])
            for c in range(NCH):
                bk = 6 + (c % 2)
                p.wait("tensor", s["vB"], s["vB"].v - 1)
                mv = p.ei("tensor", s["mmB"], lambda q, i=i, c=c, bk=bk: q.transpose(
                    self.PS[bk][:, 0:128].bitcast(F32R),
                    self.X[:, c * T + i * 128: c * T + (i + 1) * 128], self.IDR[:, :]))
                p.wait("vector", s["mmB"], mv)
                p.drain("vector")
                p.ei("vector", s["vB"], lambda q, c=c, bk=bk: q.tensor_copy(
                    ttile[:, c * 128:(c + 1) * 128], self.PS[bk][:, 0:128]))
            p.wait("scalar", s["vB"], s["vB"].v)
            p.ei("scalar", s["sst"], lambda q, i=i: q.dma_start(
                out=self.taps[name][i * 128:(i + 1) * 128, :], in_=ttile[:, :]), 16)
            p.wait("scalar", s["sst"], s["sst"].v)
        p.run()


    def phase_moe(self):
        nc, s = self.nc, self.sem
        p = Prog(nc)
        GWT = self.sbuf_local("GWT", [128, 64], F32R)
        RW = self.sbuf_local("RW", [8, T], BF16)
        RWSB = self.sbuf_local("RWSB", [128, 2 * 1024], BF16)
        RWROW = self.sbuf_local("RWROW", [1, 512], BF16)
        ONEROWB = self.sbuf_local("ONEROWB", [1, 128], BF16)
        WAB = self.sbuf_local("WAB", [128, 2048], F32R)
        W3B = self.sbuf_local("W3B", [128, 2048], F32R)
        W2B = self.sbuf_local("W2B", [128, 2 * 1408], BF16)
        HB = self.sbuf_local("HB", [128, HCH * 1024], BF16)
        VS1 = self.XSQ
        VS2 = self.NTMP

        p.ei("vector", s["vA"], lambda q: q.tensor_copy(
            ONEROWB[0:1, :], self.ONEROW[0:1, :]))
        for c in range(NCH):
            p.drain("vector")
            p.ei("vector", s["vA"], lambda q, c=c: q.tensor_copy(
                self.ACC[:, c * T:(c + 1) * T], self.X[:, c * T:(c + 1) * T]))

        p.wait("sync", s["mmA"], s["mmA"].v)
        gv = p.ei("sync", s["sld"], lambda q: q.dma_start(
            out=GWT[:, :], in_=self.dram["gate_wT"]), 16)
        if True:
            GROW = self.XSQ[0:8, 0:512].bitcast(F32R)
            GTM = self.NTMP[:, 0:40]
            p.wait("tensor", s["sld"], gv)
            for t in range(4):
                tof = t * 512
                def mkg(q, tof=tof):
                    for c in range(NCH):
                        ins = q.matmul(self.PS[6][0:8, :], GWT[:, c * 8:(c + 1) * 8],
                                       self.X[:, c * T + tof: c * T + tof + 512],
                                       start=(c == 0), stop=(c == NCH - 1))
                    return ins
                mv = p.ei("tensor", s["mmA"], mkg)
                p.wait("scalar", s["mmA"], mv)
                p.drain("scalar")
                scv = p.ei("scalar", s["scA"], lambda q: q.copy(GROW[0:8, :], self.PS[6][0:8, :]))
                p.wait("tensor", s["scA"], scv)
                for sc in range(4):
                    mv = p.ei("tensor", s["mmB"], lambda q, sc=sc: q.transpose(
                        self.PS[7][0:128, 0:8].bitcast(F32R), GROW[0:8, sc * 128:(sc + 1) * 128],
                        self.IDR[0:8, 0:8]))
                    p.wait("vector", s["mmB"], mv)
                    p.drain("vector")
                    p.ei("vector", s["vA"], lambda q: q.tensor_copy(
                        GTM[:, 0:8], self.PS[7][0:128, 0:8]))
                    p.drain("vector")
                    p.ei("vector", s["vA"], lambda q: q.reduce_max(
                        GTM[:, 32:33], GTM[:, 0:8], AX))
                    p.drain("vector")
                    p.ei("vector", s["vA"], lambda q: q.tensor_scalar(
                        GTM[:, 8:16], GTM[:, 0:8], GTM[:, 32:33].bitcast(F32), None, Al.subtract))
                    vv = s["vA"].v
                    p.wait("scalar", s["vA"], vv)
                    p.drain("scalar")
                    scv = p.ei("scalar", s["scA"], lambda q: q.activation(
                        GTM[:, 8:16], GTM[:, 8:16], AF.Exp))
                    p.wait("vector", s["scA"], scv)
                    p.drain("vector")
                    p.ei("vector", s["vA"], lambda q: q.reduce_sum(
                        GTM[:, 33:34], GTM[:, 8:16], AX))
                    p.drain("vector")
                    p.ei("vector", s["vA"], lambda q: q.reciprocal(
                        GTM[:, 34:35], GTM[:, 33:34]))
                    p.drain("vector")
                    p.ei("vector", s["vA"], lambda q: q.tensor_scalar(
                        GTM[:, 8:16], GTM[:, 8:16], GTM[:, 34:35].bitcast(F32), None, Al.mult))
                    p.drain("vector")
                    p.ei("vector", s["vA"], lambda q: q.tensor_tensor(
                        GTM[:, 0:8], GTM[:, 8:16],
                        self.CP[:, CPI["gbrow"]: CPI["gbrow"] + 8], Al.add))
                    p.drain("vector")
                    p.ei("vector", s["vA"], lambda q: q.reduce_max(
                        GTM[:, 32:33], GTM[:, 0:8], AX))
                    p.drain("vector")
                    p.ei("vector", s["vA"], lambda q: q.tensor_scalar(
                        GTM[:, 16:24], GTM[:, 0:8], GTM[:, 32:33].bitcast(F32), None, Al.is_equal))
                    p.drain("vector")
                    p.ei("vector", s["vA"], lambda q: q.scalar_tensor_tensor(
                        GTM[:, 16:24], GTM[:, 16:24], -1e30, GTM[:, 0:8], Al.mult, Al.add))
                    p.drain("vector")
                    p.ei("vector", s["vA"], lambda q: q.reduce_max(
                        GTM[:, 32:33], GTM[:, 16:24], AX))
                    p.drain("vector")
                    p.ei("vector", s["vA"], lambda q: q.tensor_scalar(
                        GTM[:, 16:24], GTM[:, 0:8], GTM[:, 32:33].bitcast(F32), None, Al.is_ge))
                    p.drain("vector")
                    vv = p.ei("vector", s["vA"], lambda q: q.tensor_tensor(
                        GTM[:, 24:32], GTM[:, 8:16], GTM[:, 16:24], Al.mult))
                    p.wait("tensor", s["vA"], vv)
                    mv = p.ei("tensor", s["mmB"], lambda q: q.transpose(
                        self.PS[7][0:8, 0:128].bitcast(F32R), GTM[:, 24:32].bitcast(F32R),
                        self.IDR[:, :]))
                    p.wait("vector", s["mmB"], mv)
                    p.drain("vector")
                    vv = p.ei("vector", s["vA"], lambda q, tof=tof, sc=sc: q.tensor_copy(
                        RW[0:8, tof + sc * 128: tof + (sc + 1) * 128], self.PS[7][0:8, 0:128]))
                    p.wait("tensor", s["vA"], vv)
                p.wait("tensor", s["scA"], s["scA"].v)

        mmA = s["mmA"]

        def w1_slice(e, mh):
            return self.dram["w1"][e * HCH + mh]

        def w3_slice(e, mh):
            return self.dram["w3"][e * HCH + mh]

        def w2_slice(e, md):
            return self.dram["w2"][e * 8 + md]

        mm0 = mmA.v
        a_marks = []
        b_marks = []
        ep_v = []
        ep_s = []
        bep_v = []
        rw_s = []
        rw_units = [(h, e) for h in range(2) for e in range(8)]
        rw_ptr = [0]

        def emit_rw(p):
            if rw_ptr[0] >= len(rw_units):
                return
            h_t, e_t = rw_units[rw_ptr[0]]
            eslot = rw_ptr[0] % 2
            rw_ptr[0] += 1
            tof_t = h_t * 1024
            if ep_v:
                p.wait("tensor", s["vA"], ep_v[-1])
            else:
                p.wait("tensor", s["vA"], s["vA"].v)
            for sub, bank in ((0, 6), (1, 7)):
                p.wait("scalar", s["mmB"], s["mmB"].v)
                rv = p.ei("scalar", s["srw"], lambda q, e_t=e_t, tof_t=tof_t, sub=sub:
                          q.dma_start(
                    out=RWROW[0:1, :],
                    in_=RW[e_t:e_t + 1, tof_t + sub * 512: tof_t + sub * 512 + 512]), 16)
                p.wait("tensor", s["srw"], rv)
                mv = p.ei("tensor", s["mmB"], lambda q, bank=bank: q.matmul(
                    self.PS[bank][:, :], ONEROWB[0:1, :],
                    RWROW[0:1, :], start=True, stop=True))
                p.wait("scalar", s["mmB"], mv)
                p.drain("scalar")
                scv = p.ei("scalar", s["scA"], lambda q, sub=sub, bank=bank, eslot=eslot:
                           q.copy(RWSB[:, eslot * 1024 + sub * 512:
                                       eslot * 1024 + sub * 512 + 512],
                                  self.PS[bank][:, :]))
                rw_s.append(scv)

        emit_rw(p)
        nrw = 0
        for half in range(2):
            tof = half * 1024
            for e in range(9):
                if e < 8:
                    cur_eslot = nrw % 2
                    nrw += 1
                for mh in range(HCH):
                    slot = len(a_marks) % 2
                    s1 = s["sld"] if slot == 0 else s["sldB"]
                    s3 = s["sld2"] if slot == 0 else s["sld2B"]
                    pace = a_marks[-2] if len(a_marks) >= 2 else mm0
                    p.wait("sync", mmA, pace)
                    lv1 = p.ei("sync", s1, lambda q, e=e, mh=mh, slot=slot: q.dma_start(
                        out=WAB[:, slot * 1024: slot * 1024 + 1024],
                        in_=w1_slice(e, mh)), 16)
                    lv3 = p.ei("sync", s3, lambda q, e=e, mh=mh, slot=slot: q.dma_start(
                        out=W3B[:, slot * 1024: slot * 1024 + 1024],
                        in_=w3_slice(e, mh)), 16)
                    p.wait("tensor", s1, lv1)
                    p.wait("tensor", s3, lv3)
                    bank_guard = max((ep_v[-2] if len(ep_v) >= 2 else 0),
                                     (bep_v[-1] if bep_v else 0))
                    p.wait("tensor", s["vA"], bank_guard)
                    if mh == 0 and rw_s:
                        p.wait("tensor", s["scA"], rw_s[-1])
                    pa0, pa1 = (0, 1) if slot == 0 else (2, 3)
                    pb0, pb1 = (4, 5) if slot == 0 else (6, 7)
                    def mka(q, slot=slot, tof=tof, pa0=pa0, pa1=pa1, pb0=pb0, pb1=pb1):
                        for k in range(NCH):
                            st, sp = (k == 0), (k == NCH - 1)
                            w1g = WAB[:, slot * 1024 + k * 128: slot * 1024 + (k + 1) * 128]
                            w3g = W3B[:, slot * 1024 + k * 128: slot * 1024 + (k + 1) * 128]
                            xs0 = self.X[:, k * T + tof: k * T + tof + 512]
                            xs1 = self.X[:, k * T + tof + 512: k * T + tof + 1024]
                            q.matmul(self.PS[pa0][:, :], w1g, xs0, start=st, stop=sp)
                            q.matmul(self.PS[pa1][:, :], w1g, xs1, start=st, stop=sp)
                            q.matmul(self.PS[pb0][:, :], w3g, xs0, start=st, stop=sp)
                            ins = q.matmul(self.PS[pb1][:, :], w3g, xs1, start=st, stop=sp)
                        return ins
                    amv = p.ei("tensor", mmA, mka)
                    a_marks.append(amv)
                    b1k, b3k = ("sb1", "sb3") if e == 8 else ("b1", "b3")
                    b1j = mh if e == 8 else e * HCH + mh
                    p.wait("vector", mmA, amv)
                    if mh == 0 and b_marks:
                        p.wait("scalar", mmA, b_marks[-1])
                    for sub, pa, pb in ((0, pa0, pb0), (1, pa1, pb1)):
                        p.drain("vector")
                        p.ei("vector", s["vA"], lambda q, pb=pb, b3k=b3k, b1j=b1j: q.tensor_scalar(
                            VS1[:, :], self.PS[pb][:, :], self.cpc(b3k, b1j), None, Al.add))
                        p.drain("vector")
                        p.wait("vector", s["scA"])
                        vv = p.ei("vector", s["vA"], lambda q, pa=pa, b1k=b1k, b1j=b1j: q.scalar_tensor_tensor(
                            VS2[:, :], self.PS[pa][:, :], self.cpc(b1k, b1j), VS1[:, :],
                            Al.add, Al.mult))
                        p.wait("scalar", s["vA"], vv)
                        p.drain("scalar")
                        scv = p.ei("scalar", s["scA"], lambda q, mh=mh, sub=sub: q.activation(
                            HB[:, mh * 1024 + sub * 512: mh * 1024 + sub * 512 + 512],
                            VS2[:, :], AF.Silu))
                    ep_v.append(vv)
                    ep_s.append(scv)
                for md in range(NCH):
                    mslot = len(b_marks) % 2
                    sg = s["gld"] if mslot == 0 else s["gldB"]
                    p.wait("gpsimd", mmA, b_marks[-2] if len(b_marks) >= 2 else mm0)
                    lv2 = p.ei("gpsimd", sg, lambda q, e=e, md=md, mslot=mslot: q.dma_start(
                        out=W2B[:, mslot * 1408: mslot * 1408 + 1408],
                        in_=w2_slice(e, md)), 16)
                    p.wait("tensor", sg, lv2)
                    if md == 0:
                        p.wait("tensor", s["scA"], ep_s[-1])
                    p.wait("tensor", s["vA"], max(
                        ep_v[-1], (bep_v[-2] if len(bep_v) >= 2 else 0)))
                    pc0, pc1 = (0, 1) if mslot == 0 else (2, 3)
                    def mkb2(q, pc0=pc0, pc1=pc1, mslot=mslot):
                        for kh in range(HCH):
                            st, sp = (kh == 0), (kh == HCH - 1)
                            w2g = W2B[:, mslot * 1408 + kh * 128: mslot * 1408 + (kh + 1) * 128]
                            q.matmul(self.PS[pc0][:, :], w2g,
                                     HB[:, kh * 1024: kh * 1024 + 512], start=st, stop=sp)
                            ins = q.matmul(self.PS[pc1][:, :], w2g,
                                           HB[:, kh * 1024 + 512: kh * 1024 + 1024], start=st, stop=sp)
                        return ins
                    bmv = p.ei("tensor", mmA, mkb2)
                    b_marks.append(bmv)
                    if md == 3 and e < 8:
                        emit_rw(p)
                    b2k = "sb2" if e == 8 else "b2"
                    b2j = md if e == 8 else e * NCH + md
                    p.wait("vector", mmA, bmv)
                    for sub, pc in ((0, pc0), (1, pc1)):
                        p.drain("vector")
                        if e < 8:
                            p.ei("vector", s["vA"], lambda q, pc=pc, b2k=b2k, b2j=b2j, sub=sub, cur_eslot=cur_eslot: q.scalar_tensor_tensor(
                                VS1[:, :], self.PS[pc][:, :], self.cpc(b2k, b2j),
                                RWSB[:, cur_eslot * 1024 + sub * 512:
                                     cur_eslot * 1024 + sub * 512 + 512],
                                Al.add, Al.mult))
                        else:
                            p.ei("vector", s["vA"], lambda q, pc=pc, b2k=b2k, b2j=b2j: q.tensor_scalar(
                                VS1[:, :], self.PS[pc][:, :], self.cpc(b2k, b2j), None, Al.add))
                        p.drain("vector")
                        vv = p.ei("vector", s["vA"], lambda q, md=md, tof=tof, sub=sub: q.tensor_tensor(
                            self.ACC[:, md * T + tof + sub * 512: md * T + tof + sub * 512 + 512],
                            self.ACC[:, md * T + tof + sub * 512: md * T + tof + sub * 512 + 512],
                            VS1[:, :], Al.add))
                    bep_v.append(vv)
        p.run()

        p = Prog(nc)
        self.ln_stats2(p, self.ACC, self.ROWA, self.ROWB, sq=self.X)
        self.ln_rows_finalize(p, self.ROWA, self.ROWB)
        self.ln_norm(p, self.ACC, self.X, self.ROWB, self.ROWA)
        p.run()

    def phase_gla_d0(self):
        nc, s = self.nc, self.sem
        gctx = ExitStack()
        self.gla_ctx = gctx
        self.HBF = gctx.enter_context(nc.sbuf_tensor("HBF", [128, NCH * T], BF16))
        self.RSTg = gctx.enter_context(nc.sbuf_tensor("RSTg", [128, NCK * 4], F32))
        self.ACSTg = gctx.enter_context(nc.sbuf_tensor("ACSTg", [128, NCK * 4], F32))
        self.FL2 = gctx.enter_context(nc.sbuf_tensor("FL2", [128, 1], F32))
        self.SCUR = gctx.enter_context(nc.sbuf_tensor("SCUR", [128, 1024], F32))

        p = Prog(nc)
        ROWA = self.sbuf_local("ROWg1", [1, T], F32R)
        ROWB = self.sbuf_local("ROWg2", [1, T], F32R)
        self.ln_stats2(p, self.X, ROWA, ROWB)
        self.ln_rows_finalize(p, ROWA, ROWB)
        self.ln_norm(p, self.X, self.HBF, ROWB, ROWA, affine=("lnw", "lnb"))
        p.drain("vector")
        p.ei("vector", s["vA"], lambda q: q.tensor_scalar(
            self.FL2[:, :], self.cpc("flag"), -1.0, 1.0, Al.mult, Al.add))
        p.drain("vector")
        p.ei("vector", s["vA"], lambda q: q.memset(self.SCUR[:, :], 0.0))
        for i in range(4):
            p.ei("scalar", s["sst"], lambda q, i=i: q.dma_start(
                out=self.x2_spill[:, i * 4096:(i + 1) * 4096],
                in_=self.X[:, i * 4096:(i + 1) * 4096]), 16)
        p.run()

    def phase_gla_main(self):
        nc, s = self.nc, self.sem
        HBF, SCUR = self.HBF, self.SCUR
        p = Prog(nc)
        UC = self.sbuf_local("UC", [17, 128], BF16)
        GKW1B = self.sbuf_local("GKW1B", [128, 8 * 16], BF16)
        GKW2B = self.sbuf_local("GKW2B", [17, 512], BF16)
        ACCB = self.ACCB
        WQ0, WK0, WV0 = 16384, 20480, 24576
        GTMc = self.sbuf_local("GTMc", [128, 512], BF16)
        FAC = self.sbuf_local("FACq", [128, 512], BF16)
        FACK = self.sbuf_local("FACk", [128, 512], BF16)
        KF = self.sbuf_local("KF", [128, 512], BF16)
        KT = self.sbuf_local("KT", [128, 512], BF16)
        VT = self.sbuf_local("VT", [128, 1024], BF16)
        PSB = self.sbuf_local("PSB", [128, 512], BF16)
        SOP = self.sbuf_local("SOPt", [128, 256], BF16)
        QSTc = self.sbuf_local("QSTc", [128, 512], BF16)
        AC = self.sbuf_local("ACc", [128, 20], F32)
        RACC = self.sbuf_local("RACC", [128, 4], F32)

        mm0 = s["mmA"].v
        p.wait("gpsimd", s["mmA"], mm0)
        p.ei("gpsimd", s["gq"], lambda q: q.dma_start(
            out=GKW1B[:, :], in_=self.dram["gkw1"]), 16)
        gk_ready = p.ei("gpsimd", s["gq"], lambda q: q.dma_start(
            out=GKW2B[:, :], in_=self.dram["gkw2e"]), 16)
        p.drain("vector")
        p.ei("vector", s["vA"], lambda q: q.memset(RACC[:, :], 0.0))
        p.wait("gpsimd", s["mmA"], mm0)
        uc1 = p.ei("gpsimd", s["guc"], lambda q: q.dma_start(
            out=UC[16:17, :], in_=self.ONEROW[0:1, :]), 16)
        for g in range(4):
            wq_rdy = p.ei("gpsimd", s["gqB"], lambda q, g=g: q.dma_start(
                out=ACCB[:, WQ0 + g * 1024: WQ0 + (g + 1) * 1024],
                in_=self.dram["Wq"][g]), 16)
            wk_rdy = p.ei("gpsimd", s["gk2"], lambda q, g=g: q.dma_start(
                out=ACCB[:, WK0 + g * 1024: WK0 + (g + 1) * 1024],
                in_=self.dram["Wk"][g]), 16)
        for k in range(NCH):
            for n in range(2):
                wv_rdy = p.ei("gpsimd", s["gld"], lambda q, k=k, n=n: q.dma_start(
                    out=ACCB[:, WV0 + (k * 2 + n) * 512: WV0 + (k * 2 + n + 1) * 512],
                    in_=self.dram["Wv"][k, n]), 16)

        wq_marks = []
        v_kmarks = []
        for c in range(NCK):
            p.wait("tensor", s["gq"], gk_ready)
            p.wait("tensor", s["vA"])
            p.wait("tensor", s["vB"])
            def mku(q, c=c):
                for k in range(NCH):
                    ins = q.matmul(self.PS[0][0:16, 0:128], GKW1B[:, k * 16:(k + 1) * 16],
                                   HBF[:, k * T + c * 128: k * T + (c + 1) * 128],
                                   start=(k == 0), stop=(k == NCH - 1))
                return ins
            mv = p.ei("tensor", s["mmA"], mku)
            p.wait("scalar", s["mmA"], mv)
            p.drain("scalar")
            scv = p.ei("scalar", s["scA"], lambda q: q.copy(UC[0:16, :], self.PS[0][0:16, 0:128]))
            p.wait("tensor", s["scA"], scv)
            p.wait("tensor", s["guc"], uc1)
            mv = p.ei("tensor", s["mmA"], lambda q: q.matmul(
                self.PS[1][:, :], UC[:, :], GKW2B[:, :], start=True, stop=True))
            p.wait("scalar", s["mmA"], mv)
            p.wait("scalar", s["vA"])
            p.drain("scalar")
            p.ei("scalar", s["scA"], lambda q: q.activation(
                self.XSQ[:, :], self.PS[1][:, :], AF.Abs))
            p.drain("scalar")
            p.ei("scalar", s["scA"], lambda q: q.activation(
                self.NTMP[:, :], self.XSQ[:, :], AF.Exp, scale=-1.0))
            p.drain("scalar")
            p.ei("scalar", s["scA"], lambda q: q.activation(
                self.NTMP[:, :], self.NTMP[:, :], AF.Ln, bias=1.0))
            p.drain("scalar")
            scv = p.ei("scalar", s["scA"], lambda q: q.activation(
                self.XSQ[:, :], self.PS[1][:, :], AF.Relu, scale=-1.0))
            p.wait("vector", s["scA"], scv)
            p.drain("vector")
            vv = p.ei("vector", s["vA"], lambda q: q.tensor_tensor(
                GTMc[:, :], self.XSQ[:, :], self.NTMP[:, :], Al.add))
            p.wait("tensor", s["vA"], vv)
            def mkcs(q):
                for g in range(4):
                    ins = q.matmul(self.PS[2][:, g * 128:(g + 1) * 128],
                                   GTMc[:, g * 128:(g + 1) * 128], self.MASKB[:, :],
                                   start=True, stop=True)
                return ins
            mv_cs = p.ei("tensor", s["mmA"], mkcs)
            p.wait("vector", s["mmA"], mv_cs)
            p.drain("vector")
            p.ei("vector", s["vA"], lambda q: q.tensor_copy(
                AC[:, 0:4], self.PS[2][:, 127::128]))
            p.drain("vector")
            p.ei("vector", s["vA"], lambda q, c=c: q.tensor_copy(
                self.ACSTg[:, c * 4:(c + 1) * 4], AC[:, 0:4]))
            p.drain("vector")
            p.ei("vector", s["vA"], lambda q, c=c: q.tensor_copy(
                self.RSTg[:, c * 4:(c + 1) * 4], RACC[:, :]))
            p.drain("vector")
            p.ei("vector", s["vA"], lambda q: q.tensor_tensor(
                RACC[:, :], RACC[:, :], AC[:, 0:4], Al.add))
            p.drain("vector")
            p.ei("vector", s["vA"], lambda q: q.tensor_scalar(
                AC[:, 12:16], AC[:, 0:4], 1.0 / 32.0, None, Al.mult))
            p.drain("vector")
            vv = p.ei("vector", s["vA"], lambda q: q.tensor_scalar(
                AC[:, 16:20], AC[:, 0:4], -1.0 / 32.0, LN_DKS, Al.mult, Al.add))
            p.wait("scalar", s["vA"], vv)
            p.drain("scalar")
            p.ei("scalar", s["scA"], lambda q: q.activation(
                AC[:, 4:8], AC[:, 0:4], AF.Exp, scale=-1.0 / 32.0))
            p.drain("scalar")
            p.ei("scalar", s["scA"], lambda q: q.activation(
                AC[:, 8:12], AC[:, 0:4], AF.Exp, scale=-1.0 / 16.0))
            for g in range(4):
                p.drain("scalar")
                p.ei("scalar", s["scA"], lambda q, g=g: q.activation(
                    FAC[:, g * 128:(g + 1) * 128], self.PS[2][:, g * 128:(g + 1) * 128],
                    AF.Exp, scale=-1.0 / 16.0, bias=AC[:, 12 + g:13 + g]))
                p.drain("scalar")
                scv = p.ei("scalar", s["scA"], lambda q, g=g: q.activation(
                    FACK[:, g * 128:(g + 1) * 128], self.PS[2][:, g * 128:(g + 1) * 128],
                    AF.Exp, scale=1.0 / 16.0, bias=AC[:, 16 + g:17 + g]))
            fac_done = scv
            p.wait("vector", s["scA"], fac_done)
            p.wait("vector", s["srw"])
            if c == 0:
                p.wait("tensor", s["gqB"], wq_rdy)
                p.wait("tensor", s["gk2"], wk_rdy)
            for g in range(4):
                p.wait("tensor", s["vA"])
                def mkq(q, c=c, g=g):
                    for k in range(NCH):
                        ins = q.matmul(self.PS[3][:, 0:128],
                                       ACCB[:, WQ0 + g * 1024 + k * 128: WQ0 + g * 1024 + (k + 1) * 128],
                                       HBF[:, k * T + c * 128: k * T + (c + 1) * 128],
                                       start=(k == 0), stop=(k == NCH - 1))
                    return ins
                mvq = p.ei("tensor", s["mmA"], mkq)
                p.wait("vector", s["mmA"], mvq)
                p.drain("vector")
                p.ei("vector", s["vA"], lambda q, g=g: q.tensor_tensor(
                    QSTc[:, g * 128:(g + 1) * 128], self.PS[3][:, 0:128],
                    FAC[:, g * 128:(g + 1) * 128], Al.mult))
                def mkk(q, c=c, g=g):
                    for k in range(NCH):
                        ins = q.matmul(self.PS[0][:, 0:128],
                                       ACCB[:, WK0 + g * 1024 + k * 128: WK0 + g * 1024 + (k + 1) * 128],
                                       HBF[:, k * T + c * 128: k * T + (c + 1) * 128],
                                       start=(k == 0), stop=(k == NCH - 1))
                    return ins
                mvk = p.ei("tensor", s["mmA"], mkk)
                wq_marks.append(mvk)
                p.wait("vector", s["mmA"], mvk)
                p.drain("vector")
                p.ei("vector", s["vA"], lambda q, g=g: q.tensor_tensor(
                    KF[:, g * 128:(g + 1) * 128], self.PS[0][:, 0:128],
                    FACK[:, g * 128:(g + 1) * 128], Al.mult))
            vq = s["vA"].v
            p.wait("sync", s["vA"], vq)
            p.ei("sync", s["srw"], lambda q, c=c: q.dma_start(
                out=self.qst_d[:, c * 512:(c + 1) * 512], in_=QSTc[:, :]), 16)
            if c == 0:
                p.wait("tensor", s["gld"], wv_rdy)
            for n in range(2):
                p.wait("tensor", s["scA"])
                def mkv(q, c=c, n=n):
                    for k in range(NCH):
                        ins = q.matmul(self.PS[5][:, :],
                                       HBF[:, k * T + c * 128: k * T + (c + 1) * 128],
                                       ACCB[:, WV0 + (k * 2 + n) * 512: WV0 + (k * 2 + n + 1) * 512],
                                       start=(k == 0), stop=(k == NCH - 1))
                    return ins
                mvv = p.ei("tensor", s["mmA"], mkv)
                v_kmarks.append(mvv)
                p.wait("scalar", s["mmA"], v_kmarks[-1])
                p.wait("scalar", s["mmB"])
                p.drain("scalar")
                p.ei("scalar", s["scA"], lambda q, n=n: q.copy(
                    VT[:, n * 512:(n + 1) * 512], self.PS[5][:, :]))
            vk = s["vA"].v
            p.wait("tensor", s["vA"], vk)
            for g in range(4):
                bk2 = 3 if g % 2 == 0 else 0
                p.wait("tensor", s["vB"], s["vB"].v - 1)
                mvt = p.ei("tensor", s["mmB"], lambda q, g=g, bk2=bk2: q.transpose(
                    self.PS[bk2][:, :].bitcast(BF16)[:, 0:128], KF[:, g * 128:(g + 1) * 128],
                    self.IDB[:, :]))
                p.wait("vector", s["mmB"], mvt)
                p.drain("vector")
                p.ei("vector", s["vB"], lambda q, g=g, bk2=bk2: q.tensor_copy(
                    KT[:, g * 128:(g + 1) * 128], self.PS[bk2][:, :].bitcast(BF16)[:, 0:128]))
            scv_v = s["scA"].v
            p.wait("tensor", s["scA"], scv_v)
            p.wait("tensor", s["vB"])
            def mkU(q):
                for g in range(4):
                    bank, half = (1, 0) if g == 0 else (1, 1) if g == 1 else (2, 0) if g == 2 else (2, 1)
                    ins = q.matmul(self.PS[bank][:, half * 256: half * 256 + 256],
                                   KT[:, g * 128:(g + 1) * 128],
                                   VT[:, g * 256:(g + 1) * 256], start=True, stop=True)
                return ins
            mvU = p.ei("tensor", s["mmA"], mkU)
            p.wait("vector", s["mmA"], mvU)
            for g in range(4):
                bank, half = (1, 0) if g == 0 else (1, 1) if g == 1 else (2, 0) if g == 2 else (2, 1)
                p.drain("vector")
                p.ei("vector", s["vA"], lambda q, g=g: q.tensor_scalar(
                    SOP[:, :], SCUR[:, g * 256:(g + 1) * 256], AC[:, 4 + g:5 + g],
                    None, Al.mult))
                p.drain("vector")
                p.ei("vector", s["vA"], lambda q, g=g, bank=bank, half=half: q.tensor_scalar(
                    self.NTMP[:, 0:256], self.PS[bank][:, half * 256: half * 256 + 256],
                    AC[:, 4 + g:5 + g], None, Al.mult))
                p.drain("vector")
                p.ei("vector", s["vA"], lambda q, g=g: q.scalar_tensor_tensor(
                    SCUR[:, g * 256:(g + 1) * 256], SCUR[:, g * 256:(g + 1) * 256],
                    AC[:, 8 + g:9 + g], self.NTMP[:, 0:256], Al.mult, Al.add))
                vv = s["vA"].v
                p.wait("tensor", s["vA"], vv)
                def mkP(q, g=g):
                    for hh in range(4):
                        tp = (96, 0) if hh == 3 else None
                        ins = q.matmul(self.PS[4 + hh][:, 0:128],
                                       KF[32 * hh:32 * hh + 32, g * 128:(g + 1) * 128],
                                       QSTc[32 * hh:32 * hh + 32, g * 128:(g + 1) * 128],
                                       start=True, stop=True, tile_position=tp)
                    return ins
                mvP = p.ei("tensor", s["mmB"], mkP)
                p.wait("vector", s["mmB"], mvP)
                for hh in range(4):
                    p.drain("vector")
                    p.ei("vector", s["vA"], lambda q, hh=hh: q.tensor_tensor(
                        PSB[:, hh * 128:(hh + 1) * 128], self.PS[4 + hh][:, 0:128],
                        self.MASKB[:, :], Al.mult))
                pv = s["vA"].v
                p.wait("tensor", s["vA"], pv)
                def mkO(q, g=g):
                    for hh in range(4):
                        tp = (96, 0) if hh == 3 else None
                        q.matmul(self.PS[4 + hh][:, 128:192], PSB[:, hh * 128:(hh + 1) * 128],
                                 VT[:, g * 256 + hh * 64: g * 256 + (hh + 1) * 64],
                                 start=True, stop=False)
                        ins = q.matmul(self.PS[4 + hh][:, 128:192],
                                       QSTc[32 * hh:32 * hh + 32, g * 128:(g + 1) * 128],
                                       SOP[32 * hh:32 * hh + 32, hh * 64:(hh + 1) * 64],
                                       start=False, stop=True, tile_position=tp)
                    return ins
                mvO = p.ei("tensor", s["mmB"], mkO)
                p.wait("vector", s["mmB"], mvO)
                for hh in range(4):
                    p.drain("vector")
                    p.ei("vector", s["vA"], lambda q, g=g, hh=hh, c=c: q.tensor_copy(
                        ACCB[:, c * 1024 + (g * 4 + hh) * 64: c * 1024 + (g * 4 + hh) * 64 + 64],
                        self.PS[4 + hh][:, 128:192]))
        p.drain("vector")
        p.ei("vector", s["vA"], lambda q: q.tensor_scalar(
            self.XSQ[:, :], SCUR[:, 0:512], self.FL2[:, 0:1], None, Al.mult))
        p.drain("vector")
        vv = p.ei("vector", s["vA"], lambda q: q.tensor_scalar(
            self.NTMP[:, :], SCUR[:, 512:1024], self.FL2[:, 0:1], None, Al.mult))
        p.wait("gpsimd", s["vA"], vv)
        p.ei("gpsimd", s["gld"], lambda q: q.dma_start(
            out=self.cc_in[:, 0:512], in_=self.XSQ[:, :]), 16)
        p.wait("gpsimd", s["gld"])
        p.ei("gpsimd", s["gld"], lambda q: q.dma_start(
            out=self.cc_in[:, 512:1024], in_=self.NTMP[:, :]), 16)
        p.wait("gpsimd", s["gld"])
        p.ei("gpsimd", s["cc"], lambda q: q.collective_compute(
            "AllReduce", Al.add,
            replica_groups=[[0, 1], [2, 3], [4, 5], [6, 7]],
            ins=[self.cc_in], outs=[self.cc_out]))
        p.run()

    def phase_gla_post(self):
        nc, s = self.nc, self.sem
        HBF, SCUR = self.HBF, self.SCUR
        p = Prog(nc)
        SRECV = self.sbuf_local("SRECV", [128, 1024], BF16)
        SCO = self.sbuf_local("SCO", [128, 1024], BF16)
        RMSWB = self.sbuf_local("RMSWBl", [128, DV], BF16)
        GSW = self.sbuf_local("GSW", [128, 1024], BF16)
        OF = self.sbuf_local("OF", [128, 1024], F32R)
        OGT = self.sbuf_local("OGT", [128, 1024], BF16)
        QSTc = self.sbuf_local("QSTc2", [128, 512], BF16)
        RSTD = self.sbuf_local("RSTD", [128, 16], F32)
        ECC = self.sbuf_local("ECC", [128, 12], F32)
        ACCB = self.ACCB
        WG0, WO0 = 16384, 24576

        mm0 = s["mmA"].v
        p.wait("gpsimd", s["mmA"], mm0)
        rmw = p.ei("gpsimd", s["guc"], lambda q: q.dma_start(
            out=RMSWB[:, :], in_=self.dram["rms_wb"]), 16)
        p.wait("vector", s["guc"], rmw)
        for k in range(NCH):
            for n in range(2):
                wg_rdy = p.ei("gpsimd", s["gld"], lambda q, k=k, n=n: q.dma_start(
                    out=ACCB[:, WG0 + (k * 2 + n) * 512: WG0 + (k * 2 + n + 1) * 512],
                    in_=self.dram["Wg"][k, n]), 16)
        for md in range(NCH):
            wo_rdy = p.ei("gpsimd", s["gldB"], lambda q, md=md: q.dma_start(
                out=ACCB[:, WO0 + md * 1024: WO0 + (md + 1) * 1024],
                in_=self.dram["Wo"][md]), 16)
        p.wait("gpsimd", s["cc"], s["cc"].v)
        rv = p.ei("gpsimd", s["gq"], lambda q: q.dma_start(
            out=SRECV[:, :], in_=self.cc_out), 16)
        p.wait("vector", s["gq"], rv)
        p.drain("vector")
        srecv_mark = p.ei("vector", s["vA"], lambda q: q.tensor_scalar(
            SRECV[:, :], SRECV[:, :], self.cpc("flag"), None, Al.mult))
        p.wait("gpsimd", s["vA"], srecv_mark)
        p.wait("vector", s["sst"])

        wg_marks = []
        wo_marks = []
        wo_eps = []
        corr_marks = []
        for c in range(NCK):
            p.wait("sync", s["mmA"], corr_marks[-1] if corr_marks else mm0)
            if not corr_marks:
                p.wait("sync", s["srw"])
            lq = p.ei("sync", s["srw"], lambda q, c=c: q.dma_start(
                out=QSTc[:, :], in_=self.qst_d[:, c * 512:(c + 1) * 512]), 16)
            p.drain("scalar")
            p.wait("scalar", s["vA"])
            p.ei("scalar", s["scA"], lambda q, c=c: q.activation(
                ECC[:, 0:4], self.ACSTg[:, c * 4:(c + 1) * 4], AF.Exp, scale=-1.0 / 32.0))
            p.drain("scalar")
            scv = p.ei("scalar", s["scA"], lambda q, c=c: q.activation(
                ECC[:, 4:8], self.RSTg[:, c * 4:(c + 1) * 4], AF.Exp, scale=-1.0 / 16.0))
            p.wait("vector", s["scA"], scv)
            p.wait("vector", s["mmA"], corr_marks[-1] if corr_marks else mm0)
            p.drain("vector")
            p.ei("vector", s["vA"], lambda q: q.tensor_tensor(
                ECC[:, 8:12], ECC[:, 0:4], ECC[:, 4:8], Al.mult))
            for g in range(4):
                p.drain("vector")
                p.ei("vector", s["vA"], lambda q, g=g: q.tensor_scalar(
                    SCO[:, g * 256:(g + 1) * 256], SRECV[:, g * 256:(g + 1) * 256],
                    ECC[:, 8 + g:9 + g], None, Al.mult))
            vv = s["vA"].v
            p.wait("tensor", s["vA"], vv)
            p.wait("tensor", s["srw"], lq)
            def mkcorr(q):
                for g in range(4):
                    for hh in range(4):
                        tp = (96, 0) if hh == 3 else None
                        ins = q.matmul(self.PS[4 + hh][:, g * 64:(g + 1) * 64],
                                       QSTc[32 * hh:32 * hh + 32, g * 128:(g + 1) * 128],
                                       SCO[32 * hh:32 * hh + 32, g * 256 + hh * 64: g * 256 + (hh + 1) * 64],
                                       start=True, stop=True, tile_position=tp)
                return ins
            mvc = p.ei("tensor", s["mmA"], mkcorr)
            corr_marks.append(mvc)
            p.wait("vector", s["mmA"], mvc)
            p.wait("vector", s["mmB"])
            for g in range(4):
                for hh in range(4):
                    p.drain("vector")
                    p.ei("vector", s["vA"], lambda q, g=g, hh=hh, c=c: q.tensor_tensor(
                        OF[:, (g * 4 + hh) * 64:(g * 4 + hh) * 64 + 64],
                        ACCB[:, c * 1024 + (g * 4 + hh) * 64: c * 1024 + (g * 4 + hh) * 64 + 64],
                        self.PS[4 + hh][:, g * 64:(g + 1) * 64], Al.add))
            vv = s["vA"].v
            p.wait("scalar", s["vA"], vv)
            p.wait("scalar", s["vA"])
            p.drain("scalar")
            p.ei("scalar", s["scA"], lambda q: q.activation(
                self.XSQ[:, :], OF[:, 0:512], AF.Square))
            p.drain("scalar")
            scv = p.ei("scalar", s["scA"], lambda q: q.activation(
                self.NTMP[:, :], OF[:, 512:1024], AF.Square))
            p.wait("vector", s["scA"], scv)
            p.drain("vector")
            p.ei("vector", s["vA"], lambda q: q.tensor_reduce(
                RSTD[:, 0:8], self.XSQ[:, :].rearrange(
                    "p (h v) -> p h v", h=8), AX, Al.add))
            p.drain("vector")
            p.ei("vector", s["vA"], lambda q: q.tensor_reduce(
                RSTD[:, 8:16], self.NTMP[:, :].rearrange(
                    "p (h v) -> p h v", h=8), AX, Al.add))
            p.drain("vector")
            p.ei("vector", s["vA"], lambda q: q.tensor_scalar(
                RSTD[:, :], RSTD[:, :], 1.0 / DV, EPS, Al.mult, Al.add))
            p.drain("vector")
            vv = p.ei("vector", s["vA"], lambda q: q.reciprocal(RSTD[:, :], RSTD[:, :]))
            p.wait("scalar", s["vA"], vv)
            p.drain("scalar")
            sc_rstd = p.ei("scalar", s["scA"], lambda q: q.activation(
                RSTD[:, :], RSTD[:, :], AF.Sqrt))
            if c == 0:
                p.wait("tensor", s["gld"], wg_rdy)
            for n in range(2):
                p.wait("tensor", s["scA"])
                p.wait("tensor", s["vA"])
                def mkg2(q, n=n, c=c):
                    for k in range(NCH):
                        ins = q.matmul(self.PS[n][:, :],
                                       HBF[:, k * T + c * 128: k * T + (c + 1) * 128],
                                       ACCB[:, WG0 + (k * 2 + n) * 512: WG0 + (k * 2 + n + 1) * 512],
                                       start=(k == 0), stop=(k == NCH - 1))
                    return ins
                mvg = p.ei("tensor", s["mmA"], mkg2)
                wg_marks.append(mvg)
                p.wait("scalar", s["mmA"], wg_marks[-1])
                p.drain("scalar")
                p.wait("scalar", s["vA"])
                scv = p.ei("scalar", s["scA"], lambda q, n=n: q.activation(
                    self.XSQ[:, :], self.PS[n][:, :], AF.Silu))
                p.wait("vector", s["scA"], scv)
                p.drain("vector")
                p.ei("vector", s["vA"], lambda q, n=n: q.tensor_tensor(
                    GSW[:, n * 512:(n + 1) * 512].rearrange("p (h v) -> p h v", h=8),
                    self.XSQ[:, :].rearrange("p (h v) -> p h v", h=8),
                    RMSWB[:, :].rearrange("p (o v) -> p o v", o=1).to_broadcast(
                        (128, 8, 64)), Al.mult))
            p.wait("vector", s["scA"], sc_rstd)
            p.drain("vector")
            p.ei("vector", s["vA"], lambda q: q.tensor_tensor(
                OF[:, :].rearrange("p (h v) -> p h v", h=16),
                OF[:, :].rearrange("p (h v) -> p h v", h=16),
                RSTD[:, :].rearrange("p (h o) -> p h o", o=1).to_broadcast((128, 16, 64)),
                Al.mult))
            p.drain("vector")
            vv = p.ei("vector", s["vA"], lambda q: q.tensor_tensor(
                OF[:, :], OF[:, :], GSW[:, :], Al.mult))
            p.wait("tensor", s["vA"], vv)
            p.wait("vector", s["mmA"])
            for t8 in range(8):
                bk2 = 2 + (t8 % 2)
                p.wait("tensor", s["vB"], s["vB"].v - 1)
                mvt = p.ei("tensor", s["mmB"], lambda q, t8=t8, bk2=bk2: q.transpose(
                    self.PS[bk2][:, 0:128].bitcast(F32R), OF[:, t8 * 128:(t8 + 1) * 128],
                    self.IDR[:, :]))
                p.wait("vector", s["mmB"], mvt)
                p.drain("vector")
                p.ei("vector", s["vB"], lambda q, t8=t8, bk2=bk2: q.tensor_copy(
                    OGT[:, t8 * 128:(t8 + 1) * 128], self.PS[bk2][:, 0:128]))
            p.wait("tensor", s["vB"])
            p.wait("tensor", s["scA"])
            if not wo_marks:
                p.wait("tensor", s["gldB"], wo_rdy)
            for wave in range(2):
                for mdl in range(4):
                    md = wave * 4 + mdl
                    if len(wo_eps) >= 4:
                        p.wait("tensor", s["vA"], wo_eps[-4])
                    def mkwo(q, md=md, mdl=mdl):
                        for k in range(NCH):
                            ins = q.matmul(self.PS[mdl][:, 0:128],
                                           ACCB[:, WO0 + md * 1024 + k * 128: WO0 + md * 1024 + (k + 1) * 128],
                                           OGT[:, k * 128:(k + 1) * 128],
                                           start=(k == 0), stop=(k == NCH - 1))
                        return ins
                    mvw = p.ei("tensor", s["mmA"], mkwo)
                    wo_marks.append(mvw)
                    p.wait("vector", s["mmA"], mvw)
                    p.drain("vector")
                    vv = p.ei("vector", s["vA"], lambda q, md=md, c=c, mdl=mdl: q.tensor_tensor(
                        self.X[:, md * T + c * 128: md * T + (c + 1) * 128],
                        self.X[:, md * T + c * 128: md * T + (c + 1) * 128],
                        self.PS[mdl][:, 0:128], Al.add))
                    wo_eps.append(vv)
        p.run()

    def phase_mlp(self):
        nc, s = self.nc, self.sem
        p = Prog(nc)
        ROWA = self.sbuf_local("ROWm1", [1, T], F32R)
        ROWB = self.sbuf_local("ROWm2", [1, T], F32R)
        W1B = self.sbuf_local("W1B", [128, 2 * 1024], F32R)
        W2B2 = self.sbuf_local("W2B2", [128, 2 * 2048], F32R)
        X2R = self.sbuf_local("X2R", [128, 4096], F32R)
        GEL = self.ACC[:, 0:8192]
        H2 = self.ACC[:, 8192:12288]

        self.ln_stats2(p, self.X, ROWA, ROWB)
        self.ln_rows_finalize(p, ROWA, ROWB)
        mm0 = s["mmA"].v
        w1_marks = []
        w2_marks = []
        gel_marks = []
        fin_marks = []
        for tt in range(4):
            tof = tt * 512
            p.wait("tensor", s["vA"])
            def mkb(q, tof=tof, ROWA=ROWA, ROWB=ROWB):
                q.matmul(self.PS[4][:, :], self.ONEROW[0:1, :],
                         ROWB[0:1, tof: tof + 512], start=True, stop=True)
                return q.matmul(self.PS[5][:, :], self.ONEROW[0:1, :],
                                ROWA[0:1, tof: tof + 512], start=True, stop=True)
            mv = p.ei("tensor", s["mmA"], mkb)
            p.wait("vector", s["mmA"], mv)
            for cch in range(NCH):
                p.drain("vector")
                p.ei("vector", s["vA"], lambda q, cch=cch, tof=tof: q.tensor_tensor(
                    self.NTMP[:, :], self.X[:, cch * T + tof: cch * T + tof + 512],
                    self.PS[4][:, :], Al.mult))
                p.drain("vector")
                p.ei("vector", s["vA"], lambda q: q.tensor_tensor(
                    self.NTMP[:, :], self.NTMP[:, :], self.PS[5][:, :], Al.add))
                p.drain("vector")
                p.ei("vector", s["vA"], lambda q, cch=cch: q.tensor_scalar(
                    H2[:, cch * 512:(cch + 1) * 512], self.NTMP[:, :],
                    self.cpc("lnw", cch), self.cpc("lnb", cch), Al.mult, Al.add))
            hv = s["vA"].v
            p.wait("tensor", s["vA"], hv)
            p.wait("sync", s["srw"])
            if fin_marks:
                p.wait("sync", s["vA"], fin_marks[-1])
            x2v = p.ei("sync", s["srw"], lambda q, tt=tt: q.dma_start(
                out=X2R[:, :].rearrange("p (c t) -> p c t", c=8),
                in_=self.x2_spill[:, :].rearrange("p (c f) -> p c f", c=8)[
                    :, :, tt * 512:(tt + 1) * 512]), 16)
            for im in range(16):
                slot = len(w1_marks) % 2
                sw = s["sld"] if slot == 0 else s["sldB"]
                p.wait("sync", s["mmA"], w1_marks[-2] if len(w1_marks) >= 2 else mm0)
                lw = p.ei("sync", sw, lambda q, im=im, slot=slot: q.dma_start(
                    out=W1B[:, slot * 1024: slot * 1024 + 1024],
                    in_=self.dram["W1"][im]), 16)
                p.wait("tensor", sw, lw)
                if len(gel_marks) >= 2:
                    p.wait("tensor", s["scA"], gel_marks[-2])
                pg = im % 2
                def mkw1(q, slot=slot, pg=pg):
                    for k in range(NCH):
                        ins = q.matmul(self.PS[pg][:, :],
                                       W1B[:, slot * 1024 + k * 128: slot * 1024 + (k + 1) * 128],
                                       H2[:, k * 512:(k + 1) * 512],
                                       start=(k == 0), stop=(k == NCH - 1))
                    return ins
                mv = p.ei("tensor", s["mmA"], mkw1)
                w1_marks.append(mv)
                p.wait("scalar", s["mmA"], mv)
                p.drain("scalar")
                gv = p.ei("scalar", s["scA"], lambda q, im=im, pg=pg: q.activation(
                    GEL[:, im * 512:(im + 1) * 512], self.PS[pg][:, :], AF.Gelu,
                    bias=self.cpc("l1b1", im)))
                gel_marks.append(gv)
            p.wait("tensor", s["scA"], gel_marks[-1])
            p.wait("vector", s["srw"], x2v)
            for md in range(NCH):
                slot = len(w2_marks) % 2
                sw = s["sld2"] if slot == 0 else s["sld2B"]
                p.wait("sync", s["mmA"], w2_marks[-2] if len(w2_marks) >= 2 else mm0)
                lw = p.ei("sync", sw, lambda q, md=md, slot=slot: q.dma_start(
                    out=W2B2[:, slot * 2048: slot * 2048 + 2048],
                    in_=self.dram["W2"][md]), 16)
                p.wait("tensor", sw, lw)
                if len(fin_marks) >= 2:
                    p.wait("tensor", s["vA"], fin_marks[-2])
                pg = 2 + md % 2
                def mkw2(q, slot=slot, pg=pg):
                    for kh in range(16):
                        ins = q.matmul(self.PS[pg][:, :],
                                       W2B2[:, slot * 2048 + kh * 128: slot * 2048 + (kh + 1) * 128],
                                       GEL[:, kh * 512:(kh + 1) * 512],
                                       start=(kh == 0), stop=(kh == 15))
                    return ins
                mv = p.ei("tensor", s["mmA"], mkw2)
                w2_marks.append(mv)
                p.wait("vector", s["mmA"], mv)
                p.drain("vector")
                p.ei("vector", s["vA"], lambda q, md=md, pg=pg, tof=tof: q.scalar_tensor_tensor(
                    self.XSQ[:, :], self.PS[pg][:, :], self.cpc("l1b2", md),
                    self.X[:, md * T + tof: md * T + tof + 512], Al.add, Al.add))
                p.drain("vector")
                fv = p.ei("vector", s["vA"], lambda q, md=md, tof=tof: q.tensor_tensor(
                    self.X[:, md * T + tof: md * T + tof + 512], self.XSQ[:, :],
                    X2R[:, md * 512:(md + 1) * 512], Al.add))
                fin_marks.append(fv)
        p.run()
        p = Prog(nc)
        self.ln_stats2(p, self.X, ROWA, ROWB)
        self.ln_rows_finalize(p, ROWA, ROWB)
        self.ln_norm(p, self.X, self.X, ROWB, ROWA)
        p.run()

    def phase_bitffn_prep(self):
        nc, s = self.nc, self.sem
        p = Prog(nc)
        self.bit_ctx = ExitStack()
        SWC = self.bit_ctx.enter_context(nc.sbuf_tensor("SWC", [128, 16], F32))
        self.SWC = SWC
        WT = self.sbuf_local("WT", [128, 2 * 4096], F32R)
        TERNB = self.sbuf_local("TERNB", [128, 2 * 4096], BF16)
        TERN = self.sbuf_local("TERN", [128, 2 * 1024], F32R)
        MC = self.sbuf_local("MC", [128, 16], F32R)
        mm0 = s["mmA"].v
        p.drain("vector")
        p.ei("vector", s["vA"], lambda q: q.tensor_single_scalar(
            MC[:, 3:4], self.ONECOL[:, 0:1], 1e30, Al.is_ge))
        mats = [("wgateT", 8, FI, 0), ("wupT", 8, FI, 1), ("wdownT", 32, D, 2)]
        ab_marks = []
        idx = 0
        for name, ntile, width, j in mats:
            p.drain("vector")
            p.ei("vector", s["vA"], lambda q, j=j: q.tensor_single_scalar(
                MC[:, j:j + 1], self.ONECOL[:, 0:1], 1e30, Al.is_ge))
            for m in range(ntile):
                slot = idx % 2
                ring = "sync" if slot == 0 else "scalar"
                sl = s["sld"] if slot == 0 else s["sld2"]
                p.wait(ring, s["scA"], ab_marks[-2] if len(ab_marks) >= 2 else 0)
                lw = p.ei(ring, sl, lambda q, name=name, m=m, width=width, slot=slot:
                          q.dma_start(out=WT[:, slot * 4096: slot * 4096 + width],
                                      in_=self.dram[name][m * 128:(m + 1) * 128, :]), 16)
                p.wait("scalar", sl, lw)
                p.wait("scalar", s["vA"])
                p.drain("scalar")
                sv = p.ei("scalar", s["scA"], lambda q, width=width, slot=slot: q.activation(
                    WT[:, slot * 4096: slot * 4096 + width],
                    WT[:, slot * 4096: slot * 4096 + width], AF.Abs,
                    accum_out=MC[:, 15:16]))
                ab_marks.append(sv)
                idx += 1
                p.wait("vector", s["scA"], sv)
                p.drain("vector")
                p.ei("vector", s["vA"], lambda q, j=j: q.tensor_tensor(
                    MC[:, j:j + 1], MC[:, j:j + 1], MC[:, 15:16], Al.add))
        vv = s["vA"].v
        p.wait("tensor", s["vA"], vv)
        mv = p.ei("tensor", s["mmA"], lambda q: q.matmul(
            self.PS[6][0:1, 0:4], self.ONECOL[:, :], MC[:, 0:4], start=True, stop=True))
        p.wait("vector", s["mmA"], mv)
        p.drain("vector")
        vv = p.ei("vector", s["vA"], lambda q: q.tensor_copy(
            MC[0:1, 4:8], self.PS[6][0:1, 0:4]))
        p.wait("tensor", s["vA"], vv)
        mv = p.ei("tensor", s["mmA"], lambda q: q.matmul(
            self.PS[6][:, 0:4], self.ONEROW[0:1, :], MC[0:1, 4:8], start=True, stop=True))
        p.wait("vector", s["mmA"], mv)
        p.drain("vector")
        p.ei("vector", s["vA"], lambda q: q.tensor_scalar(
            SWC[:, 4:6], self.PS[6][:, 0:2], 1.0 / (FI * D), EPS, Al.mult, Al.max))
        p.drain("vector")
        p.ei("vector", s["vA"], lambda q: q.tensor_scalar(
            SWC[:, 6:7], self.PS[6][:, 2:3], 1.0 / (FI * D), EPS, Al.mult, Al.max))
        p.drain("vector")
        p.ei("vector", s["vA"], lambda q: q.reciprocal(SWC[:, 0:3], SWC[:, 4:7]))
        p.drain("vector")
        p.ei("vector", s["vA"], lambda q: q.tensor_tensor(
            SWC[:, 7:8], SWC[:, 0:1], SWC[:, 0:1], Al.mult))
        p.drain("vector")
        vv = p.ei("vector", s["vA"], lambda q: q.tensor_tensor(
            SWC[:, 7:8], SWC[:, 7:8], SWC[:, 1:2], Al.mult))
        st_marks = [0, 0]
        q_marks = [vv]
        ts1_marks = []
        ab_last = ab_marks[-1]
        qmats = [("wgateT", self.qwg_t, 8, FI, 0, True), ("wupT", self.qwu_t, 8, FI, 1, True),
                 ("wdownT", self.qwd_t, 32, D, 2, False)]
        tiles = []
        for name, dst, ntile, width, j, isb in qmats:
            for m in range(ntile):
                tiles.append((name, dst, m, width, j, isb))
        for i, (name, dst, m, width, j, isb) in enumerate(tiles):
            slot = i % 2
            ring = "sync" if slot == 0 else "scalar"
            sl = s["sld"] if slot == 0 else s["sld2"]
            if i >= 2:
                p.wait(ring, s["vA"], q_marks[-2])
            else:
                p.wait(ring, s["scA"], ab_last)
            lw = p.ei(ring, sl, lambda q, name=name, m=m, width=width, slot=slot:
                      q.dma_start(out=WT[:, slot * 4096: slot * 4096 + width],
                                  in_=self.dram[name][m * 128:(m + 1) * 128, :]), 16)
            p.wait("scalar", sl, lw)
            p.wait("scalar", s["vA"], q_marks[0])
            p.drain("scalar")
            sv = p.ei("scalar", s["scA"], lambda q, width=width, slot=slot, j=j: q.activation(
                WT[:, :].bitcast(F32)[:, slot * 4096: slot * 4096 + width],
                WT[:, slot * 4096: slot * 4096 + width], AF.Copy,
                scale=SWC[:, j:j + 1], bias=MAGIC))
            ts1_marks.append(sv)
            p.wait("vector", s["scA"], sv)
            sstm = s["sst"] if slot == 0 else s["srw"]
            p.wait("vector", sstm, st_marks[-2])
            p.drain("vector")
            p.ei("vector", s["vA"], lambda q, width=width, slot=slot: q.tensor_scalar(
                WT[:, slot * 4096: slot * 4096 + width],
                WT[:, :].bitcast(F32)[:, slot * 4096: slot * 4096 + width],
                -MAGIC, -1.0, Al.add, Al.max))
            p.drain("vector")
            if isb:
                qv = p.ei("vector", s["vA"], lambda q, width=width, slot=slot: q.tensor_scalar(
                    TERNB[:, slot * 4096: slot * 4096 + width],
                    WT[:, slot * 4096: slot * 4096 + width], 1.0, None, Al.min))
            else:
                qv = p.ei("vector", s["vA"], lambda q, width=width, slot=slot: q.tensor_scalar(
                    TERN[:, slot * 1024: slot * 1024 + width],
                    WT[:, slot * 4096: slot * 4096 + width], 1.0, None, Al.min))
            q_marks.append(qv)
            p.wait("scalar", s["vA"], qv)
            if isb:
                sv2 = p.ei("scalar", sstm, lambda q, dst=dst, m=m, slot=slot: q.dma_start(
                    out=dst[:, :, m * 128:(m + 1) * 128].rearrange("a p f -> p a f"),
                    in_=TERNB[:, slot * 4096: slot * 4096 + 4096].rearrange(
                        "p (a f) -> p a f", f=128)), 16)
            else:
                quar, kk = m // 8, m % 8
                sv2 = p.ei("scalar", sstm, lambda q, dst=dst, quar=quar, kk=kk, slot=slot:
                           q.dma_start(
                    out=dst[:, :, quar * 1024 + kk * 128: quar * 1024 + kk * 128 + 128].rearrange(
                        "a p f -> p a f"),
                    in_=TERN[:, slot * 1024: slot * 1024 + 1024].rearrange(
                        "p (a f) -> p a f", f=128)), 16)
            st_marks.append(sv2)
        p.wait("scalar", sstm, st_marks[-1])
        p.wait("scalar", s["sst"] if (len(tiles) - 1) % 2 == 1 else s["srw"], st_marks[-2])

        if DBGOUT:
            p.wait("scalar", s["vA"])
            p.ei("scalar", s["sst"], lambda q: q.dma_start(
                out=self.dbg_swc, in_=SWC[:, :]), 16)
            p.ei("scalar", s["sst"], lambda q: q.dma_start(
                out=self.dbg_mc, in_=MC[:, :].bitcast(F32)), 16)
            p.ei("scalar", s["sst"], lambda q: q.dma_start(
                out=self.dbg_tern, in_=TERN[:, :].bitcast(F32)), 16)
            p.ei("scalar", s["sst"], lambda q: q.dma_start(
                out=self.dbg_wt, in_=WT[:, :].bitcast(F32)), 16)
            p.wait("scalar", s["sst"], s["sst"].v)
        p.run()

    def phase_bitffn(self):
        nc, s = self.nc, self.sem
        SWC = self.SWC
        p = Prog(nc)
        QXB = self.sbuf_local("QXB", [128, NCH * T], BF16)
        QWGT = self.sbuf_local("QWGT", [128, 2 * 1024], BF16)
        QWUT = self.sbuf_local("QWUT", [128, 2 * 1024], BF16)
        QWDT = self.sbuf_local("QWDT", [128, 2 * 1024], F32R)
        HMX = self.sbuf_local("HMX", [128, 512], BF16)
        HMN = self.sbuf_local("HMN", [128, 512], BF16)
        SXC = self.sbuf_local("SXC", [128, 48], F32R)
        SXROW = self.sbuf_local("SXROW", [1, 512], F32R)
        SHROW = self.sbuf_local("SHROW", [1, 512], F32R)
        OSCROW = self.sbuf_local("OSCROW", [1, 512], F32R)
        mm0 = s["mmA"].v

        AMAX = self.ACC[:, 0:2048]
        NMIN = self.ACC[:, 2048:4096]
        p.drain("vector")
        p.ei("vector", s["vA"], lambda q: q.tensor_tensor(
            AMAX, self.X[:, 0:T], self.X[:, T:2 * T], Al.max))
        p.drain("vector")
        p.ei("vector", s["vA"], lambda q: q.tensor_tensor(
            NMIN, self.X[:, 0:T], self.X[:, T:2 * T], Al.min))
        for cch in range(2, NCH):
            p.drain("vector")
            p.ei("vector", s["vA"], lambda q, cch=cch: q.tensor_tensor(
                AMAX, AMAX, self.X[:, cch * T:(cch + 1) * T], Al.max))
            p.drain("vector")
            p.ei("vector", s["vA"], lambda q, cch=cch: q.tensor_tensor(
                NMIN, NMIN, self.X[:, cch * T:(cch + 1) * T], Al.min))
        p.drain("vector")
        p.ei("vector", s["vA"], lambda q: q.tensor_scalar(
            NMIN, NMIN, -1.0, None, Al.mult))
        p.drain("vector")
        p.ei("vector", s["vA"], lambda q: q.tensor_tensor(AMAX, AMAX, NMIN, Al.max))
        vv = s["vA"].v
        p.wait("tensor", s["vA"], vv)
        for tc in range(16):
            bk2 = 6 + (tc % 2)
            p.wait("tensor", s["vB"], s["vB"].v - 1)
            mvt = p.ei("tensor", s["mmB"], lambda q, tc=tc, bk2=bk2: q.transpose(
                self.PS[bk2][:, 0:128].bitcast(F32R),
                self.ACC[:, tc * 128:(tc + 1) * 128], self.IDR[:, :]))
            p.wait("vector", s["mmB"], mvt)
            p.drain("vector")
            p.ei("vector", s["vB"], lambda q, tc=tc, bk2=bk2: q.tensor_reduce(
                SXC[:, tc:tc + 1], self.PS[bk2][:, 0:128], AX, Al.max))
        p.drain("vector")
        p.ei("vector", s["vB"], lambda q: q.tensor_scalar(
            SXC[:, 16:32], SXC[:, 0:16], EPS, None, Al.max))
        p.drain("vector")
        p.ei("vector", s["vB"], lambda q: q.reciprocal(SXC[:, 32:48], SXC[:, 16:32]))
        p.drain("vector")
        vv = p.ei("vector", s["vB"], lambda q: q.tensor_scalar(
            SXC[:, 32:48], SXC[:, 32:48], 127.0, None, Al.mult))
        p.wait("tensor", s["vB"], vv)
        for tt in range(4):
            tof = tt * 512
            for sc4 in range(4):
                tc = tt * 4 + sc4
                bk2 = 6 + (sc4 % 2)
                p.wait("tensor", s["vB"], s["vB"].v - 1)
                mvt = p.ei("tensor", s["mmB"], lambda q, tc=tc, bk2=bk2: q.transpose(
                    self.PS[bk2][0:1, 0:128].bitcast(F32R),
                    SXC[:, 32 + tc:33 + tc].bitcast(F32R), self.IDR[:, :]))
                p.wait("vector", s["mmB"], mvt)
                p.drain("vector")
                p.ei("vector", s["vB"], lambda q, sc4=sc4, bk2=bk2: q.tensor_copy(
                    SXROW[0:1, sc4 * 128:(sc4 + 1) * 128], self.PS[bk2][0:1, 0:128]))
            p.wait("tensor", s["vB"])
            p.wait("tensor", s["vA"])
            mv = p.ei("tensor", s["mmA"], lambda q: q.matmul(
                self.PS[4][:, :], self.ONEROW[0:1, :], SXROW[0:1, :],
                start=True, stop=True))
            p.wait("vector", s["mmA"], mv)
            for cch in range(NCH):
                p.drain("vector")
                p.ei("vector", s["vA"], lambda q, cch=cch, tof=tof: q.tensor_tensor(
                    self.NTMP[:, :], self.X[:, cch * T + tof: cch * T + tof + 512],
                    self.PS[4][:, :], Al.mult))
                p.drain("vector")
                p.ei("vector", s["vA"], lambda q, cch=cch, tof=tof: q.tensor_scalar(
                    QXB[:, cch * T + tof: cch * T + tof + 512], self.NTMP[:, :],
                    MAGIC, -MAGIC, Al.add, Al.add))
            p.wait("tensor", s["vA"], s["vA"].v)
        gq_marks = []
        dn_marks = []
        fin_marks = []
        for tt in range(4):
            tof = tt * 512
            hm_marks = []
            for m in range(32):
                slot = len(gq_marks) % 2
                sw1 = s["sld"] if slot == 0 else s["sldB"]
                sw2 = s["gld"] if slot == 0 else s["gldB"]
                pace = gq_marks[-2] if len(gq_marks) >= 2 else mm0
                p.wait("sync", s["mmA"], pace)
                p.wait("gpsimd", s["mmA"], pace)
                l1 = p.ei("sync", sw1, lambda q, m=m, slot=slot: q.dma_start(
                    out=QWGT[:, slot * 1024: slot * 1024 + 1024],
                    in_=self.qwg_t[m]), 16)
                l2 = p.ei("gpsimd", sw2, lambda q, m=m, slot=slot: q.dma_start(
                    out=QWUT[:, slot * 1024: slot * 1024 + 1024],
                    in_=self.qwu_t[m]), 16)
                p.wait("tensor", sw1, l1)
                p.wait("tensor", sw2, l2)
                if len(hm_marks) >= 2:
                    p.wait("tensor", s["vA"], hm_marks[-2])
                pg0, pg1 = (0, 1) if m % 2 == 0 else (2, 3)
                def mkgu(q, tof=tof, slot=slot, pg0=pg0, pg1=pg1):
                    for k in range(NCH):
                        st, sp = (k == 0), (k == NCH - 1)
                        q.matmul(self.PS[pg0][:, :],
                                 QWGT[:, slot * 1024 + k * 128: slot * 1024 + (k + 1) * 128],
                                 QXB[:, k * T + tof: k * T + tof + 512], start=st, stop=sp)
                        ins = q.matmul(self.PS[pg1][:, :],
                                       QWUT[:, slot * 1024 + k * 128: slot * 1024 + (k + 1) * 128],
                                       QXB[:, k * T + tof: k * T + tof + 512], start=st, stop=sp)
                    return ins
                if m == 0:
                    p.wait("tensor", s["scA"], s["scA"].v)
                    p.wait("tensor", s["vA"], s["vA"].v)
                mv = p.ei("tensor", s["mmA"], mkgu)
                gq_marks.append(mv)
                p.wait("vector", s["mmA"], mv)
                if m == 0:
                    p.drain("vector")
                p.ei("vector", s["vA"], lambda q, pg0=pg0: q.tensor_relu(
                    self.XSQ[:, :], self.PS[pg0][:, :]))
                vvr = s["vA"].v
                p.wait("scalar", s["vA"], vvr)
                p.drain("scalar")
                sv = p.ei("scalar", s["scA"], lambda q: q.activation(
                    self.NTMP[:, :], self.XSQ[:, :], AF.Square))
                p.wait("vector", s["scA"], sv)
                hv = p.ei("vector", s["vA"], lambda q, m=m, pg1=pg1: q.tensor_tensor(
                    self.ACC[:, m * 512:(m + 1) * 512], self.NTMP[:, :],
                    self.PS[pg1][:, :], Al.mult))
                hm_marks.append(hv)
                p.drain("vector")
                if m == 0:
                    p.ei("vector", s["vA"], lambda q: q.tensor_copy(
                        HMX[:, :], self.ACC[:, 0:512]))
                    p.ei("vector", s["vA"], lambda q: q.tensor_copy(
                        HMN[:, :], self.ACC[:, 0:512]))
                else:
                    p.ei("vector", s["vA"], lambda q, m=m: q.tensor_tensor(
                        HMX[:, :], HMX[:, :], self.ACC[:, m * 512:(m + 1) * 512], Al.max))
                    p.ei("vector", s["vA"], lambda q, m=m: q.tensor_tensor(
                        HMN[:, :], HMN[:, :], self.ACC[:, m * 512:(m + 1) * 512], Al.min))
            p.drain("vector")
            p.ei("vector", s["vA"], lambda q: q.tensor_scalar(
                HMN[:, :], HMN[:, :], -1.0, None, Al.mult))
            p.drain("vector")
            hmv = p.ei("vector", s["vA"], lambda q: q.tensor_tensor(
                HMX[:, :], HMX[:, :], HMN[:, :], Al.max))
            p.wait("tensor", s["vA"], hmv)
            for sc4 in range(4):
                bk2 = 6 + (sc4 % 2)
                p.wait("tensor", s["vB"], s["vB"].v - 1)
                mvt = p.ei("tensor", s["mmB"], lambda q, sc4=sc4, bk2=bk2: q.transpose(
                    self.PS[bk2][:, :].bitcast(BF16)[:, 0:128],
                    HMX[:, sc4 * 128:(sc4 + 1) * 128], self.IDB[:, :]))
                p.wait("vector", s["mmB"], mvt)
                p.drain("vector")
                p.ei("vector", s["vB"], lambda q, sc4=sc4, bk2=bk2: q.tensor_reduce(
                    SXC[:, sc4:sc4 + 1], self.PS[bk2][:, :].bitcast(BF16)[:, 0:128],
                    AX, Al.max))
            p.drain("vector")
            p.ei("vector", s["vB"], lambda q, tt=tt: q.tensor_tensor(
                SXC[:, 4:8], SXC[:, 32 + tt * 4: 36 + tt * 4],
                SXC[:, 32 + tt * 4: 36 + tt * 4], Al.mult))
            p.drain("vector")
            p.ei("vector", s["vB"], lambda q, tt=tt: q.tensor_tensor(
                SXC[:, 4:8], SXC[:, 4:8], SXC[:, 32 + tt * 4: 36 + tt * 4], Al.mult))
            p.drain("vector")
            p.ei("vector", s["vB"], lambda q: q.tensor_scalar(
                SXC[:, 4:8], SXC[:, 4:8], SWC[:, 7:8], None, Al.mult))
            p.drain("vector")
            p.ei("vector", s["vB"], lambda q: q.tensor_scalar(
                SXC[:, 8:12], SXC[:, 4:8], EPS, None, Al.mult))
            p.drain("vector")
            p.ei("vector", s["vB"], lambda q: q.tensor_tensor(
                SXC[:, 12:16], SXC[:, 8:12], SXC[:, 0:4], Al.max))
            p.drain("vector")
            p.ei("vector", s["vB"], lambda q: q.reciprocal(SXC[:, 0:4], SXC[:, 12:16]))
            p.drain("vector")
            p.ei("vector", s["vB"], lambda q: q.tensor_scalar(
                SXC[:, 0:4], SXC[:, 0:4], 127.0, None, Al.mult))
            p.drain("vector")
            p.ei("vector", s["vB"], lambda q: q.reciprocal(SXC[:, 8:12], SXC[:, 4:8]))
            p.drain("vector")
            p.ei("vector", s["vB"], lambda q: q.tensor_tensor(
                SXC[:, 16:20], SXC[:, 12:16], SXC[:, 8:12], Al.mult))
            p.drain("vector")
            p.ei("vector", s["vB"], lambda q: q.tensor_scalar(
                SXC[:, 16:20], SXC[:, 16:20], SWC[:, 6:7], 1.0 / 127.0,
                Al.mult, Al.mult))
            vv = s["vB"].v
            p.wait("tensor", s["vB"], vv)
            for sc4 in range(4):
                bk2 = 6 + (sc4 % 2)
                p.wait("tensor", s["vB"], s["vB"].v - 1)
                mvt = p.ei("tensor", s["mmB"], lambda q, sc4=sc4, bk2=bk2: q.transpose(
                    self.PS[bk2][0:1, 0:128].bitcast(F32R),
                    SXC[:, sc4:sc4 + 1].bitcast(F32R), self.IDR[:, :]))
                p.wait("vector", s["mmB"], mvt)
                p.drain("vector")
                p.ei("vector", s["vB"], lambda q, sc4=sc4, bk2=bk2: q.tensor_copy(
                    SHROW[0:1, sc4 * 128:(sc4 + 1) * 128], self.PS[bk2][0:1, 0:128]))
                p.wait("tensor", s["vB"])
                mvt = p.ei("tensor", s["mmB"], lambda q, sc4=sc4, bk2=bk2: q.transpose(
                    self.PS[bk2][0:1, 0:128].bitcast(F32R),
                    SXC[:, 16 + sc4:17 + sc4].bitcast(F32R), self.IDR[:, :]))
                p.wait("vector", s["mmB"], mvt)
                p.drain("vector")
                p.ei("vector", s["vB"], lambda q, sc4=sc4, bk2=bk2: q.tensor_copy(
                    OSCROW[0:1, sc4 * 128:(sc4 + 1) * 128], self.PS[bk2][0:1, 0:128]))
            vv = s["vB"].v
            p.wait("tensor", s["vB"], vv)
            mv = p.ei("tensor", s["mmA"], lambda q: q.matmul(
                self.PS[4][:, :], self.ONEROW[0:1, :], SHROW[0:1, :],
                start=True, stop=True))
            mv2 = p.ei("tensor", s["mmA"], lambda q: q.matmul(
                self.PS[5][:, :], self.ONEROW[0:1, :], OSCROW[0:1, :],
                start=True, stop=True))
            p.wait("vector", s["mmA"], mv2)
            p.wait("scalar", s["mmA"], mv2)
            osv = s["mmA"].v
            qh_marks = []
            for m in range(32):
                scr = self.NTMP if m % 2 == 0 else self.XSQ
                if m >= 2:
                    p.drain("vector")
                p.ei("vector", s["vA"], lambda q, m=m, scr=scr: q.tensor_tensor(
                    scr[:, :], self.ACC[:, m * 512:(m + 1) * 512],
                    self.PS[4][:, :], Al.mult))
                p.drain("vector")
                qh_marks.append(p.ei("vector", s["vA"], lambda q, m=m, scr=scr: q.tensor_scalar(
                    self.ACC[:, m * 512:(m + 1) * 512], scr[:, :],
                    MAGIC, -MAGIC, Al.add, Al.add)))
            for md in range(NCH):
                pg = 2 + (md % 2)
                p.wait("tensor", s["scA"])
                for quar in range(4):
                    dslot = len(dn_marks) % 2
                    sd = s["srw"] if dslot == 0 else s["sld2"]
                    p.wait("sync", s["mmA"], dn_marks[-2] if len(dn_marks) >= 2 else mm0)
                    lw = p.ei("sync", sd, lambda q, md=md, quar=quar, dslot=dslot: q.dma_start(
                        out=QWDT[:, dslot * 1024: dslot * 1024 + 1024],
                        in_=self.qwd_t[md][:, quar * 1024:(quar + 1) * 1024]), 16)
                    p.wait("tensor", sd, lw)
                    p.wait("tensor", s["vA"], qh_marks[quar * 8 + 7])
                    def mkdn(q, md=md, quar=quar, pg=pg, dslot=dslot):
                        for kk in range(8):
                            kh = quar * 8 + kk
                            ins = q.matmul(self.PS[pg][:, :],
                                           QWDT[:, dslot * 1024 + kk * 128: dslot * 1024 + (kk + 1) * 128],
                                           self.ACC[:, kh * 512:(kh + 1) * 512],
                                           start=(kh == 0), stop=(kh == 31))
                        return ins
                    mv = p.ei("tensor", s["mmA"], mkdn)
                    dn_marks.append(mv)
                if fin_marks:
                    p.wait("scalar", s["vA"], fin_marks[-1][0])
                p.wait("scalar", s["mmA"], dn_marks[-1])
                p.drain("scalar")
                scv = p.ei("scalar", s["scA"], lambda q, pg=pg: q.copy(
                    self.XSQ[:, :], self.PS[pg][:, :]))
                p.wait("vector", s["scA"], scv)
                p.drain("vector")
                p.ei("vector", s["vA"], lambda q: q.tensor_tensor(
                    self.XSQ[:, :], self.XSQ[:, :], self.PS[5][:, :], Al.mult))
                p.drain("vector")
                fv = p.ei("vector", s["vA"], lambda q, md=md, tof=tof: q.tensor_tensor(
                    self.X[:, md * T + tof: md * T + tof + 512],
                    self.X[:, md * T + tof: md * T + tof + 512],
                    self.XSQ[:, :], Al.add))
                fin_marks.append((fv, scv))
        p.run()

    def phase_final(self):
        nc, s = self.nc, self.sem
        p = Prog(nc)
        ROWA = self.sbuf_local("ROWf1", [1, T], F32R)
        ROWB = self.sbuf_local("ROWf2", [1, T], F32R)
        OTILE = self.sbuf_local("OTILE", [128, D], F32)
        self.ln_stats2(p, self.X, ROWA, ROWB)
        self.ln_rows_finalize(p, ROWA, ROWB)
        self.ln_norm(p, self.X, self.X, ROWB, ROWA)
        for i in range(16):
            p.wait("vector", s["sst"])
            for cch in range(NCH):
                bk = 6 + (cch % 2)
                p.wait("tensor", s["vB"], s["vB"].v - 1)
                p.wait("tensor", s["vA"])
                mv = p.ei("tensor", s["mmB"], lambda q, i=i, cch=cch, bk=bk: q.transpose(
                    self.PS[bk][:, 0:128].bitcast(F32R),
                    self.X[:, cch * T + i * 128: cch * T + (i + 1) * 128], self.IDR[:, :]))
                p.wait("vector", s["mmB"], mv)
                p.drain("vector")
                p.ei("vector", s["vB"], lambda q, cch=cch, bk=bk: q.tensor_copy(
                    OTILE[:, cch * 128:(cch + 1) * 128], self.PS[bk][:, 0:128]))
            p.wait("scalar", s["vB"], s["vB"].v)
            p.ei("scalar", s["sst"], lambda q, i=i: q.dma_start(
                out=self.out[i * 128:(i + 1) * 128, :], in_=OTILE[:, :]), 16)
        p.wait("scalar", s["sst"], s["sst"].v)
        p.run()


def build_nc():
    b = Builder()
    b.conv_ep_marks = {}
    return b.build()


def kernel(**inputs):
    in_maps = _host_layout(inputs)
    nc = build_nc()
    res = run_bass_kernel_spmd(nc, in_maps, core_ids=list(range(8)))
    full = np.zeros((4, 4096, D), np.float32)
    for core in range(8):
        b, half = core // 2, core % 2
        full[b, half * 2048:(half + 1) * 2048] = res.results[core]["out"]
    return full


# version 14 (final)
# speedup vs baseline: 1.0181x; 1.0181x over previous
import numpy as np
from contextlib import ExitStack

import concourse.bass as bass
import concourse.mybir as mybir
from concourse.bass_utils import run_bass_kernel_spmd

F32 = mybir.dt.float32
F32R = mybir.dt.float32r
BF16 = mybir.dt.bfloat16
AF = mybir.ActivationFunctionType
Al = mybir.AluOpType
AX = mybir.AxisListType.X

T = 2048
D = 1024
NCH = 8
HE = 1408
HCH = 11
CH = 128
NCK = T // CH
HEADS, DK, DV = 16, 32, 64
FI = 4096
EPS = 1e-5
MAGIC = float(np.float32(1.5 * 2 ** 23))
LN_DKS = float(np.log(DK ** -0.5))

TAPS = ()
PHASES = 99
DBGOUT = False



def _pack_cols(*arrs):
    cols = []
    for a in arrs:
        a = np.ascontiguousarray(np.asarray(a, np.float32))
        if a.ndim == 2 and a.shape[0] == 128 and a.shape[1] <= 16:
            cols.append(np.ascontiguousarray(a))
            continue
        a = a.reshape(-1)
        if a.size <= 128:
            c = np.zeros(128, np.float32)
            c[: a.size] = a
            cols.append(c[:, None])
        else:
            assert a.size % 128 == 0
            cols.append(a.reshape(-1, 128).T)
    return np.ascontiguousarray(np.concatenate(cols, axis=1))


def _cpack_index():
    sizes = {
        "b1": 8 * HCH, "b3": 8 * HCH, "b2": 8 * NCH,
        "sb1": HCH, "sb3": HCH, "sb2": NCH,
        "lnw": NCH, "lnb": NCH, "gb": 1, "gbrow": 8,
        "l1b1": 16, "l1b2": NCH, "flag": 1,
    }
    off, idx = 0, {}
    for k, s in sizes.items():
        idx[k] = off
        off += s
    return idx, off


CPI, CP_COLS = _cpack_index()


def _ctile(w, nk, nm):
    return np.ascontiguousarray(
        w.reshape(nk, 128, nm, 128).transpose(2, 1, 0, 3).reshape(nm, 128, nk * 128))


def _rtile(w, nk, nn):
    return np.ascontiguousarray(
        w.reshape(nk, 128, nn, 512).transpose(0, 2, 1, 3))


def _host_layout(inputs):
    inp = {k: np.ascontiguousarray(np.asarray(v, np.float32)) for k, v in inputs.items()}
    filt = inp["L0_attn_filt"]
    fpad = np.zeros(128 * 17 + 127, np.float32)
    fpad[127:127 + 2048] = filt
    d_i = np.arange(17)[:, None, None]
    q_i = np.arange(128)[None, :, None]
    p_i = np.arange(128)[None, None, :]
    toep = fpad[127 + 128 * d_i + q_i - p_i]
    toep_t = np.ascontiguousarray(toep.transpose(1, 0, 2).reshape(128, 17 * 128))

    gkw2e = np.ascontiguousarray(np.concatenate(
        [inp["L1_attn_gk_w2"], inp["L1_attn_gk_b2"][None, :]], axis=0))
    gate_wT = np.ascontiguousarray(
        inp["L0_ffn_gate_w"].T.reshape(8, 128, 8).transpose(1, 0, 2).reshape(128, 64))
    gkw1t = np.ascontiguousarray(
        inp["L1_attn_gk_w1"].reshape(8, 128, 16).transpose(1, 0, 2).reshape(128, 128))
    rms_wb = np.ascontiguousarray(
        np.broadcast_to(inp["L1_attn_rms_w"][None, :], (128, DV)))

    w1t = np.concatenate([_ctile(inp["L0_ffn_w1"][e], 8, HCH) for e in range(8)]
                         + [_ctile(inp["L0_ffn_sw1"], 8, HCH)])
    w3t = np.concatenate([_ctile(inp["L0_ffn_w3"][e], 8, HCH) for e in range(8)]
                         + [_ctile(inp["L0_ffn_sw3"], 8, HCH)])
    w2t = np.concatenate([_ctile(inp["L0_ffn_w2"][e], HCH, 8) for e in range(8)]
                         + [_ctile(inp["L0_ffn_sw2"], HCH, 8)])

    common = {
        "toep": toep_t, "gkw2e": gkw2e, "gate_wT": gate_wT, "rms_wb": rms_wb,
        "w1": w1t, "w3": w3t, "w2": w2t,
        "Wq": _ctile(inp["L1_attn_Wq"], 8, 4), "Wk": _ctile(inp["L1_attn_Wk"], 8, 4),
        "Wv": _rtile(inp["L1_attn_Wv"], 8, 2), "Wg": _rtile(inp["L1_attn_Wg"], 8, 2),
        "Wo": _ctile(inp["L1_attn_Wo"], 8, 8),
        "gkw1": gkw1t,
        "W1": _ctile(inp["L1_attn_W1"], 8, 16), "W2": _ctile(inp["L1_attn_W2"], 16, 8),
        "wgateT": np.ascontiguousarray(inp["L1_ffn_w_gate"].T),
        "wupT": np.ascontiguousarray(inp["L1_ffn_w_up"].T),
        "wdownT": np.ascontiguousarray(inp["L1_ffn_w_down"].T),
    }
    x = inp["L0_attn_x"]
    in_maps = []
    for core in range(8):
        b, half = core // 2, core % 2
        x_ext = np.zeros((4096, D), np.float32)
        avail = 4096 - half * 2048
        x_ext[:avail] = x[b, half * 2048: half * 2048 + avail]
        cpack = _pack_cols(
            inp["L0_ffn_b1"], inp["L0_ffn_b3"], inp["L0_ffn_b2"],
            inp["L0_ffn_sb1"], inp["L0_ffn_sb3"], inp["L0_ffn_sb2"],
            inp["L1_attn_ln_w"], inp["L1_attn_ln_b"],
            inp["L0_ffn_gate_b"],
            np.broadcast_to(inp["L0_ffn_gate_b"][None, :], (128, 8)),
            inp["L1_attn_b1"], inp["L1_attn_b2"],
            np.full(128, float(half), np.float32),
        )
        m = dict(common)
        m["x_ext"] = x_ext
        m["cpack"] = cpack
        in_maps.append(m)
    return in_maps



ENGINES = ("sync", "gpsimd", "tensor", "vector", "scalar")


class SemT:
    def __init__(self, ctx, nc, name):
        self.h = ctx.enter_context(nc.semaphore(name))
        self.v = 0


class Prog:
    def __init__(self, nc):
        self.nc = nc
        self.streams = {e: [] for e in ENGINES}

    def em(self, eng, fn):
        self.streams[eng].append(fn)

    def wait(self, eng, sem, val=None):
        v = sem.v if val is None else val
        if v > 0:
            self.streams[eng].append(lambda q, s=sem, v=v: q.wait_ge(s.h, v))

    def drain(self, eng):
        self.streams[eng].append(lambda q: q.drain())

    def ei(self, eng, sem, fn, k=1):
        if sem is None:
            self.streams[eng].append(fn)
            return None
        sem.v += k
        self.streams[eng].append(lambda q, fn=fn, sem=sem, k=k: fn(q).then_inc(sem.h, k))
        return sem.v

    def run(self):
        with self.nc.Block() as block:
            for eng in ENGINES:
                fns = self.streams[eng]
                if not fns:
                    continue
                getattr(block, eng)(lambda q, fns=fns: [f(q) for f in fns])



class Builder:
    def __init__(self):
        self.nc = bass.Bass("TRN2", target_bir_lowering=False)
        nc = self.nc
        self.dram = {}

        def din(name, shape, dtype=F32R):
            self.dram[name] = nc.dram_tensor(name, list(shape), dtype,
                                             kind="ExternalInput").ap()
        din("x_ext", (4096, D))
        din("toep", (128, 17 * 128))
        din("gate_wT", (128, 64))
        din("w1", (99, 128, 1024)); din("w3", (99, 128, 1024))
        din("w2", (72, 128, HE))
        din("Wq", (4, 128, 1024)); din("Wk", (4, 128, 1024))
        din("Wv", (8, 2, 128, 512)); din("Wg", (8, 2, 128, 512))
        din("Wo", (8, 128, 1024)); din("gkw1", (128, 128)); din("gkw2e", (17, 512))
        din("W1", (16, 128, 1024)); din("W2", (8, 128, 2048))
        din("wgateT", (D, FI)); din("wupT", (D, FI)); din("wdownT", (FI, D))
        din("cpack", (128, CP_COLS), F32)
        din("rms_wb", (128, DV), F32)
        self.out = nc.dram_tensor("out", [T, D], F32, kind="ExternalOutput").ap()
        self.taps = {t: nc.dram_tensor("tap_" + t, [T, D], F32,
                                       kind="ExternalOutput").ap() for t in TAPS}
        self.x2_spill = nc.dram_tensor("x2_spill", [128, NCH * T], F32R).ap()
        self.cc_in = nc.dram_tensor("cc_in", [128, 1024], F32).ap()
        self.cc_out = nc.dram_tensor("cc_out", [128, 1024], F32).ap()
        self.qst_d = nc.dram_tensor("qst_d", [128, NCK * 512], BF16).ap()
        qk = "ExternalOutput" if DBGOUT else "Internal"
        self.qwg_t = nc.dram_tensor("qwg_t", [32, 128, 1024], BF16, kind=qk).ap()
        self.qwu_t = nc.dram_tensor("qwu_t", [32, 128, 1024], BF16, kind=qk).ap()
        self.qwd_t = nc.dram_tensor("qwd_t", [NCH, 128, FI], F32R, kind=qk).ap()
        if DBGOUT:
            self.dbg_swc = nc.dram_tensor("dbg_swc", [128, 16], F32, kind=qk).ap()
            self.dbg_mc = nc.dram_tensor("dbg_mc", [128, 16], F32, kind=qk).ap()
            self.dbg_tern = nc.dram_tensor("dbg_tern", [128, 2048], F32, kind=qk).ap()
            self.dbg_wt = nc.dram_tensor("dbg_wt", [128, 8192], F32, kind=qk).ap()

        self.ctx = ExitStack()
        self.sb = {}

    def sbuf(self, name, shape, dtype):
        t = self.ctx.enter_context(self.nc.sbuf_tensor(name, list(shape), dtype))
        self.sb[name] = t
        return t

    def sbuf_local(self, name, shape, dtype):
        t = self.phase_ctx.enter_context(self.nc.sbuf_tensor(name, list(shape), dtype))
        self.sb[name] = t
        return t

    def cpc(self, key, j=0):
        return self.CP[:, CPI[key] + j: CPI[key] + j + 1]

    def build(self):
        nc = self.nc
        with self.ctx, nc.allow_low_precision(
                "f32r (12-bit significand) stream accumulation is intentional; "
                "precision exceeds the sanctioned bf16+fp32 bar"):
            self.X = self.sbuf("X", [128, NCH * T], F32R)
            self.ACC = self.sbuf("ACC", [128, NCH * T], F32R)
            save_base = nc.sbuf_base
            nc.sbuf_base = save_base - NCH * T * 4
            self.ACCB = nc.sbuf_tensor("ACCB", [128, 2 * NCH * T], BF16).__enter__()
            assert nc.sbuf_base == save_base
            self.CP = self.sbuf("CP", [128, CP_COLS], F32)
            self.IDR = self.sbuf("IDR", [128, 128], F32R)
            self.IDB = self.sbuf("IDB", [128, 128], BF16)
            self.MASKB = self.sbuf("MASKB", [128, 128], BF16)
            self.ONECOL = self.sbuf("ONECOL", [128, 1], F32R)
            self.ONEROW = self.sbuf("ONEROW", [1, 128], F32R)
            self.ROWA = self.sbuf("ROWA", [1, T], F32R)
            self.ROWB = self.sbuf("ROWB", [1, T], F32R)
            self.XSQ = self.sbuf("XSQ", [128, 512], F32R)
            self.NTMP = self.sbuf("NTMP", [128, 512], F32R)
            self.PS = [self.ctx.enter_context(nc.psum_tensor(f"PS{i}", [128, 512], F32))
                       for i in range(8)]
            self.sem = {n: SemT(self.ctx, nc, n) for n in
                        ["sld", "sldB", "sld2", "sld2B", "gld", "gldB", "gq", "gqB",
                         "gk2", "guc", "srw", "sst", "mmA", "mmB", "vA", "vB", "scA",
                         "cc", "ginit"]}
            phases = [self.phase_init]
            if PHASES >= 1:
                phases += [self.phase_conv, lambda: self.tap("x1")]
            if PHASES >= 2:
                phases += [self.phase_moe, lambda: self.tap("x2")]
            if PHASES >= 3:
                phases += [self.phase_gla_d0, self.phase_gla_main,
                           self.phase_gla_post, lambda: self.gla_ctx.close(),
                           lambda: self.tap("xp")]
            if PHASES >= 4:
                phases += [self.phase_mlp, lambda: self.tap("x3")]
            if PHASES >= 5:
                phases += [self.phase_bitffn_prep, self.phase_bitffn,
                           lambda: self.bit_ctx.close(), lambda: self.tap("fin2")]
            if PHASES >= 6:
                phases += [self.phase_final]
            for ph in phases:
                with ExitStack() as pc:
                    self.phase_ctx = pc
                    name = getattr(ph, "__name__", "misc")
                    with nc.named_scope(name):
                        ph()
        return nc

    def phase_init(self):
        p = Prog(self.nc)
        s = self.sem
        IOTA = self.sbuf_local("IOTA", [128, 128], F32)
        p.ei("sync", s["sld"], lambda q: q.dma_start(out=self.CP[:, :], in_=self.dram["cpack"]), 16)

        gi = p.ei("gpsimd", s["ginit"], lambda q: q.iota(
            IOTA[:, :], [[1, 128]], base=0, channel_multiplier=-1,
            allow_small_or_imprecise_dtypes=True))
        p.wait("vector", s["ginit"], gi)
        p.ei("vector", s["vA"], lambda q: q.tensor_single_scalar(self.IDR[:, :], IOTA[:, :], 0.0, Al.is_equal))
        p.ei("vector", s["vA"], lambda q: q.tensor_single_scalar(self.IDB[:, :], IOTA[:, :], 0.0, Al.is_equal))
        p.ei("vector", s["vA"], lambda q: q.tensor_single_scalar(self.MASKB[:, :], IOTA[:, :], 0.0, Al.is_ge))
        p.ei("vector", s["vA"], lambda q: q.tensor_single_scalar(
            self.ONECOL[:, :], IOTA[:, 0:1], -1e30, Al.is_ge))
        p.ei("vector", s["vA"], lambda q: q.tensor_single_scalar(
            self.ONEROW[:, :], IOTA[0:1, 0:128], -1e30, Al.is_ge))
        p.run()

    def ln_stats2(self, p, buf, row_a, row_b, sq=None):
        s = self.sem
        sq = self.ACC if sq is None else sq
        mmB_prev = 0
        for t in range(8):
            tof = t * 256
            p.wait("tensor", s["vA"])
            def mksum(q, tof=tof):
                for c in range(NCH):
                    ins = q.matmul(self.PS[6][0:1, 0:256], self.ONECOL[:, :],
                                   buf[:, c * T + tof: c * T + tof + 256],
                                   start=(c == 0), stop=(c == NCH - 1))
                return ins
            mv = p.ei("tensor", s["mmA"], mksum)
            p.wait("vector", s["mmA"], mv)
            p.drain("vector")
            p.ei("vector", s["vA"], lambda q, tof=tof: q.tensor_copy(
                row_a[0:1, tof: tof + 256], self.PS[6][0:1, 0:256]))
            p.wait("scalar", s["mmB"], mmB_prev)
            p.drain("scalar")
            for c in range(NCH):
                scv = p.ei("scalar", s["scA"], lambda q, c=c, tof=tof: q.activation(
                    sq[:, c * 256:(c + 1) * 256],
                    buf[:, c * T + tof: c * T + tof + 256], AF.Square))
            p.wait("tensor", s["scA"], scv)
            def mksq(q):
                for c in range(NCH):
                    ins = q.matmul(self.PS[7][0:1, 0:256], self.ONECOL[:, :],
                                   sq[:, c * 256:(c + 1) * 256],
                                   start=(c == 0), stop=(c == NCH - 1))
                return ins
            mmB_prev = p.ei("tensor", s["mmB"], mksq)
            p.wait("vector", s["mmB"], mmB_prev)
            p.drain("vector")
            p.ei("vector", s["vA"], lambda q, tof=tof: q.tensor_copy(
                row_b[0:1, tof: tof + 256], self.PS[7][0:1, 0:256]))

    def ln_rows_finalize(self, p, row_a, row_b, n=D):
        s = self.sem
        p.drain("vector")
        p.ei("vector", s["vA"], lambda q: q.tensor_scalar(
            row_a[0:1, :], row_a[0:1, :], 1.0 / n, None, Al.mult))
        p.drain("vector")
        p.ei("vector", s["vA"], lambda q: q.tensor_scalar(
            row_b[0:1, :], row_b[0:1, :], 1.0 / n, EPS, Al.mult, Al.add))
        for seg in range(4):
            sl = slice(seg * 512, seg * 512 + 512)
            p.drain("vector")
            p.ei("vector", s["vA"], lambda q, sl=sl: q.scalar_tensor_tensor(
                self.XSQ[0:1, 0:512], row_a[0:1, sl], -1.0, row_a[0:1, sl],
                Al.mult, Al.mult))
            p.drain("vector")
            p.ei("vector", s["vA"], lambda q, sl=sl: q.tensor_tensor(
                row_b[0:1, sl], row_b[0:1, sl], self.XSQ[0:1, 0:512], Al.add))
        p.drain("vector")
        vv = p.ei("vector", s["vA"], lambda q: q.reciprocal(
            row_b[0:1, :], row_b[0:1, :]))
        p.wait("scalar", s["vA"], vv)
        p.drain("scalar")
        scv = p.ei("scalar", s["scA"], lambda q: q.activation(
            row_b[0:1, :], row_b[0:1, :], AF.Sqrt))
        p.wait("vector", s["scA"], scv)
        p.drain("vector")
        p.ei("vector", s["vA"], lambda q: q.scalar_tensor_tensor(
            row_a[0:1, :], row_a[0:1, :], -1.0, row_b[0:1, :], Al.mult, Al.mult))

    def ln_norm(self, p, src, dst, row_rstd, row_mrstd, affine=None, dst_dtype=None):
        s = self.sem
        p.wait("tensor", s["vA"], s["vA"].v)
        p.wait("tensor", s["scA"], s["scA"].v)
        for t in range(4):
            tof = t * 512
            def mkb(q, tof=tof):
                q.matmul(self.PS[4][:, :], self.ONEROW[0:1, :],
                         row_rstd[0:1, tof: tof + 512].bitcast(F32R), start=True, stop=True)
                return q.matmul(self.PS[5][:, :], self.ONEROW[0:1, :],
                                row_mrstd[0:1, tof: tof + 512].bitcast(F32R), start=True, stop=True)
            mv = p.ei("tensor", s["mmA"], mkb)
            p.wait("vector", s["mmA"], mv)
            for c in range(NCH):
                p.drain("vector")
                p.ei("vector", s["vA"], lambda q, c=c, tof=tof: q.tensor_tensor(
                    self.NTMP[:, :], src[:, c * T + tof: c * T + tof + 512],
                    self.PS[4][:, :], Al.mult))
                p.drain("vector")
                if affine is None:
                    vv = p.ei("vector", s["vA"], lambda q, c=c, tof=tof: q.tensor_tensor(
                        dst[:, c * T + tof: c * T + tof + 512], self.NTMP[:, :],
                        self.PS[5][:, :], Al.add))
                else:
                    wk, bk = affine
                    p.ei("vector", s["vA"], lambda q, c=c, tof=tof: q.tensor_tensor(
                        self.NTMP[:, :], self.NTMP[:, :], self.PS[5][:, :], Al.add))
                    p.drain("vector")
                    vv = p.ei("vector", s["vA"], lambda q, c=c, tof=tof, wk=wk, bk=bk: q.tensor_scalar(
                        dst[:, c * T + tof: c * T + tof + 512], self.NTMP[:, :],
                        self.cpc(wk, c), self.cpc(bk, c), Al.mult, Al.add))
            p.wait("tensor", s["vA"], s["vA"].v)

    def phase_conv(self):
        nc, s = self.nc, self.sem
        p = Prog(nc)
        RING = 18
        xr0 = self.sbuf_local("xre0", [128, D], F32R)
        xr1 = self.sbuf_local("xre1", [128, D], F32R)
        TOEP = self.sbuf_local("TOEP", [128, 17 * 128], F32R)
        xtm = self.sbuf_local("xtm_b", [128, D], F32R)
        stt = self.sbuf_local("stt_b", [128, 4], F32)
        sq1 = self.sbuf_local("sq1_b", [128, D], F32)
        cvE = SemT(self.ctx, nc, "cvE")

        def ring(j):
            sl = j % RING
            if sl < 16:
                return self.ACC[:, sl * D:(sl + 1) * D]
            return (xr0 if sl == 16 else xr1)[:, :]

        def emit_load(j):
            if j >= RING:
                p.wait("sync", cvE, j - RING + 1)
            p.ei("sync", s["sld"], lambda q, j=j: q.dma_start(
                out=ring(j), in_=self.dram["x_ext"][j * 128:(j + 1) * 128, :]), 16)

        p.ei("sync", s["sld"], lambda q: q.dma_start(
            out=TOEP[:, :], in_=self.dram["toep"]), 16)
        for j in range(17):
            emit_load(j)

        mmA = s["mmA"]
        for i in range(16):
            if i + 17 < 32:
                emit_load(i + 17)
            b0, b1 = (i % 2) * 2, (i % 2) * 2 + 1
            if i >= 2:
                p.wait("tensor", s["vA"], self.conv_ep_marks[i - 2])
            p.wait("tensor", s["sld"])
            def mkconv(q, i=i, b0=b0, b1=b1):
                for d in range(17):
                    st, sp = (d == 0), (d == 16)
                    lhsT = TOEP[:, d * 128:(d + 1) * 128]
                    src = ring(i + d)
                    q.matmul(self.PS[b0][:, :], lhsT, src[:, 0:512], start=st, stop=sp)
                    ins = q.matmul(self.PS[b1][:, :], lhsT, src[:, 512:1024], start=st, stop=sp)
                return ins
            mv = p.ei("tensor", mmA, mkconv)
            p.wait("vector", mmA, mv)
            p.drain("vector")
            p.ei("vector", s["vA"], lambda q, i=i, b0=b0: q.tensor_tensor(
                xtm[:, 0:512], ring(i)[:, 0:512], self.PS[b0][:, :], Al.add))
            p.drain("vector")
            vv = p.ei("vector", cvE, lambda q, i=i, b1=b1: q.tensor_tensor(
                xtm[:, 512:1024], ring(i)[:, 512:1024], self.PS[b1][:, :], Al.add))
            p.wait("scalar", cvE, vv)
            p.drain("scalar")
            scv = p.ei("scalar", s["scA"], lambda q: q.activation(sq1[:, :], xtm[:, :], AF.Square))
            p.wait("vector", s["scA"], scv)
            p.drain("vector")
            p.ei("vector", s["vA"], lambda q: q.reduce_sum(stt[:, 0:1], xtm[:, :], AX))
            p.drain("vector")
            p.ei("vector", s["vA"], lambda q: q.reduce_sum(stt[:, 1:2], sq1[:, :], AX))
            p.drain("vector")
            p.ei("vector", s["vA"], lambda q: q.tensor_scalar(
                stt[:, 0:1], stt[:, 0:1], 1.0 / D, None, Al.mult))
            p.drain("vector")
            p.ei("vector", s["vA"], lambda q: q.scalar_tensor_tensor(
                stt[:, 2:3], stt[:, 0:1], -1.0, stt[:, 0:1], Al.mult, Al.mult))
            p.drain("vector")
            p.ei("vector", s["vA"], lambda q: q.tensor_scalar(
                stt[:, 1:2], stt[:, 1:2], 1.0 / D, EPS, Al.mult, Al.add))
            p.drain("vector")
            p.ei("vector", s["vA"], lambda q: q.tensor_tensor(
                stt[:, 1:2], stt[:, 1:2], stt[:, 2:3], Al.add))
            p.drain("vector")
            vv = p.ei("vector", s["vA"], lambda q: q.reciprocal(stt[:, 2:3], stt[:, 1:2]))
            p.wait("scalar", s["vA"], vv)
            p.drain("scalar")
            scv = p.ei("scalar", s["scA"], lambda q: q.activation(
                stt[:, 3:4], stt[:, 2:3], AF.Sqrt))
            p.wait("vector", s["scA"], scv)
            p.drain("vector")
            vv = p.ei("vector", s["vA"], lambda q: q.tensor_scalar(
                xtm[:, :], xtm[:, :], stt[:, 0:1], stt[:, 3:4], Al.subtract, Al.mult))
            p.wait("tensor", s["vA"], vv)
            for c in range(NCH):
                bk = 4 + (c % 2)
                p.wait("tensor", s["vB"], s["vB"].v - 1)
                mv2 = p.ei("tensor", s["mmB"], lambda q, c=c, bk=bk: q.transpose(
                    self.PS[bk][:, 0:128].bitcast(F32R), xtm[:, c * 128:(c + 1) * 128],
                    self.IDR[:, :]))
                p.wait("vector", s["mmB"], mv2)
                p.drain("vector")
                p.ei("vector", s["vB"], lambda q, c=c, bk=bk, i=i: q.tensor_copy(
                    self.X[:, c * T + i * 128: c * T + (i + 1) * 128], self.PS[bk][:, 0:128]))
            self.conv_ep_marks[i] = s["vA"].v
        p.wait("tensor", s["vB"], s["vB"].v)
        p.run()

    def tap(self, name):
        if name not in self.taps:
            return
        nc, s = self.nc, self.sem
        p = Prog(nc)
        ttile = self.sbuf_local("tap_tm_" + name, [128, D], F32)
        for i in range(16):
            p.wait("vector", s["sst"])
            for c in range(NCH):
                bk = 6 + (c % 2)
                p.wait("tensor", s["vB"], s["vB"].v - 1)
                mv = p.ei("tensor", s["mmB"], lambda q, i=i, c=c, bk=bk: q.transpose(
                    self.PS[bk][:, 0:128].bitcast(F32R),
                    self.X[:, c * T + i * 128: c * T + (i + 1) * 128], self.IDR[:, :]))
                p.wait("vector", s["mmB"], mv)
                p.drain("vector")
                p.ei("vector", s["vB"], lambda q, c=c, bk=bk: q.tensor_copy(
                    ttile[:, c * 128:(c + 1) * 128], self.PS[bk][:, 0:128]))
            p.wait("scalar", s["vB"], s["vB"].v)
            p.ei("scalar", s["sst"], lambda q, i=i: q.dma_start(
                out=self.taps[name][i * 128:(i + 1) * 128, :], in_=ttile[:, :]), 16)
            p.wait("scalar", s["sst"], s["sst"].v)
        p.run()


    def phase_moe(self):
        nc, s = self.nc, self.sem
        p = Prog(nc)
        GWT = self.sbuf_local("GWT", [128, 64], F32R)
        RW = self.sbuf_local("RW", [8, T], BF16)
        RWSB = self.sbuf_local("RWSB", [128, 2 * 1024], BF16)
        RWROW = self.sbuf_local("RWROW", [1, 512], BF16)
        ONEROWB = self.sbuf_local("ONEROWB", [1, 128], BF16)
        WAB = self.sbuf_local("WAB", [128, 2048], F32R)
        W3B = self.sbuf_local("W3B", [128, 2048], F32R)
        W2B = self.sbuf_local("W2B", [128, 2 * 1408], BF16)
        HB = self.sbuf_local("HB", [128, HCH * 1024], BF16)
        VS1 = self.XSQ
        VS2 = self.NTMP

        p.ei("vector", s["vA"], lambda q: q.tensor_copy(
            ONEROWB[0:1, :], self.ONEROW[0:1, :]))
        for c in range(NCH):
            p.drain("vector")
            p.ei("vector", s["vA"], lambda q, c=c: q.tensor_copy(
                self.ACC[:, c * T:(c + 1) * T], self.X[:, c * T:(c + 1) * T]))

        p.wait("sync", s["mmA"], s["mmA"].v)
        gv = p.ei("sync", s["sld"], lambda q: q.dma_start(
            out=GWT[:, :], in_=self.dram["gate_wT"]), 16)
        if True:
            GROW = self.XSQ[0:8, 0:512].bitcast(F32R)
            GTM = self.NTMP[:, 0:40]
            p.wait("tensor", s["sld"], gv)
            for t in range(4):
                tof = t * 512
                def mkg(q, tof=tof):
                    for c in range(NCH):
                        ins = q.matmul(self.PS[6][0:8, :], GWT[:, c * 8:(c + 1) * 8],
                                       self.X[:, c * T + tof: c * T + tof + 512],
                                       start=(c == 0), stop=(c == NCH - 1))
                    return ins
                mv = p.ei("tensor", s["mmA"], mkg)
                p.wait("scalar", s["mmA"], mv)
                p.drain("scalar")
                scv = p.ei("scalar", s["scA"], lambda q: q.copy(GROW[0:8, :], self.PS[6][0:8, :]))
                p.wait("tensor", s["scA"], scv)
                for sc in range(4):
                    mv = p.ei("tensor", s["mmB"], lambda q, sc=sc: q.transpose(
                        self.PS[7][0:128, 0:8].bitcast(F32R), GROW[0:8, sc * 128:(sc + 1) * 128],
                        self.IDR[0:8, 0:8]))
                    p.wait("vector", s["mmB"], mv)
                    p.drain("vector")
                    p.ei("vector", s["vA"], lambda q: q.tensor_copy(
                        GTM[:, 0:8], self.PS[7][0:128, 0:8]))
                    p.drain("vector")
                    p.ei("vector", s["vA"], lambda q: q.reduce_max(
                        GTM[:, 32:33], GTM[:, 0:8], AX))
                    p.drain("vector")
                    p.ei("vector", s["vA"], lambda q: q.tensor_scalar(
                        GTM[:, 8:16], GTM[:, 0:8], GTM[:, 32:33].bitcast(F32), None, Al.subtract))
                    vv = s["vA"].v
                    p.wait("scalar", s["vA"], vv)
                    p.drain("scalar")
                    scv = p.ei("scalar", s["scA"], lambda q: q.activation(
                        GTM[:, 8:16], GTM[:, 8:16], AF.Exp))
                    p.wait("vector", s["scA"], scv)
                    p.drain("vector")
                    p.ei("vector", s["vA"], lambda q: q.reduce_sum(
                        GTM[:, 33:34], GTM[:, 8:16], AX))
                    p.drain("vector")
                    p.ei("vector", s["vA"], lambda q: q.reciprocal(
                        GTM[:, 34:35], GTM[:, 33:34]))
                    p.drain("vector")
                    p.ei("vector", s["vA"], lambda q: q.tensor_scalar(
                        GTM[:, 8:16], GTM[:, 8:16], GTM[:, 34:35].bitcast(F32), None, Al.mult))
                    p.drain("vector")
                    p.ei("vector", s["vA"], lambda q: q.tensor_tensor(
                        GTM[:, 0:8], GTM[:, 8:16],
                        self.CP[:, CPI["gbrow"]: CPI["gbrow"] + 8], Al.add))
                    p.drain("vector")
                    p.ei("vector", s["vA"], lambda q: q.reduce_max(
                        GTM[:, 32:33], GTM[:, 0:8], AX))
                    p.drain("vector")
                    p.ei("vector", s["vA"], lambda q: q.tensor_scalar(
                        GTM[:, 16:24], GTM[:, 0:8], GTM[:, 32:33].bitcast(F32), None, Al.is_equal))
                    p.drain("vector")
                    p.ei("vector", s["vA"], lambda q: q.scalar_tensor_tensor(
                        GTM[:, 16:24], GTM[:, 16:24], -1e30, GTM[:, 0:8], Al.mult, Al.add))
                    p.drain("vector")
                    p.ei("vector", s["vA"], lambda q: q.reduce_max(
                        GTM[:, 32:33], GTM[:, 16:24], AX))
                    p.drain("vector")
                    p.ei("vector", s["vA"], lambda q: q.tensor_scalar(
                        GTM[:, 16:24], GTM[:, 0:8], GTM[:, 32:33].bitcast(F32), None, Al.is_ge))
                    p.drain("vector")
                    vv = p.ei("vector", s["vA"], lambda q: q.tensor_tensor(
                        GTM[:, 24:32], GTM[:, 8:16], GTM[:, 16:24], Al.mult))
                    p.wait("tensor", s["vA"], vv)
                    mv = p.ei("tensor", s["mmB"], lambda q: q.transpose(
                        self.PS[7][0:8, 0:128].bitcast(F32R), GTM[:, 24:32].bitcast(F32R),
                        self.IDR[:, :]))
                    p.wait("vector", s["mmB"], mv)
                    p.drain("vector")
                    vv = p.ei("vector", s["vA"], lambda q, tof=tof, sc=sc: q.tensor_copy(
                        RW[0:8, tof + sc * 128: tof + (sc + 1) * 128], self.PS[7][0:8, 0:128]))
                    p.wait("tensor", s["vA"], vv)
                p.wait("tensor", s["scA"], s["scA"].v)

        mmA = s["mmA"]

        def w1_slice(e, mh):
            return self.dram["w1"][e * HCH + mh]

        def w3_slice(e, mh):
            return self.dram["w3"][e * HCH + mh]

        def w2_slice(e, md):
            return self.dram["w2"][e * 8 + md]

        mm0 = mmA.v
        a_marks = []
        b_marks = []
        ep_v = []
        ep_s = []
        bep_v = []
        rw_s = []
        rw_units = [(h, e) for h in range(2) for e in range(8)]
        rw_ptr = [0]

        def emit_rw(p):
            if rw_ptr[0] >= len(rw_units):
                return
            h_t, e_t = rw_units[rw_ptr[0]]
            eslot = rw_ptr[0] % 2
            rw_ptr[0] += 1
            tof_t = h_t * 1024
            if ep_v:
                p.wait("tensor", s["vA"], ep_v[-1])
            else:
                p.wait("tensor", s["vA"], s["vA"].v)
            for sub, bank in ((0, 6), (1, 7)):
                p.wait("scalar", s["mmB"], s["mmB"].v)
                rv = p.ei("scalar", s["srw"], lambda q, e_t=e_t, tof_t=tof_t, sub=sub:
                          q.dma_start(
                    out=RWROW[0:1, :],
                    in_=RW[e_t:e_t + 1, tof_t + sub * 512: tof_t + sub * 512 + 512]), 16)
                p.wait("tensor", s["srw"], rv)
                mv = p.ei("tensor", s["mmB"], lambda q, bank=bank: q.matmul(
                    self.PS[bank][:, :], ONEROWB[0:1, :],
                    RWROW[0:1, :], start=True, stop=True))
                p.wait("scalar", s["mmB"], mv)
                p.drain("scalar")
                scv = p.ei("scalar", s["scA"], lambda q, sub=sub, bank=bank, eslot=eslot:
                           q.copy(RWSB[:, eslot * 1024 + sub * 512:
                                       eslot * 1024 + sub * 512 + 512],
                                  self.PS[bank][:, :]))
                rw_s.append(scv)

        emit_rw(p)
        nrw = 0
        for half in range(2):
            tof = half * 1024
            for e in range(9):
                if e < 8:
                    cur_eslot = nrw % 2
                    nrw += 1
                for mh in range(HCH):
                    slot = len(a_marks) % 2
                    s1 = s["sld"] if slot == 0 else s["sldB"]
                    s3 = s["sld2"] if slot == 0 else s["sld2B"]
                    pace = a_marks[-2] if len(a_marks) >= 2 else mm0
                    p.wait("sync", mmA, pace)
                    lv1 = p.ei("sync", s1, lambda q, e=e, mh=mh, slot=slot: q.dma_start(
                        out=WAB[:, slot * 1024: slot * 1024 + 1024],
                        in_=w1_slice(e, mh)), 16)
                    lv3 = p.ei("sync", s3, lambda q, e=e, mh=mh, slot=slot: q.dma_start(
                        out=W3B[:, slot * 1024: slot * 1024 + 1024],
                        in_=w3_slice(e, mh)), 16)
                    p.wait("tensor", s1, lv1)
                    p.wait("tensor", s3, lv3)
                    bank_guard = max((ep_v[-2] if len(ep_v) >= 2 else 0),
                                     (bep_v[-1] if bep_v else 0))
                    p.wait("tensor", s["vA"], bank_guard)
                    if mh == 0 and rw_s:
                        p.wait("tensor", s["scA"], rw_s[-1])
                    pa0, pa1 = (0, 1) if slot == 0 else (2, 3)
                    pb0, pb1 = (4, 5) if slot == 0 else (6, 7)
                    def mka(q, slot=slot, tof=tof, pa0=pa0, pa1=pa1, pb0=pb0, pb1=pb1):
                        for k in range(NCH):
                            st, sp = (k == 0), (k == NCH - 1)
                            w1g = WAB[:, slot * 1024 + k * 128: slot * 1024 + (k + 1) * 128]
                            w3g = W3B[:, slot * 1024 + k * 128: slot * 1024 + (k + 1) * 128]
                            xs0 = self.X[:, k * T + tof: k * T + tof + 512]
                            xs1 = self.X[:, k * T + tof + 512: k * T + tof + 1024]
                            q.matmul(self.PS[pa0][:, :], w1g, xs0, start=st, stop=sp)
                            q.matmul(self.PS[pa1][:, :], w1g, xs1, start=st, stop=sp)
                            q.matmul(self.PS[pb0][:, :], w3g, xs0, start=st, stop=sp)
                            ins = q.matmul(self.PS[pb1][:, :], w3g, xs1, start=st, stop=sp)
                        return ins
                    amv = p.ei("tensor", mmA, mka)
                    a_marks.append(amv)
                    b1k, b3k = ("sb1", "sb3") if e == 8 else ("b1", "b3")
                    b1j = mh if e == 8 else e * HCH + mh
                    p.wait("vector", mmA, amv)
                    if mh == 0 and b_marks:
                        p.wait("scalar", mmA, b_marks[-1])
                    for sub, pa, pb in ((0, pa0, pb0), (1, pa1, pb1)):
                        p.drain("vector")
                        p.ei("vector", s["vA"], lambda q, pb=pb, b3k=b3k, b1j=b1j: q.tensor_scalar(
                            VS1[:, :], self.PS[pb][:, :], self.cpc(b3k, b1j), None, Al.add))
                        p.drain("vector")
                        p.wait("vector", s["scA"])
                        vv = p.ei("vector", s["vA"], lambda q, pa=pa, b1k=b1k, b1j=b1j: q.scalar_tensor_tensor(
                            VS2[:, :], self.PS[pa][:, :], self.cpc(b1k, b1j), VS1[:, :],
                            Al.add, Al.mult))
                        p.wait("scalar", s["vA"], vv)
                        p.drain("scalar")
                        scv = p.ei("scalar", s["scA"], lambda q, mh=mh, sub=sub: q.activation(
                            HB[:, mh * 1024 + sub * 512: mh * 1024 + sub * 512 + 512],
                            VS2[:, :], AF.Silu))
                    ep_v.append(vv)
                    ep_s.append(scv)
                for md in range(NCH):
                    mslot = len(b_marks) % 2
                    sg = s["gld"] if mslot == 0 else s["gldB"]
                    p.wait("gpsimd", mmA, b_marks[-2] if len(b_marks) >= 2 else mm0)
                    lv2 = p.ei("gpsimd", sg, lambda q, e=e, md=md, mslot=mslot: q.dma_start(
                        out=W2B[:, mslot * 1408: mslot * 1408 + 1408],
                        in_=w2_slice(e, md)), 16)
                    p.wait("tensor", sg, lv2)
                    if md == 0:
                        p.wait("tensor", s["scA"], ep_s[-1])
                    p.wait("tensor", s["vA"], max(
                        ep_v[-1], (bep_v[-2] if len(bep_v) >= 2 else 0)))
                    pc0, pc1 = (0, 1) if mslot == 0 else (2, 3)
                    def mkb2(q, pc0=pc0, pc1=pc1, mslot=mslot):
                        for kh in range(HCH):
                            st, sp = (kh == 0), (kh == HCH - 1)
                            w2g = W2B[:, mslot * 1408 + kh * 128: mslot * 1408 + (kh + 1) * 128]
                            q.matmul(self.PS[pc0][:, :], w2g,
                                     HB[:, kh * 1024: kh * 1024 + 512], start=st, stop=sp)
                            ins = q.matmul(self.PS[pc1][:, :], w2g,
                                           HB[:, kh * 1024 + 512: kh * 1024 + 1024], start=st, stop=sp)
                        return ins
                    bmv = p.ei("tensor", mmA, mkb2)
                    b_marks.append(bmv)
                    if md == 3 and e < 8:
                        emit_rw(p)
                    b2k = "sb2" if e == 8 else "b2"
                    b2j = md if e == 8 else e * NCH + md
                    p.wait("vector", mmA, bmv)
                    for sub, pc in ((0, pc0), (1, pc1)):
                        p.drain("vector")
                        if e < 8:
                            p.ei("vector", s["vA"], lambda q, pc=pc, b2k=b2k, b2j=b2j, sub=sub, cur_eslot=cur_eslot: q.scalar_tensor_tensor(
                                VS1[:, :], self.PS[pc][:, :], self.cpc(b2k, b2j),
                                RWSB[:, cur_eslot * 1024 + sub * 512:
                                     cur_eslot * 1024 + sub * 512 + 512],
                                Al.add, Al.mult))
                        else:
                            p.ei("vector", s["vA"], lambda q, pc=pc, b2k=b2k, b2j=b2j: q.tensor_scalar(
                                VS1[:, :], self.PS[pc][:, :], self.cpc(b2k, b2j), None, Al.add))
                        p.drain("vector")
                        vv = p.ei("vector", s["vA"], lambda q, md=md, tof=tof, sub=sub: q.tensor_tensor(
                            self.ACC[:, md * T + tof + sub * 512: md * T + tof + sub * 512 + 512],
                            self.ACC[:, md * T + tof + sub * 512: md * T + tof + sub * 512 + 512],
                            VS1[:, :], Al.add))
                    bep_v.append(vv)
        p.run()

        p = Prog(nc)
        self.ln_stats2(p, self.ACC, self.ROWA, self.ROWB, sq=self.X)
        self.ln_rows_finalize(p, self.ROWA, self.ROWB)
        self.ln_norm(p, self.ACC, self.X, self.ROWB, self.ROWA)
        p.run()

    def phase_gla_d0(self):
        nc, s = self.nc, self.sem
        gctx = ExitStack()
        self.gla_ctx = gctx
        self.HBF = gctx.enter_context(nc.sbuf_tensor("HBF", [128, NCH * T], BF16))
        self.RSTg = gctx.enter_context(nc.sbuf_tensor("RSTg", [128, NCK * 4], F32))
        self.ACSTg = gctx.enter_context(nc.sbuf_tensor("ACSTg", [128, NCK * 4], F32))
        self.FL2 = gctx.enter_context(nc.sbuf_tensor("FL2", [128, 1], F32))
        self.SCUR = gctx.enter_context(nc.sbuf_tensor("SCUR", [128, 1024], F32))

        p = Prog(nc)
        ROWA = self.sbuf_local("ROWg1", [1, T], F32R)
        ROWB = self.sbuf_local("ROWg2", [1, T], F32R)
        self.ln_stats2(p, self.X, ROWA, ROWB)
        self.ln_rows_finalize(p, ROWA, ROWB)
        self.ln_norm(p, self.X, self.HBF, ROWB, ROWA, affine=("lnw", "lnb"))
        p.drain("vector")
        p.ei("vector", s["vA"], lambda q: q.tensor_scalar(
            self.FL2[:, :], self.cpc("flag"), -1.0, 1.0, Al.mult, Al.add))
        p.drain("vector")
        p.ei("vector", s["vA"], lambda q: q.memset(self.SCUR[:, :], 0.0))
        for i in range(4):
            p.ei("scalar", s["sst"], lambda q, i=i: q.dma_start(
                out=self.x2_spill[:, i * 4096:(i + 1) * 4096],
                in_=self.X[:, i * 4096:(i + 1) * 4096]), 16)
        p.run()

    def phase_gla_main(self):
        nc, s = self.nc, self.sem
        HBF, SCUR = self.HBF, self.SCUR
        p = Prog(nc)
        UC = self.sbuf_local("UC", [17, 128], BF16)
        GKW1B = self.sbuf_local("GKW1B", [128, 8 * 16], BF16)
        GKW2B = self.sbuf_local("GKW2B", [17, 512], BF16)
        ACCB = self.ACCB
        WQ0, WK0, WV0 = 16384, 20480, 24576
        GTMc = self.sbuf_local("GTMc", [128, 512], BF16)
        FAC = self.sbuf_local("FACq", [128, 512], BF16)
        FACK = self.sbuf_local("FACk", [128, 512], BF16)
        KF = self.sbuf_local("KF", [128, 512], BF16)
        KT = self.sbuf_local("KT", [128, 512], BF16)
        VT = self.sbuf_local("VT", [128, 1024], BF16)
        PSB = self.sbuf_local("PSB", [128, 512], BF16)
        SOP = self.sbuf_local("SOPt", [128, 256], BF16)
        QSTc = self.sbuf_local("QSTc", [128, 512], BF16)
        AC = self.sbuf_local("ACc", [128, 20], F32)
        RACC = self.sbuf_local("RACC", [128, 4], F32)

        mm0 = s["mmA"].v
        p.wait("gpsimd", s["mmA"], mm0)
        p.ei("gpsimd", s["gq"], lambda q: q.dma_start(
            out=GKW1B[:, :], in_=self.dram["gkw1"]), 16)
        gk_ready = p.ei("gpsimd", s["gq"], lambda q: q.dma_start(
            out=GKW2B[:, :], in_=self.dram["gkw2e"]), 16)
        p.drain("vector")
        p.ei("vector", s["vA"], lambda q: q.memset(RACC[:, :], 0.0))
        p.wait("gpsimd", s["mmA"], mm0)
        uc1 = p.ei("gpsimd", s["guc"], lambda q: q.dma_start(
            out=UC[16:17, :], in_=self.ONEROW[0:1, :]), 16)
        for g in range(4):
            wq_rdy = p.ei("gpsimd", s["gqB"], lambda q, g=g: q.dma_start(
                out=ACCB[:, WQ0 + g * 1024: WQ0 + (g + 1) * 1024],
                in_=self.dram["Wq"][g]), 16)
            wk_rdy = p.ei("gpsimd", s["gk2"], lambda q, g=g: q.dma_start(
                out=ACCB[:, WK0 + g * 1024: WK0 + (g + 1) * 1024],
                in_=self.dram["Wk"][g]), 16)
        for k in range(NCH):
            for n in range(2):
                wv_rdy = p.ei("gpsimd", s["gld"], lambda q, k=k, n=n: q.dma_start(
                    out=ACCB[:, WV0 + (k * 2 + n) * 512: WV0 + (k * 2 + n + 1) * 512],
                    in_=self.dram["Wv"][k, n]), 16)

        wq_marks = []
        v_kmarks = []
        for c in range(NCK):
            p.wait("tensor", s["gq"], gk_ready)
            p.wait("tensor", s["vA"])
            p.wait("tensor", s["vB"])
            def mku(q, c=c):
                for k in range(NCH):
                    ins = q.matmul(self.PS[0][0:16, 0:128], GKW1B[:, k * 16:(k + 1) * 16],
                                   HBF[:, k * T + c * 128: k * T + (c + 1) * 128],
                                   start=(k == 0), stop=(k == NCH - 1))
                return ins
            mv = p.ei("tensor", s["mmA"], mku)
            p.wait("scalar", s["mmA"], mv)
            p.drain("scalar")
            scv = p.ei("scalar", s["scA"], lambda q: q.copy(UC[0:16, :], self.PS[0][0:16, 0:128]))
            p.wait("tensor", s["scA"], scv)
            p.wait("tensor", s["guc"], uc1)
            mv = p.ei("tensor", s["mmA"], lambda q: q.matmul(
                self.PS[1][:, :], UC[:, :], GKW2B[:, :], start=True, stop=True))
            p.wait("scalar", s["mmA"], mv)
            p.wait("scalar", s["vA"])
            p.drain("scalar")
            p.ei("scalar", s["scA"], lambda q: q.activation(
                self.XSQ[:, :], self.PS[1][:, :], AF.Abs))
            p.drain("scalar")
            p.ei("scalar", s["scA"], lambda q: q.activation(
                self.NTMP[:, :], self.XSQ[:, :], AF.Exp, scale=-1.0))
            p.drain("scalar")
            p.ei("scalar", s["scA"], lambda q: q.activation(
                self.NTMP[:, :], self.NTMP[:, :], AF.Ln, bias=1.0))
            p.drain("scalar")
            scv = p.ei("scalar", s["scA"], lambda q: q.activation(
                self.XSQ[:, :], self.PS[1][:, :], AF.Relu, scale=-1.0))
            p.wait("vector", s["scA"], scv)
            p.drain("vector")
            vv = p.ei("vector", s["vA"], lambda q: q.tensor_tensor(
                GTMc[:, :], self.XSQ[:, :], self.NTMP[:, :], Al.add))
            p.wait("tensor", s["vA"], vv)
            def mkcs(q):
                for g in range(4):
                    ins = q.matmul(self.PS[2][:, g * 128:(g + 1) * 128],
                                   GTMc[:, g * 128:(g + 1) * 128], self.MASKB[:, :],
                                   start=True, stop=True)
                return ins
            mv_cs = p.ei("tensor", s["mmA"], mkcs)
            p.wait("vector", s["mmA"], mv_cs)
            p.drain("vector")
            p.ei("vector", s["vA"], lambda q: q.tensor_copy(
                AC[:, 0:4], self.PS[2][:, 127::128]))
            p.drain("vector")
            p.ei("vector", s["vA"], lambda q, c=c: q.tensor_copy(
                self.ACSTg[:, c * 4:(c + 1) * 4], AC[:, 0:4]))
            p.drain("vector")
            p.ei("vector", s["vA"], lambda q, c=c: q.tensor_copy(
                self.RSTg[:, c * 4:(c + 1) * 4], RACC[:, :]))
            p.drain("vector")
            p.ei("vector", s["vA"], lambda q: q.tensor_tensor(
                RACC[:, :], RACC[:, :], AC[:, 0:4], Al.add))
            p.drain("vector")
            p.ei("vector", s["vA"], lambda q: q.tensor_scalar(
                AC[:, 12:16], AC[:, 0:4], 1.0 / 32.0, None, Al.mult))
            p.drain("vector")
            vv = p.ei("vector", s["vA"], lambda q: q.tensor_scalar(
                AC[:, 16:20], AC[:, 0:4], -1.0 / 32.0, LN_DKS, Al.mult, Al.add))
            p.wait("scalar", s["vA"], vv)
            p.drain("scalar")
            p.ei("scalar", s["scA"], lambda q: q.activation(
                AC[:, 4:8], AC[:, 0:4], AF.Exp, scale=-1.0 / 32.0))
            p.drain("scalar")
            p.ei("scalar", s["scA"], lambda q: q.activation(
                AC[:, 8:12], AC[:, 0:4], AF.Exp, scale=-1.0 / 16.0))
            for g in range(4):
                p.drain("scalar")
                p.ei("scalar", s["scA"], lambda q, g=g: q.activation(
                    FAC[:, g * 128:(g + 1) * 128], self.PS[2][:, g * 128:(g + 1) * 128],
                    AF.Exp, scale=-1.0 / 16.0, bias=AC[:, 12 + g:13 + g]))
                p.drain("scalar")
                scv = p.ei("scalar", s["scA"], lambda q, g=g: q.activation(
                    FACK[:, g * 128:(g + 1) * 128], self.PS[2][:, g * 128:(g + 1) * 128],
                    AF.Exp, scale=1.0 / 16.0, bias=AC[:, 16 + g:17 + g]))
            fac_done = scv
            p.wait("vector", s["scA"], fac_done)
            p.wait("vector", s["srw"])
            if c == 0:
                p.wait("tensor", s["gqB"], wq_rdy)
                p.wait("tensor", s["gk2"], wk_rdy)
            qk_marks = []
            for g in range(4):
                qb = 3 if g % 2 == 0 else 4
                kb = 0 if g % 2 == 0 else 6
                if g == 0:
                    p.wait("tensor", s["vA"])
                elif g >= 2:
                    p.wait("tensor", s["vA"], qk_marks[g - 2])
                def mkq(q, c=c, g=g, qb=qb):
                    for k in range(NCH):
                        ins = q.matmul(self.PS[qb][:, 0:128],
                                       ACCB[:, WQ0 + g * 1024 + k * 128: WQ0 + g * 1024 + (k + 1) * 128],
                                       HBF[:, k * T + c * 128: k * T + (c + 1) * 128],
                                       start=(k == 0), stop=(k == NCH - 1))
                    return ins
                mvq = p.ei("tensor", s["mmA"], mkq)
                p.wait("vector", s["mmA"], mvq)
                p.drain("vector")
                p.ei("vector", s["vA"], lambda q, g=g, qb=qb: q.tensor_tensor(
                    QSTc[:, g * 128:(g + 1) * 128], self.PS[qb][:, 0:128],
                    FAC[:, g * 128:(g + 1) * 128], Al.mult))
                def mkk(q, c=c, g=g, kb=kb):
                    for k in range(NCH):
                        ins = q.matmul(self.PS[kb][:, 0:128],
                                       ACCB[:, WK0 + g * 1024 + k * 128: WK0 + g * 1024 + (k + 1) * 128],
                                       HBF[:, k * T + c * 128: k * T + (c + 1) * 128],
                                       start=(k == 0), stop=(k == NCH - 1))
                    return ins
                mvk = p.ei("tensor", s["mmA"], mkk)
                wq_marks.append(mvk)
                p.wait("vector", s["mmA"], mvk)
                p.drain("vector")
                qk_marks.append(p.ei("vector", s["vA"], lambda q, g=g, kb=kb: q.tensor_tensor(
                    KF[:, g * 128:(g + 1) * 128], self.PS[kb][:, 0:128],
                    FACK[:, g * 128:(g + 1) * 128], Al.mult)))
            vq = s["vA"].v
            p.wait("sync", s["vA"], vq)
            p.ei("sync", s["srw"], lambda q, c=c: q.dma_start(
                out=self.qst_d[:, c * 512:(c + 1) * 512], in_=QSTc[:, :]), 16)
            if c == 0:
                p.wait("tensor", s["gld"], wv_rdy)
            for n in range(2):
                p.wait("tensor", s["scA"])
                def mkv(q, c=c, n=n):
                    for k in range(NCH):
                        ins = q.matmul(self.PS[5][:, :],
                                       HBF[:, k * T + c * 128: k * T + (c + 1) * 128],
                                       ACCB[:, WV0 + (k * 2 + n) * 512: WV0 + (k * 2 + n + 1) * 512],
                                       start=(k == 0), stop=(k == NCH - 1))
                    return ins
                mvv = p.ei("tensor", s["mmA"], mkv)
                v_kmarks.append(mvv)
                p.wait("scalar", s["mmA"], v_kmarks[-1])
                p.wait("scalar", s["mmB"])
                p.drain("scalar")
                p.ei("scalar", s["scA"], lambda q, n=n: q.copy(
                    VT[:, n * 512:(n + 1) * 512], self.PS[5][:, :]))
            vk = s["vA"].v
            p.wait("tensor", s["vA"], vk)
            for g in range(4):
                bk2 = 3 if g % 2 == 0 else 0
                p.wait("tensor", s["vB"], s["vB"].v - 1)
                mvt = p.ei("tensor", s["mmB"], lambda q, g=g, bk2=bk2: q.transpose(
                    self.PS[bk2][:, :].bitcast(BF16)[:, 0:128], KF[:, g * 128:(g + 1) * 128],
                    self.IDB[:, :]))
                p.wait("vector", s["mmB"], mvt)
                p.drain("vector")
                p.ei("vector", s["vB"], lambda q, g=g, bk2=bk2: q.tensor_copy(
                    KT[:, g * 128:(g + 1) * 128], self.PS[bk2][:, :].bitcast(BF16)[:, 0:128]))
            scv_v = s["scA"].v
            p.wait("tensor", s["scA"], scv_v)
            p.wait("tensor", s["vB"])
            def mkU(q):
                for g in range(4):
                    bank, half = (1, 0) if g == 0 else (1, 1) if g == 1 else (2, 0) if g == 2 else (2, 1)
                    ins = q.matmul(self.PS[bank][:, half * 256: half * 256 + 256],
                                   KT[:, g * 128:(g + 1) * 128],
                                   VT[:, g * 256:(g + 1) * 256], start=True, stop=True)
                return ins
            mvU = p.ei("tensor", s["mmA"], mkU)
            p.wait("vector", s["mmA"], mvU)
            for g in range(4):
                bank, half = (1, 0) if g == 0 else (1, 1) if g == 1 else (2, 0) if g == 2 else (2, 1)
                p.drain("vector")
                p.ei("vector", s["vA"], lambda q, g=g: q.tensor_scalar(
                    SOP[:, :], SCUR[:, g * 256:(g + 1) * 256], AC[:, 4 + g:5 + g],
                    None, Al.mult))
                p.drain("vector")
                p.ei("vector", s["vA"], lambda q, g=g, bank=bank, half=half: q.tensor_scalar(
                    self.NTMP[:, 0:256], self.PS[bank][:, half * 256: half * 256 + 256],
                    AC[:, 4 + g:5 + g], None, Al.mult))
                p.drain("vector")
                p.ei("vector", s["vA"], lambda q, g=g: q.scalar_tensor_tensor(
                    SCUR[:, g * 256:(g + 1) * 256], SCUR[:, g * 256:(g + 1) * 256],
                    AC[:, 8 + g:9 + g], self.NTMP[:, 0:256], Al.mult, Al.add))
                vv = s["vA"].v
                p.wait("tensor", s["vA"], vv)
                def mkP(q, g=g):
                    for hh in range(4):
                        tp = (96, 0) if hh == 3 else None
                        ins = q.matmul(self.PS[4 + hh][:, 0:128],
                                       KF[32 * hh:32 * hh + 32, g * 128:(g + 1) * 128],
                                       QSTc[32 * hh:32 * hh + 32, g * 128:(g + 1) * 128],
                                       start=True, stop=True, tile_position=tp)
                    return ins
                mvP = p.ei("tensor", s["mmB"], mkP)
                p.wait("vector", s["mmB"], mvP)
                for hh in range(4):
                    p.drain("vector")
                    p.ei("vector", s["vA"], lambda q, hh=hh: q.tensor_tensor(
                        PSB[:, hh * 128:(hh + 1) * 128], self.PS[4 + hh][:, 0:128],
                        self.MASKB[:, :], Al.mult))
                pv = s["vA"].v
                p.wait("tensor", s["vA"], pv)
                def mkO(q, g=g):
                    for hh in range(4):
                        tp = (96, 0) if hh == 3 else None
                        q.matmul(self.PS[4 + hh][:, 128:192], PSB[:, hh * 128:(hh + 1) * 128],
                                 VT[:, g * 256 + hh * 64: g * 256 + (hh + 1) * 64],
                                 start=True, stop=False)
                        ins = q.matmul(self.PS[4 + hh][:, 128:192],
                                       QSTc[32 * hh:32 * hh + 32, g * 128:(g + 1) * 128],
                                       SOP[32 * hh:32 * hh + 32, hh * 64:(hh + 1) * 64],
                                       start=False, stop=True, tile_position=tp)
                    return ins
                mvO = p.ei("tensor", s["mmB"], mkO)
                p.wait("vector", s["mmB"], mvO)
                for hh in range(4):
                    p.drain("vector")
                    p.ei("vector", s["vA"], lambda q, g=g, hh=hh, c=c: q.tensor_copy(
                        ACCB[:, c * 1024 + (g * 4 + hh) * 64: c * 1024 + (g * 4 + hh) * 64 + 64],
                        self.PS[4 + hh][:, 128:192]))
        p.drain("vector")
        p.ei("vector", s["vA"], lambda q: q.tensor_scalar(
            self.XSQ[:, :], SCUR[:, 0:512], self.FL2[:, 0:1], None, Al.mult))
        p.drain("vector")
        vv = p.ei("vector", s["vA"], lambda q: q.tensor_scalar(
            self.NTMP[:, :], SCUR[:, 512:1024], self.FL2[:, 0:1], None, Al.mult))
        p.wait("gpsimd", s["vA"], vv)
        p.ei("gpsimd", s["gld"], lambda q: q.dma_start(
            out=self.cc_in[:, 0:512], in_=self.XSQ[:, :]), 16)
        p.wait("gpsimd", s["gld"])
        p.ei("gpsimd", s["gld"], lambda q: q.dma_start(
            out=self.cc_in[:, 512:1024], in_=self.NTMP[:, :]), 16)
        p.wait("gpsimd", s["gld"])
        p.ei("gpsimd", s["cc"], lambda q: q.collective_compute(
            "AllReduce", Al.add,
            replica_groups=[[0, 1], [2, 3], [4, 5], [6, 7]],
            ins=[self.cc_in], outs=[self.cc_out]))
        p.run()

    def phase_gla_post(self):
        nc, s = self.nc, self.sem
        HBF, SCUR = self.HBF, self.SCUR
        p = Prog(nc)
        SRECV = self.sbuf_local("SRECV", [128, 1024], BF16)
        SCO = self.sbuf_local("SCO", [128, 1024], BF16)
        RMSWB = self.sbuf_local("RMSWBl", [128, DV], BF16)
        GSW = self.sbuf_local("GSW", [128, 1024], BF16)
        OF = self.sbuf_local("OF", [128, 1024], F32R)
        OGT = self.sbuf_local("OGT", [128, 1024], BF16)
        QSTc = self.sbuf_local("QSTc2", [128, 512], BF16)
        RSTD = self.sbuf_local("RSTD", [128, 16], F32)
        ECC = self.sbuf_local("ECC", [128, 12], F32)
        ACCB = self.ACCB
        WG0, WO0 = 16384, 24576

        mm0 = s["mmA"].v
        p.wait("gpsimd", s["mmA"], mm0)
        rmw = p.ei("gpsimd", s["guc"], lambda q: q.dma_start(
            out=RMSWB[:, :], in_=self.dram["rms_wb"]), 16)
        p.wait("vector", s["guc"], rmw)
        for k in range(NCH):
            for n in range(2):
                wg_rdy = p.ei("gpsimd", s["gld"], lambda q, k=k, n=n: q.dma_start(
                    out=ACCB[:, WG0 + (k * 2 + n) * 512: WG0 + (k * 2 + n + 1) * 512],
                    in_=self.dram["Wg"][k, n]), 16)
        for md in range(NCH):
            wo_rdy = p.ei("gpsimd", s["gldB"], lambda q, md=md: q.dma_start(
                out=ACCB[:, WO0 + md * 1024: WO0 + (md + 1) * 1024],
                in_=self.dram["Wo"][md]), 16)
        p.wait("gpsimd", s["cc"], s["cc"].v)
        rv = p.ei("gpsimd", s["gq"], lambda q: q.dma_start(
            out=SRECV[:, :], in_=self.cc_out), 16)
        p.wait("vector", s["gq"], rv)
        p.drain("vector")
        srecv_mark = p.ei("vector", s["vA"], lambda q: q.tensor_scalar(
            SRECV[:, :], SRECV[:, :], self.cpc("flag"), None, Al.mult))
        p.wait("gpsimd", s["vA"], srecv_mark)
        p.wait("vector", s["sst"])

        wg_marks = []
        wo_marks = []
        wo_eps = []
        corr_marks = []
        for c in range(NCK):
            p.wait("sync", s["mmA"], corr_marks[-1] if corr_marks else mm0)
            if not corr_marks:
                p.wait("sync", s["srw"])
            lq = p.ei("sync", s["srw"], lambda q, c=c: q.dma_start(
                out=QSTc[:, :], in_=self.qst_d[:, c * 512:(c + 1) * 512]), 16)
            p.drain("scalar")
            p.wait("scalar", s["vA"])
            p.ei("scalar", s["scA"], lambda q, c=c: q.activation(
                ECC[:, 0:4], self.ACSTg[:, c * 4:(c + 1) * 4], AF.Exp, scale=-1.0 / 32.0))
            p.drain("scalar")
            scv = p.ei("scalar", s["scA"], lambda q, c=c: q.activation(
                ECC[:, 4:8], self.RSTg[:, c * 4:(c + 1) * 4], AF.Exp, scale=-1.0 / 16.0))
            p.wait("vector", s["scA"], scv)
            p.wait("vector", s["mmA"], corr_marks[-1] if corr_marks else mm0)
            p.drain("vector")
            p.ei("vector", s["vA"], lambda q: q.tensor_tensor(
                ECC[:, 8:12], ECC[:, 0:4], ECC[:, 4:8], Al.mult))
            for g in range(4):
                p.drain("vector")
                p.ei("vector", s["vA"], lambda q, g=g: q.tensor_scalar(
                    SCO[:, g * 256:(g + 1) * 256], SRECV[:, g * 256:(g + 1) * 256],
                    ECC[:, 8 + g:9 + g], None, Al.mult))
            vv = s["vA"].v
            p.wait("tensor", s["vA"], vv)
            p.wait("tensor", s["srw"], lq)
            def mkcorr(q):
                for g in range(4):
                    for hh in range(4):
                        tp = (96, 0) if hh == 3 else None
                        ins = q.matmul(self.PS[4 + hh][:, g * 64:(g + 1) * 64],
                                       QSTc[32 * hh:32 * hh + 32, g * 128:(g + 1) * 128],
                                       SCO[32 * hh:32 * hh + 32, g * 256 + hh * 64: g * 256 + (hh + 1) * 64],
                                       start=True, stop=True, tile_position=tp)
                return ins
            mvc = p.ei("tensor", s["mmA"], mkcorr)
            corr_marks.append(mvc)
            p.wait("vector", s["mmA"], mvc)
            p.wait("vector", s["mmB"])
            for g in range(4):
                for hh in range(4):
                    p.drain("vector")
                    p.ei("vector", s["vA"], lambda q, g=g, hh=hh, c=c: q.tensor_tensor(
                        OF[:, (g * 4 + hh) * 64:(g * 4 + hh) * 64 + 64],
                        ACCB[:, c * 1024 + (g * 4 + hh) * 64: c * 1024 + (g * 4 + hh) * 64 + 64],
                        self.PS[4 + hh][:, g * 64:(g + 1) * 64], Al.add))
            vv = s["vA"].v
            p.wait("scalar", s["vA"], vv)
            p.wait("scalar", s["vA"])
            p.drain("scalar")
            p.ei("scalar", s["scA"], lambda q: q.activation(
                self.XSQ[:, :], OF[:, 0:512], AF.Square))
            p.drain("scalar")
            scv = p.ei("scalar", s["scA"], lambda q: q.activation(
                self.NTMP[:, :], OF[:, 512:1024], AF.Square))
            p.wait("vector", s["scA"], scv)
            p.drain("vector")
            p.ei("vector", s["vA"], lambda q: q.tensor_reduce(
                RSTD[:, 0:8], self.XSQ[:, :].rearrange(
                    "p (h v) -> p h v", h=8), AX, Al.add))
            p.drain("vector")
            p.ei("vector", s["vA"], lambda q: q.tensor_reduce(
                RSTD[:, 8:16], self.NTMP[:, :].rearrange(
                    "p (h v) -> p h v", h=8), AX, Al.add))
            p.drain("vector")
            p.ei("vector", s["vA"], lambda q: q.tensor_scalar(
                RSTD[:, :], RSTD[:, :], 1.0 / DV, EPS, Al.mult, Al.add))
            p.drain("vector")
            vv = p.ei("vector", s["vA"], lambda q: q.reciprocal(RSTD[:, :], RSTD[:, :]))
            p.wait("scalar", s["vA"], vv)
            p.drain("scalar")
            sc_rstd = p.ei("scalar", s["scA"], lambda q: q.activation(
                RSTD[:, :], RSTD[:, :], AF.Sqrt))
            if c == 0:
                p.wait("tensor", s["gld"], wg_rdy)
            for n in range(2):
                p.wait("tensor", s["scA"])
                p.wait("tensor", s["vA"])
                def mkg2(q, n=n, c=c):
                    for k in range(NCH):
                        ins = q.matmul(self.PS[n][:, :],
                                       HBF[:, k * T + c * 128: k * T + (c + 1) * 128],
                                       ACCB[:, WG0 + (k * 2 + n) * 512: WG0 + (k * 2 + n + 1) * 512],
                                       start=(k == 0), stop=(k == NCH - 1))
                    return ins
                mvg = p.ei("tensor", s["mmA"], mkg2)
                wg_marks.append(mvg)
                p.wait("scalar", s["mmA"], wg_marks[-1])
                p.drain("scalar")
                p.wait("scalar", s["vA"])
                scv = p.ei("scalar", s["scA"], lambda q, n=n: q.activation(
                    self.XSQ[:, :], self.PS[n][:, :], AF.Silu))
                p.wait("vector", s["scA"], scv)
                p.drain("vector")
                p.ei("vector", s["vA"], lambda q, n=n: q.tensor_tensor(
                    GSW[:, n * 512:(n + 1) * 512].rearrange("p (h v) -> p h v", h=8),
                    self.XSQ[:, :].rearrange("p (h v) -> p h v", h=8),
                    RMSWB[:, :].rearrange("p (o v) -> p o v", o=1).to_broadcast(
                        (128, 8, 64)), Al.mult))
            p.wait("vector", s["scA"], sc_rstd)
            p.drain("vector")
            p.ei("vector", s["vA"], lambda q: q.tensor_tensor(
                OF[:, :].rearrange("p (h v) -> p h v", h=16),
                OF[:, :].rearrange("p (h v) -> p h v", h=16),
                RSTD[:, :].rearrange("p (h o) -> p h o", o=1).to_broadcast((128, 16, 64)),
                Al.mult))
            p.drain("vector")
            vv = p.ei("vector", s["vA"], lambda q: q.tensor_tensor(
                OF[:, :], OF[:, :], GSW[:, :], Al.mult))
            p.wait("tensor", s["vA"], vv)
            p.wait("vector", s["mmA"])
            for t8 in range(8):
                bk2 = 2 + (t8 % 2)
                p.wait("tensor", s["vB"], s["vB"].v - 1)
                mvt = p.ei("tensor", s["mmB"], lambda q, t8=t8, bk2=bk2: q.transpose(
                    self.PS[bk2][:, 0:128].bitcast(F32R), OF[:, t8 * 128:(t8 + 1) * 128],
                    self.IDR[:, :]))
                p.wait("vector", s["mmB"], mvt)
                p.drain("vector")
                p.ei("vector", s["vB"], lambda q, t8=t8, bk2=bk2: q.tensor_copy(
                    OGT[:, t8 * 128:(t8 + 1) * 128], self.PS[bk2][:, 0:128]))
            p.wait("tensor", s["vB"])
            p.wait("tensor", s["scA"])
            if not wo_marks:
                p.wait("tensor", s["gldB"], wo_rdy)
            for wave in range(2):
                for mdl in range(4):
                    md = wave * 4 + mdl
                    if len(wo_eps) >= 4:
                        p.wait("tensor", s["vA"], wo_eps[-4])
                    def mkwo(q, md=md, mdl=mdl):
                        for k in range(NCH):
                            ins = q.matmul(self.PS[mdl][:, 0:128],
                                           ACCB[:, WO0 + md * 1024 + k * 128: WO0 + md * 1024 + (k + 1) * 128],
                                           OGT[:, k * 128:(k + 1) * 128],
                                           start=(k == 0), stop=(k == NCH - 1))
                        return ins
                    mvw = p.ei("tensor", s["mmA"], mkwo)
                    wo_marks.append(mvw)
                    p.wait("vector", s["mmA"], mvw)
                    p.drain("vector")
                    vv = p.ei("vector", s["vA"], lambda q, md=md, c=c, mdl=mdl: q.tensor_tensor(
                        self.X[:, md * T + c * 128: md * T + (c + 1) * 128],
                        self.X[:, md * T + c * 128: md * T + (c + 1) * 128],
                        self.PS[mdl][:, 0:128], Al.add))
                    wo_eps.append(vv)
        p.run()

    def phase_mlp(self):
        nc, s = self.nc, self.sem
        p = Prog(nc)
        ROWA = self.sbuf_local("ROWm1", [1, T], F32R)
        ROWB = self.sbuf_local("ROWm2", [1, T], F32R)
        W1B = self.sbuf_local("W1B", [128, 2 * 1024], F32R)
        W2B2 = self.sbuf_local("W2B2", [128, 2 * 2048], F32R)
        X2R = self.sbuf_local("X2R", [128, 4096], F32R)
        GEL = self.ACC[:, 0:8192]
        H2 = self.ACC[:, 8192:12288]

        self.ln_stats2(p, self.X, ROWA, ROWB)
        self.ln_rows_finalize(p, ROWA, ROWB)
        mm0 = s["mmA"].v
        w1_marks = []
        w2_marks = []
        gel_marks = []
        fin_marks = []
        for tt in range(4):
            tof = tt * 512
            p.wait("tensor", s["vA"])
            def mkb(q, tof=tof, ROWA=ROWA, ROWB=ROWB):
                q.matmul(self.PS[4][:, :], self.ONEROW[0:1, :],
                         ROWB[0:1, tof: tof + 512], start=True, stop=True)
                return q.matmul(self.PS[5][:, :], self.ONEROW[0:1, :],
                                ROWA[0:1, tof: tof + 512], start=True, stop=True)
            mv = p.ei("tensor", s["mmA"], mkb)
            p.wait("vector", s["mmA"], mv)
            for cch in range(NCH):
                p.drain("vector")
                p.ei("vector", s["vA"], lambda q, cch=cch, tof=tof: q.tensor_tensor(
                    self.NTMP[:, :], self.X[:, cch * T + tof: cch * T + tof + 512],
                    self.PS[4][:, :], Al.mult))
                p.drain("vector")
                p.ei("vector", s["vA"], lambda q: q.tensor_tensor(
                    self.NTMP[:, :], self.NTMP[:, :], self.PS[5][:, :], Al.add))
                p.drain("vector")
                p.ei("vector", s["vA"], lambda q, cch=cch: q.tensor_scalar(
                    H2[:, cch * 512:(cch + 1) * 512], self.NTMP[:, :],
                    self.cpc("lnw", cch), self.cpc("lnb", cch), Al.mult, Al.add))
            hv = s["vA"].v
            p.wait("tensor", s["vA"], hv)
            p.wait("sync", s["srw"])
            if fin_marks:
                p.wait("sync", s["vA"], fin_marks[-1])
            x2v = p.ei("sync", s["srw"], lambda q, tt=tt: q.dma_start(
                out=X2R[:, :].rearrange("p (c t) -> p c t", c=8),
                in_=self.x2_spill[:, :].rearrange("p (c f) -> p c f", c=8)[
                    :, :, tt * 512:(tt + 1) * 512]), 16)
            for im in range(16):
                slot = len(w1_marks) % 2
                sw = s["sld"] if slot == 0 else s["sldB"]
                p.wait("sync", s["mmA"], w1_marks[-2] if len(w1_marks) >= 2 else mm0)
                lw = p.ei("sync", sw, lambda q, im=im, slot=slot: q.dma_start(
                    out=W1B[:, slot * 1024: slot * 1024 + 1024],
                    in_=self.dram["W1"][im]), 16)
                p.wait("tensor", sw, lw)
                if len(gel_marks) >= 2:
                    p.wait("tensor", s["scA"], gel_marks[-2])
                pg = im % 2
                def mkw1(q, slot=slot, pg=pg):
                    for k in range(NCH):
                        ins = q.matmul(self.PS[pg][:, :],
                                       W1B[:, slot * 1024 + k * 128: slot * 1024 + (k + 1) * 128],
                                       H2[:, k * 512:(k + 1) * 512],
                                       start=(k == 0), stop=(k == NCH - 1))
                    return ins
                mv = p.ei("tensor", s["mmA"], mkw1)
                w1_marks.append(mv)
                p.wait("scalar", s["mmA"], mv)
                p.drain("scalar")
                gv = p.ei("scalar", s["scA"], lambda q, im=im, pg=pg: q.activation(
                    GEL[:, im * 512:(im + 1) * 512], self.PS[pg][:, :], AF.Gelu,
                    bias=self.cpc("l1b1", im)))
                gel_marks.append(gv)
            p.wait("tensor", s["scA"], gel_marks[-1])
            p.wait("vector", s["srw"], x2v)
            for md in range(NCH):
                slot = len(w2_marks) % 2
                sw = s["sld2"] if slot == 0 else s["sld2B"]
                p.wait("sync", s["mmA"], w2_marks[-2] if len(w2_marks) >= 2 else mm0)
                lw = p.ei("sync", sw, lambda q, md=md, slot=slot: q.dma_start(
                    out=W2B2[:, slot * 2048: slot * 2048 + 2048],
                    in_=self.dram["W2"][md]), 16)
                p.wait("tensor", sw, lw)
                if len(fin_marks) >= 2:
                    p.wait("tensor", s["vA"], fin_marks[-2])
                pg = 2 + md % 2
                def mkw2(q, slot=slot, pg=pg):
                    for kh in range(16):
                        ins = q.matmul(self.PS[pg][:, :],
                                       W2B2[:, slot * 2048 + kh * 128: slot * 2048 + (kh + 1) * 128],
                                       GEL[:, kh * 512:(kh + 1) * 512],
                                       start=(kh == 0), stop=(kh == 15))
                    return ins
                mv = p.ei("tensor", s["mmA"], mkw2)
                w2_marks.append(mv)
                p.wait("vector", s["mmA"], mv)
                p.drain("vector")
                p.ei("vector", s["vA"], lambda q, md=md, pg=pg, tof=tof: q.scalar_tensor_tensor(
                    self.XSQ[:, :], self.PS[pg][:, :], self.cpc("l1b2", md),
                    self.X[:, md * T + tof: md * T + tof + 512], Al.add, Al.add))
                p.drain("vector")
                fv = p.ei("vector", s["vA"], lambda q, md=md, tof=tof: q.tensor_tensor(
                    self.X[:, md * T + tof: md * T + tof + 512], self.XSQ[:, :],
                    X2R[:, md * 512:(md + 1) * 512], Al.add))
                fin_marks.append(fv)
        p.run()
        p = Prog(nc)
        self.ln_stats2(p, self.X, ROWA, ROWB)
        self.ln_rows_finalize(p, ROWA, ROWB)
        self.ln_norm(p, self.X, self.X, ROWB, ROWA)
        p.run()

    def phase_bitffn_prep(self):
        nc, s = self.nc, self.sem
        p = Prog(nc)
        self.bit_ctx = ExitStack()
        SWC = self.bit_ctx.enter_context(nc.sbuf_tensor("SWC", [128, 16], F32))
        self.SWC = SWC
        WT = self.sbuf_local("WT", [128, 2 * 4096], F32R)
        TERNB = self.sbuf_local("TERNB", [128, 2 * 4096], BF16)
        TERN = self.sbuf_local("TERN", [128, 2 * 1024], F32R)
        MC = self.sbuf_local("MC", [128, 16], F32R)
        mm0 = s["mmA"].v
        p.drain("vector")
        p.ei("vector", s["vA"], lambda q: q.tensor_single_scalar(
            MC[:, 3:4], self.ONECOL[:, 0:1], 1e30, Al.is_ge))
        mats = [("wgateT", 8, FI, 0), ("wupT", 8, FI, 1), ("wdownT", 32, D, 2)]
        ab_marks = []
        idx = 0
        for name, ntile, width, j in mats:
            p.drain("vector")
            p.ei("vector", s["vA"], lambda q, j=j: q.tensor_single_scalar(
                MC[:, j:j + 1], self.ONECOL[:, 0:1], 1e30, Al.is_ge))
            for m in range(ntile):
                slot = idx % 2
                ring = "sync" if slot == 0 else "scalar"
                sl = s["sld"] if slot == 0 else s["sld2"]
                p.wait(ring, s["scA"], ab_marks[-2] if len(ab_marks) >= 2 else 0)
                lw = p.ei(ring, sl, lambda q, name=name, m=m, width=width, slot=slot:
                          q.dma_start(out=WT[:, slot * 4096: slot * 4096 + width],
                                      in_=self.dram[name][m * 128:(m + 1) * 128, :]), 16)
                p.wait("scalar", sl, lw)
                p.wait("scalar", s["vA"])
                p.drain("scalar")
                sv = p.ei("scalar", s["scA"], lambda q, width=width, slot=slot: q.activation(
                    WT[:, slot * 4096: slot * 4096 + width],
                    WT[:, slot * 4096: slot * 4096 + width], AF.Abs,
                    accum_out=MC[:, 15:16]))
                ab_marks.append(sv)
                idx += 1
                p.wait("vector", s["scA"], sv)
                p.drain("vector")
                p.ei("vector", s["vA"], lambda q, j=j: q.tensor_tensor(
                    MC[:, j:j + 1], MC[:, j:j + 1], MC[:, 15:16], Al.add))
        vv = s["vA"].v
        p.wait("tensor", s["vA"], vv)
        mv = p.ei("tensor", s["mmA"], lambda q: q.matmul(
            self.PS[6][0:1, 0:4], self.ONECOL[:, :], MC[:, 0:4], start=True, stop=True))
        p.wait("vector", s["mmA"], mv)
        p.drain("vector")
        vv = p.ei("vector", s["vA"], lambda q: q.tensor_copy(
            MC[0:1, 4:8], self.PS[6][0:1, 0:4]))
        p.wait("tensor", s["vA"], vv)
        mv = p.ei("tensor", s["mmA"], lambda q: q.matmul(
            self.PS[6][:, 0:4], self.ONEROW[0:1, :], MC[0:1, 4:8], start=True, stop=True))
        p.wait("vector", s["mmA"], mv)
        p.drain("vector")
        p.ei("vector", s["vA"], lambda q: q.tensor_scalar(
            SWC[:, 4:6], self.PS[6][:, 0:2], 1.0 / (FI * D), EPS, Al.mult, Al.max))
        p.drain("vector")
        p.ei("vector", s["vA"], lambda q: q.tensor_scalar(
            SWC[:, 6:7], self.PS[6][:, 2:3], 1.0 / (FI * D), EPS, Al.mult, Al.max))
        p.drain("vector")
        p.ei("vector", s["vA"], lambda q: q.reciprocal(SWC[:, 0:3], SWC[:, 4:7]))
        p.drain("vector")
        p.ei("vector", s["vA"], lambda q: q.tensor_tensor(
            SWC[:, 7:8], SWC[:, 0:1], SWC[:, 0:1], Al.mult))
        p.drain("vector")
        vv = p.ei("vector", s["vA"], lambda q: q.tensor_tensor(
            SWC[:, 7:8], SWC[:, 7:8], SWC[:, 1:2], Al.mult))
        st_marks = [0, 0]
        q_marks = [vv]
        ts1_marks = []
        ab_last = ab_marks[-1]
        qmats = [("wgateT", self.qwg_t, 8, FI, 0, True), ("wupT", self.qwu_t, 8, FI, 1, True),
                 ("wdownT", self.qwd_t, 32, D, 2, False)]
        tiles = []
        for name, dst, ntile, width, j, isb in qmats:
            for m in range(ntile):
                tiles.append((name, dst, m, width, j, isb))
        for i, (name, dst, m, width, j, isb) in enumerate(tiles):
            slot = i % 2
            ring = "sync" if slot == 0 else "scalar"
            sl = s["sld"] if slot == 0 else s["sld2"]
            if i >= 2:
                p.wait(ring, s["vA"], q_marks[-2])
            else:
                p.wait(ring, s["scA"], ab_last)
            lw = p.ei(ring, sl, lambda q, name=name, m=m, width=width, slot=slot:
                      q.dma_start(out=WT[:, slot * 4096: slot * 4096 + width],
                                  in_=self.dram[name][m * 128:(m + 1) * 128, :]), 16)
            p.wait("scalar", sl, lw)
            p.wait("scalar", s["vA"], q_marks[0])
            p.drain("scalar")
            sv = p.ei("scalar", s["scA"], lambda q, width=width, slot=slot, j=j: q.activation(
                WT[:, :].bitcast(F32)[:, slot * 4096: slot * 4096 + width],
                WT[:, slot * 4096: slot * 4096 + width], AF.Copy,
                scale=SWC[:, j:j + 1], bias=MAGIC))
            ts1_marks.append(sv)
            p.wait("vector", s["scA"], sv)
            sstm = s["sst"] if slot == 0 else s["srw"]
            p.wait("vector", sstm, st_marks[-2])
            p.drain("vector")
            p.ei("vector", s["vA"], lambda q, width=width, slot=slot: q.tensor_scalar(
                WT[:, slot * 4096: slot * 4096 + width],
                WT[:, :].bitcast(F32)[:, slot * 4096: slot * 4096 + width],
                -MAGIC, -1.0, Al.add, Al.max))
            p.drain("vector")
            if isb:
                qv = p.ei("vector", s["vA"], lambda q, width=width, slot=slot: q.tensor_scalar(
                    TERNB[:, slot * 4096: slot * 4096 + width],
                    WT[:, slot * 4096: slot * 4096 + width], 1.0, None, Al.min))
            else:
                qv = p.ei("vector", s["vA"], lambda q, width=width, slot=slot: q.tensor_scalar(
                    TERN[:, slot * 1024: slot * 1024 + width],
                    WT[:, slot * 4096: slot * 4096 + width], 1.0, None, Al.min))
            q_marks.append(qv)
            p.wait("scalar", s["vA"], qv)
            if isb:
                sv2 = p.ei("scalar", sstm, lambda q, dst=dst, m=m, slot=slot: q.dma_start(
                    out=dst[:, :, m * 128:(m + 1) * 128].rearrange("a p f -> p a f"),
                    in_=TERNB[:, slot * 4096: slot * 4096 + 4096].rearrange(
                        "p (a f) -> p a f", f=128)), 16)
            else:
                quar, kk = m // 8, m % 8
                sv2 = p.ei("scalar", sstm, lambda q, dst=dst, quar=quar, kk=kk, slot=slot:
                           q.dma_start(
                    out=dst[:, :, quar * 1024 + kk * 128: quar * 1024 + kk * 128 + 128].rearrange(
                        "a p f -> p a f"),
                    in_=TERN[:, slot * 1024: slot * 1024 + 1024].rearrange(
                        "p (a f) -> p a f", f=128)), 16)
            st_marks.append(sv2)
        p.wait("scalar", sstm, st_marks[-1])
        p.wait("scalar", s["sst"] if (len(tiles) - 1) % 2 == 1 else s["srw"], st_marks[-2])

        if DBGOUT:
            p.wait("scalar", s["vA"])
            p.ei("scalar", s["sst"], lambda q: q.dma_start(
                out=self.dbg_swc, in_=SWC[:, :]), 16)
            p.ei("scalar", s["sst"], lambda q: q.dma_start(
                out=self.dbg_mc, in_=MC[:, :].bitcast(F32)), 16)
            p.ei("scalar", s["sst"], lambda q: q.dma_start(
                out=self.dbg_tern, in_=TERN[:, :].bitcast(F32)), 16)
            p.ei("scalar", s["sst"], lambda q: q.dma_start(
                out=self.dbg_wt, in_=WT[:, :].bitcast(F32)), 16)
            p.wait("scalar", s["sst"], s["sst"].v)
        p.run()

    def phase_bitffn(self):
        nc, s = self.nc, self.sem
        SWC = self.SWC
        p = Prog(nc)
        QXB = self.sbuf_local("QXB", [128, NCH * T], BF16)
        QWGT = self.sbuf_local("QWGT", [128, 2 * 1024], BF16)
        QWUT = self.sbuf_local("QWUT", [128, 2 * 1024], BF16)
        QWDT = self.sbuf_local("QWDT", [128, 2 * 1024], F32R)
        HMX = self.sbuf_local("HMX", [128, 512], BF16)
        HMN = self.sbuf_local("HMN", [128, 512], BF16)
        SXC = self.sbuf_local("SXC", [128, 48], F32R)
        SXROW = self.sbuf_local("SXROW", [1, 512], F32R)
        SHROW = self.sbuf_local("SHROW", [1, 512], F32R)
        OSCROW = self.sbuf_local("OSCROW", [1, 512], F32R)
        mm0 = s["mmA"].v

        AMAX = self.ACC[:, 0:2048]
        NMIN = self.ACC[:, 2048:4096]
        p.drain("vector")
        p.ei("vector", s["vA"], lambda q: q.tensor_tensor(
            AMAX, self.X[:, 0:T], self.X[:, T:2 * T], Al.max))
        p.drain("vector")
        p.ei("vector", s["vA"], lambda q: q.tensor_tensor(
            NMIN, self.X[:, 0:T], self.X[:, T:2 * T], Al.min))
        for cch in range(2, NCH):
            p.drain("vector")
            p.ei("vector", s["vA"], lambda q, cch=cch: q.tensor_tensor(
                AMAX, AMAX, self.X[:, cch * T:(cch + 1) * T], Al.max))
            p.drain("vector")
            p.ei("vector", s["vA"], lambda q, cch=cch: q.tensor_tensor(
                NMIN, NMIN, self.X[:, cch * T:(cch + 1) * T], Al.min))
        p.drain("vector")
        p.ei("vector", s["vA"], lambda q: q.tensor_scalar(
            NMIN, NMIN, -1.0, None, Al.mult))
        p.drain("vector")
        p.ei("vector", s["vA"], lambda q: q.tensor_tensor(AMAX, AMAX, NMIN, Al.max))
        vv = s["vA"].v
        p.wait("tensor", s["vA"], vv)
        for tc in range(16):
            bk2 = 6 + (tc % 2)
            p.wait("tensor", s["vB"], s["vB"].v - 1)
            mvt = p.ei("tensor", s["mmB"], lambda q, tc=tc, bk2=bk2: q.transpose(
                self.PS[bk2][:, 0:128].bitcast(F32R),
                self.ACC[:, tc * 128:(tc + 1) * 128], self.IDR[:, :]))
            p.wait("vector", s["mmB"], mvt)
            p.drain("vector")
            p.ei("vector", s["vB"], lambda q, tc=tc, bk2=bk2: q.tensor_reduce(
                SXC[:, tc:tc + 1], self.PS[bk2][:, 0:128], AX, Al.max))
        p.drain("vector")
        p.ei("vector", s["vB"], lambda q: q.tensor_scalar(
            SXC[:, 16:32], SXC[:, 0:16], EPS, None, Al.max))
        p.drain("vector")
        p.ei("vector", s["vB"], lambda q: q.reciprocal(SXC[:, 32:48], SXC[:, 16:32]))
        p.drain("vector")
        vv = p.ei("vector", s["vB"], lambda q: q.tensor_scalar(
            SXC[:, 32:48], SXC[:, 32:48], 127.0, None, Al.mult))
        p.wait("tensor", s["vB"], vv)
        for tt in range(4):
            tof = tt * 512
            for sc4 in range(4):
                tc = tt * 4 + sc4
                bk2 = 6 + (sc4 % 2)
                p.wait("tensor", s["vB"], s["vB"].v - 1)
                mvt = p.ei("tensor", s["mmB"], lambda q, tc=tc, bk2=bk2: q.transpose(
                    self.PS[bk2][0:1, 0:128].bitcast(F32R),
                    SXC[:, 32 + tc:33 + tc].bitcast(F32R), self.IDR[:, :]))
                p.wait("vector", s["mmB"], mvt)
                p.drain("vector")
                p.ei("vector", s["vB"], lambda q, sc4=sc4, bk2=bk2: q.tensor_copy(
                    SXROW[0:1, sc4 * 128:(sc4 + 1) * 128], self.PS[bk2][0:1, 0:128]))
            p.wait("tensor", s["vB"])
            p.wait("tensor", s["vA"])
            mv = p.ei("tensor", s["mmA"], lambda q: q.matmul(
                self.PS[4][:, :], self.ONEROW[0:1, :], SXROW[0:1, :],
                start=True, stop=True))
            p.wait("vector", s["mmA"], mv)
            for cch in range(NCH):
                p.drain("vector")
                p.ei("vector", s["vA"], lambda q, cch=cch, tof=tof: q.tensor_tensor(
                    self.NTMP[:, :], self.X[:, cch * T + tof: cch * T + tof + 512],
                    self.PS[4][:, :], Al.mult))
                p.drain("vector")
                p.ei("vector", s["vA"], lambda q, cch=cch, tof=tof: q.tensor_scalar(
                    QXB[:, cch * T + tof: cch * T + tof + 512], self.NTMP[:, :],
                    MAGIC, -MAGIC, Al.add, Al.add))
            p.wait("tensor", s["vA"], s["vA"].v)
        gq_marks = []
        dn_marks = []
        fin_marks = []
        for tt in range(4):
            tof = tt * 512
            hm_marks = []
            for m in range(32):
                slot = len(gq_marks) % 2
                sw1 = s["sld"] if slot == 0 else s["sldB"]
                sw2 = s["gld"] if slot == 0 else s["gldB"]
                pace = gq_marks[-2] if len(gq_marks) >= 2 else mm0
                p.wait("sync", s["mmA"], pace)
                p.wait("gpsimd", s["mmA"], pace)
                l1 = p.ei("sync", sw1, lambda q, m=m, slot=slot: q.dma_start(
                    out=QWGT[:, slot * 1024: slot * 1024 + 1024],
                    in_=self.qwg_t[m]), 16)
                l2 = p.ei("gpsimd", sw2, lambda q, m=m, slot=slot: q.dma_start(
                    out=QWUT[:, slot * 1024: slot * 1024 + 1024],
                    in_=self.qwu_t[m]), 16)
                p.wait("tensor", sw1, l1)
                p.wait("tensor", sw2, l2)
                if len(hm_marks) >= 2:
                    p.wait("tensor", s["vA"], hm_marks[-2])
                pg0, pg1 = (0, 1) if m % 2 == 0 else (2, 3)
                def mkgu(q, tof=tof, slot=slot, pg0=pg0, pg1=pg1):
                    for k in range(NCH):
                        st, sp = (k == 0), (k == NCH - 1)
                        q.matmul(self.PS[pg0][:, :],
                                 QWGT[:, slot * 1024 + k * 128: slot * 1024 + (k + 1) * 128],
                                 QXB[:, k * T + tof: k * T + tof + 512], start=st, stop=sp)
                        ins = q.matmul(self.PS[pg1][:, :],
                                       QWUT[:, slot * 1024 + k * 128: slot * 1024 + (k + 1) * 128],
                                       QXB[:, k * T + tof: k * T + tof + 512], start=st, stop=sp)
                    return ins
                if m == 0:
                    p.wait("tensor", s["scA"], s["scA"].v)
                    p.wait("tensor", s["vA"], s["vA"].v)
                mv = p.ei("tensor", s["mmA"], mkgu)
                gq_marks.append(mv)
                p.wait("vector", s["mmA"], mv)
                if m == 0:
                    p.drain("vector")
                p.ei("vector", s["vA"], lambda q, pg0=pg0: q.tensor_relu(
                    self.XSQ[:, :], self.PS[pg0][:, :]))
                vvr = s["vA"].v
                p.wait("scalar", s["vA"], vvr)
                p.drain("scalar")
                sv = p.ei("scalar", s["scA"], lambda q: q.activation(
                    self.NTMP[:, :], self.XSQ[:, :], AF.Square))
                p.wait("vector", s["scA"], sv)
                hv = p.ei("vector", s["vA"], lambda q, m=m, pg1=pg1: q.tensor_tensor(
                    self.ACC[:, m * 512:(m + 1) * 512], self.NTMP[:, :],
                    self.PS[pg1][:, :], Al.mult))
                hm_marks.append(hv)
                p.drain("vector")
                if m == 0:
                    p.ei("vector", s["vA"], lambda q: q.tensor_copy(
                        HMX[:, :], self.ACC[:, 0:512]))
                    p.ei("vector", s["vA"], lambda q: q.tensor_copy(
                        HMN[:, :], self.ACC[:, 0:512]))
                else:
                    p.ei("vector", s["vA"], lambda q, m=m: q.tensor_tensor(
                        HMX[:, :], HMX[:, :], self.ACC[:, m * 512:(m + 1) * 512], Al.max))
                    p.ei("vector", s["vA"], lambda q, m=m: q.tensor_tensor(
                        HMN[:, :], HMN[:, :], self.ACC[:, m * 512:(m + 1) * 512], Al.min))
            p.drain("vector")
            p.ei("vector", s["vA"], lambda q: q.tensor_scalar(
                HMN[:, :], HMN[:, :], -1.0, None, Al.mult))
            p.drain("vector")
            hmv = p.ei("vector", s["vA"], lambda q: q.tensor_tensor(
                HMX[:, :], HMX[:, :], HMN[:, :], Al.max))
            p.wait("tensor", s["vA"], hmv)
            for sc4 in range(4):
                bk2 = 6 + (sc4 % 2)
                p.wait("tensor", s["vB"], s["vB"].v - 1)
                mvt = p.ei("tensor", s["mmB"], lambda q, sc4=sc4, bk2=bk2: q.transpose(
                    self.PS[bk2][:, :].bitcast(BF16)[:, 0:128],
                    HMX[:, sc4 * 128:(sc4 + 1) * 128], self.IDB[:, :]))
                p.wait("vector", s["mmB"], mvt)
                p.drain("vector")
                p.ei("vector", s["vB"], lambda q, sc4=sc4, bk2=bk2: q.tensor_reduce(
                    SXC[:, sc4:sc4 + 1], self.PS[bk2][:, :].bitcast(BF16)[:, 0:128],
                    AX, Al.max))
            p.drain("vector")
            p.ei("vector", s["vB"], lambda q, tt=tt: q.tensor_tensor(
                SXC[:, 4:8], SXC[:, 32 + tt * 4: 36 + tt * 4],
                SXC[:, 32 + tt * 4: 36 + tt * 4], Al.mult))
            p.drain("vector")
            p.ei("vector", s["vB"], lambda q, tt=tt: q.tensor_tensor(
                SXC[:, 4:8], SXC[:, 4:8], SXC[:, 32 + tt * 4: 36 + tt * 4], Al.mult))
            p.drain("vector")
            p.ei("vector", s["vB"], lambda q: q.tensor_scalar(
                SXC[:, 4:8], SXC[:, 4:8], SWC[:, 7:8], None, Al.mult))
            p.drain("vector")
            p.ei("vector", s["vB"], lambda q: q.tensor_scalar(
                SXC[:, 8:12], SXC[:, 4:8], EPS, None, Al.mult))
            p.drain("vector")
            p.ei("vector", s["vB"], lambda q: q.tensor_tensor(
                SXC[:, 12:16], SXC[:, 8:12], SXC[:, 0:4], Al.max))
            p.drain("vector")
            p.ei("vector", s["vB"], lambda q: q.reciprocal(SXC[:, 0:4], SXC[:, 12:16]))
            p.drain("vector")
            p.ei("vector", s["vB"], lambda q: q.tensor_scalar(
                SXC[:, 0:4], SXC[:, 0:4], 127.0, None, Al.mult))
            p.drain("vector")
            p.ei("vector", s["vB"], lambda q: q.reciprocal(SXC[:, 8:12], SXC[:, 4:8]))
            p.drain("vector")
            p.ei("vector", s["vB"], lambda q: q.tensor_tensor(
                SXC[:, 16:20], SXC[:, 12:16], SXC[:, 8:12], Al.mult))
            p.drain("vector")
            p.ei("vector", s["vB"], lambda q: q.tensor_scalar(
                SXC[:, 16:20], SXC[:, 16:20], SWC[:, 6:7], 1.0 / 127.0,
                Al.mult, Al.mult))
            vv = s["vB"].v
            p.wait("tensor", s["vB"], vv)
            for sc4 in range(4):
                bk2 = 6 + (sc4 % 2)
                p.wait("tensor", s["vB"], s["vB"].v - 1)
                mvt = p.ei("tensor", s["mmB"], lambda q, sc4=sc4, bk2=bk2: q.transpose(
                    self.PS[bk2][0:1, 0:128].bitcast(F32R),
                    SXC[:, sc4:sc4 + 1].bitcast(F32R), self.IDR[:, :]))
                p.wait("vector", s["mmB"], mvt)
                p.drain("vector")
                p.ei("vector", s["vB"], lambda q, sc4=sc4, bk2=bk2: q.tensor_copy(
                    SHROW[0:1, sc4 * 128:(sc4 + 1) * 128], self.PS[bk2][0:1, 0:128]))
                p.wait("tensor", s["vB"])
                mvt = p.ei("tensor", s["mmB"], lambda q, sc4=sc4, bk2=bk2: q.transpose(
                    self.PS[bk2][0:1, 0:128].bitcast(F32R),
                    SXC[:, 16 + sc4:17 + sc4].bitcast(F32R), self.IDR[:, :]))
                p.wait("vector", s["mmB"], mvt)
                p.drain("vector")
                p.ei("vector", s["vB"], lambda q, sc4=sc4, bk2=bk2: q.tensor_copy(
                    OSCROW[0:1, sc4 * 128:(sc4 + 1) * 128], self.PS[bk2][0:1, 0:128]))
            vv = s["vB"].v
            p.wait("tensor", s["vB"], vv)
            mv = p.ei("tensor", s["mmA"], lambda q: q.matmul(
                self.PS[4][:, :], self.ONEROW[0:1, :], SHROW[0:1, :],
                start=True, stop=True))
            mv2 = p.ei("tensor", s["mmA"], lambda q: q.matmul(
                self.PS[5][:, :], self.ONEROW[0:1, :], OSCROW[0:1, :],
                start=True, stop=True))
            p.wait("vector", s["mmA"], mv2)
            p.wait("scalar", s["mmA"], mv2)
            osv = s["mmA"].v
            qh_marks = []
            for m in range(32):
                scr = self.NTMP if m % 2 == 0 else self.XSQ
                if m >= 2:
                    p.drain("vector")
                p.ei("vector", s["vA"], lambda q, m=m, scr=scr: q.tensor_tensor(
                    scr[:, :], self.ACC[:, m * 512:(m + 1) * 512],
                    self.PS[4][:, :], Al.mult))
                p.drain("vector")
                qh_marks.append(p.ei("vector", s["vA"], lambda q, m=m, scr=scr: q.tensor_scalar(
                    self.ACC[:, m * 512:(m + 1) * 512], scr[:, :],
                    MAGIC, -MAGIC, Al.add, Al.add)))
            for md in range(NCH):
                pg = 2 + (md % 2)
                p.wait("tensor", s["scA"])
                for quar in range(4):
                    dslot = len(dn_marks) % 2
                    sd = s["srw"] if dslot == 0 else s["sld2"]
                    p.wait("sync", s["mmA"], dn_marks[-2] if len(dn_marks) >= 2 else mm0)
                    lw = p.ei("sync", sd, lambda q, md=md, quar=quar, dslot=dslot: q.dma_start(
                        out=QWDT[:, dslot * 1024: dslot * 1024 + 1024],
                        in_=self.qwd_t[md][:, quar * 1024:(quar + 1) * 1024]), 16)
                    p.wait("tensor", sd, lw)
                    p.wait("tensor", s["vA"], qh_marks[quar * 8 + 7])
                    def mkdn(q, md=md, quar=quar, pg=pg, dslot=dslot):
                        for kk in range(8):
                            kh = quar * 8 + kk
                            ins = q.matmul(self.PS[pg][:, :],
                                           QWDT[:, dslot * 1024 + kk * 128: dslot * 1024 + (kk + 1) * 128],
                                           self.ACC[:, kh * 512:(kh + 1) * 512],
                                           start=(kh == 0), stop=(kh == 31))
                        return ins
                    mv = p.ei("tensor", s["mmA"], mkdn)
                    dn_marks.append(mv)
                if fin_marks:
                    p.wait("scalar", s["vA"], fin_marks[-1][0])
                p.wait("scalar", s["mmA"], dn_marks[-1])
                p.drain("scalar")
                scv = p.ei("scalar", s["scA"], lambda q, pg=pg: q.copy(
                    self.XSQ[:, :], self.PS[pg][:, :]))
                p.wait("vector", s["scA"], scv)
                p.drain("vector")
                p.ei("vector", s["vA"], lambda q: q.tensor_tensor(
                    self.XSQ[:, :], self.XSQ[:, :], self.PS[5][:, :], Al.mult))
                p.drain("vector")
                fv = p.ei("vector", s["vA"], lambda q, md=md, tof=tof: q.tensor_tensor(
                    self.X[:, md * T + tof: md * T + tof + 512],
                    self.X[:, md * T + tof: md * T + tof + 512],
                    self.XSQ[:, :], Al.add))
                fin_marks.append((fv, scv))
        p.run()

    def phase_final(self):
        nc, s = self.nc, self.sem
        p = Prog(nc)
        ROWA = self.sbuf_local("ROWf1", [1, T], F32R)
        ROWB = self.sbuf_local("ROWf2", [1, T], F32R)
        OTILE = self.sbuf_local("OTILE", [128, D], F32)
        self.ln_stats2(p, self.X, ROWA, ROWB)
        self.ln_rows_finalize(p, ROWA, ROWB)
        self.ln_norm(p, self.X, self.X, ROWB, ROWA)
        for i in range(16):
            p.wait("vector", s["sst"])
            for cch in range(NCH):
                bk = 6 + (cch % 2)
                p.wait("tensor", s["vB"], s["vB"].v - 1)
                p.wait("tensor", s["vA"])
                mv = p.ei("tensor", s["mmB"], lambda q, i=i, cch=cch, bk=bk: q.transpose(
                    self.PS[bk][:, 0:128].bitcast(F32R),
                    self.X[:, cch * T + i * 128: cch * T + (i + 1) * 128], self.IDR[:, :]))
                p.wait("vector", s["mmB"], mv)
                p.drain("vector")
                p.ei("vector", s["vB"], lambda q, cch=cch, bk=bk: q.tensor_copy(
                    OTILE[:, cch * 128:(cch + 1) * 128], self.PS[bk][:, 0:128]))
            p.wait("scalar", s["vB"], s["vB"].v)
            p.ei("scalar", s["sst"], lambda q, i=i: q.dma_start(
                out=self.out[i * 128:(i + 1) * 128, :], in_=OTILE[:, :]), 16)
        p.wait("scalar", s["sst"], s["sst"].v)
        p.run()


def build_nc():
    b = Builder()
    b.conv_ep_marks = {}
    return b.build()


def kernel(**inputs):
    in_maps = _host_layout(inputs)
    nc = build_nc()
    res = run_bass_kernel_spmd(nc, in_maps, core_ids=list(range(8)))
    full = np.zeros((4, 4096, D), np.float32)
    for core in range(8):
        b, half = core // 2, core % 2
        full[b, half * 2048:(half + 1) * 2048] = res.results[core]["out"]
    return full


# version 15
# speedup vs baseline: 1.0281x; 1.0098x over previous
import numpy as np
from contextlib import ExitStack

import concourse.bass as bass
import concourse.mybir as mybir
from concourse.bass_utils import run_bass_kernel_spmd

F32 = mybir.dt.float32
F32R = mybir.dt.float32r
BF16 = mybir.dt.bfloat16
AF = mybir.ActivationFunctionType
Al = mybir.AluOpType
AX = mybir.AxisListType.X

T = 2048
D = 1024
NCH = 8
HE = 1408
HCH = 11
CH = 128
NCK = T // CH
HEADS, DK, DV = 16, 32, 64
FI = 4096
EPS = 1e-5
MAGIC = float(np.float32(1.5 * 2 ** 23))
LN_DKS = float(np.log(DK ** -0.5))

TAPS = ()
PHASES = 99
DBGOUT = False



def _pack_cols(*arrs):
    cols = []
    for a in arrs:
        a = np.ascontiguousarray(np.asarray(a, np.float32))
        if a.ndim == 2 and a.shape[0] == 128 and a.shape[1] <= 16:
            cols.append(np.ascontiguousarray(a))
            continue
        a = a.reshape(-1)
        if a.size <= 128:
            c = np.zeros(128, np.float32)
            c[: a.size] = a
            cols.append(c[:, None])
        else:
            assert a.size % 128 == 0
            cols.append(a.reshape(-1, 128).T)
    return np.ascontiguousarray(np.concatenate(cols, axis=1))


def _cpack_index():
    sizes = {
        "b1": 8 * HCH, "b3": 8 * HCH, "b2": 8 * NCH,
        "sb1": HCH, "sb3": HCH, "sb2": NCH,
        "lnw": NCH, "lnb": NCH, "gb": 1, "gbrow": 8,
        "l1b1": 16, "l1b2": NCH, "flag": 1,
    }
    off, idx = 0, {}
    for k, s in sizes.items():
        idx[k] = off
        off += s
    return idx, off


CPI, CP_COLS = _cpack_index()


def _ctile(w, nk, nm):
    return np.ascontiguousarray(
        w.reshape(nk, 128, nm, 128).transpose(2, 1, 0, 3).reshape(nm, 128, nk * 128))


def _rtile(w, nk, nn):
    return np.ascontiguousarray(
        w.reshape(nk, 128, nn, 512).transpose(0, 2, 1, 3))


def _host_layout(inputs):
    inp = {k: np.ascontiguousarray(np.asarray(v, np.float32)) for k, v in inputs.items()}
    filt = inp["L0_attn_filt"]
    fpad = np.zeros(128 * 17 + 127, np.float32)
    fpad[127:127 + 2048] = filt
    d_i = np.arange(17)[:, None, None]
    q_i = np.arange(128)[None, :, None]
    p_i = np.arange(128)[None, None, :]
    toep = fpad[127 + 128 * d_i + q_i - p_i]
    toep_t = np.ascontiguousarray(toep.transpose(1, 0, 2).reshape(128, 17 * 128))

    gkw2e = np.ascontiguousarray(np.concatenate(
        [inp["L1_attn_gk_w2"], inp["L1_attn_gk_b2"][None, :]], axis=0))
    gate_wT = np.ascontiguousarray(
        inp["L0_ffn_gate_w"].T.reshape(8, 128, 8).transpose(1, 0, 2).reshape(128, 64))
    gkw1t = np.ascontiguousarray(
        inp["L1_attn_gk_w1"].reshape(8, 128, 16).transpose(1, 0, 2).reshape(128, 128))
    rms_wb = np.ascontiguousarray(
        np.broadcast_to(inp["L1_attn_rms_w"][None, :], (128, DV)))

    w1t = np.concatenate([_ctile(inp["L0_ffn_w1"][e], 8, HCH) for e in range(8)]
                         + [_ctile(inp["L0_ffn_sw1"], 8, HCH)])
    w3t = np.concatenate([_ctile(inp["L0_ffn_w3"][e], 8, HCH) for e in range(8)]
                         + [_ctile(inp["L0_ffn_sw3"], 8, HCH)])
    w2t = np.concatenate([_ctile(inp["L0_ffn_w2"][e], HCH, 8) for e in range(8)]
                         + [_ctile(inp["L0_ffn_sw2"], HCH, 8)])

    common = {
        "toep": toep_t, "gkw2e": gkw2e, "gate_wT": gate_wT, "rms_wb": rms_wb,
        "w1": w1t, "w3": w3t, "w2": w2t,
        "Wq": _ctile(inp["L1_attn_Wq"], 8, 4), "Wk": _ctile(inp["L1_attn_Wk"], 8, 4),
        "Wv": _rtile(inp["L1_attn_Wv"], 8, 2), "Wg": _rtile(inp["L1_attn_Wg"], 8, 2),
        "Wo": _ctile(inp["L1_attn_Wo"], 8, 8),
        "gkw1": gkw1t,
        "W1": _ctile(inp["L1_attn_W1"], 8, 16), "W2": _ctile(inp["L1_attn_W2"], 16, 8),
        "wgateT": np.ascontiguousarray(inp["L1_ffn_w_gate"].T),
        "wupT": np.ascontiguousarray(inp["L1_ffn_w_up"].T),
        "wdownT": np.ascontiguousarray(inp["L1_ffn_w_down"].T),
    }
    x = inp["L0_attn_x"]
    in_maps = []
    for core in range(8):
        b, half = core // 2, core % 2
        x_ext = np.zeros((4096, D), np.float32)
        avail = 4096 - half * 2048
        x_ext[:avail] = x[b, half * 2048: half * 2048 + avail]
        cpack = _pack_cols(
            inp["L0_ffn_b1"], inp["L0_ffn_b3"], inp["L0_ffn_b2"],
            inp["L0_ffn_sb1"], inp["L0_ffn_sb3"], inp["L0_ffn_sb2"],
            inp["L1_attn_ln_w"], inp["L1_attn_ln_b"],
            inp["L0_ffn_gate_b"],
            np.broadcast_to(inp["L0_ffn_gate_b"][None, :], (128, 8)),
            inp["L1_attn_b1"], inp["L1_attn_b2"],
            np.full(128, float(half), np.float32),
        )
        m = dict(common)
        m["x_ext"] = x_ext
        m["cpack"] = cpack
        in_maps.append(m)
    return in_maps



ENGINES = ("sync", "gpsimd", "tensor", "vector", "scalar")


class SemT:
    def __init__(self, ctx, nc, name):
        self.h = ctx.enter_context(nc.semaphore(name))
        self.v = 0


class Prog:
    def __init__(self, nc):
        self.nc = nc
        self.streams = {e: [] for e in ENGINES}

    def em(self, eng, fn):
        self.streams[eng].append(fn)

    def wait(self, eng, sem, val=None):
        v = sem.v if val is None else val
        if v > 0:
            self.streams[eng].append(lambda q, s=sem, v=v: q.wait_ge(s.h, v))

    def drain(self, eng):
        self.streams[eng].append(lambda q: q.drain())

    def ei(self, eng, sem, fn, k=1):
        if sem is None:
            self.streams[eng].append(fn)
            return None
        sem.v += k
        self.streams[eng].append(lambda q, fn=fn, sem=sem, k=k: fn(q).then_inc(sem.h, k))
        return sem.v

    def run(self):
        with self.nc.Block() as block:
            for eng in ENGINES:
                fns = self.streams[eng]
                if not fns:
                    continue
                getattr(block, eng)(lambda q, fns=fns: [f(q) for f in fns])



class Builder:
    def __init__(self):
        self.nc = bass.Bass("TRN2", target_bir_lowering=False)
        nc = self.nc
        self.dram = {}

        def din(name, shape, dtype=F32R):
            self.dram[name] = nc.dram_tensor(name, list(shape), dtype,
                                             kind="ExternalInput").ap()
        din("x_ext", (4096, D))
        din("toep", (128, 17 * 128))
        din("gate_wT", (128, 64))
        din("w1", (99, 128, 1024)); din("w3", (99, 128, 1024))
        din("w2", (72, 128, HE))
        din("Wq", (4, 128, 1024)); din("Wk", (4, 128, 1024))
        din("Wv", (8, 2, 128, 512)); din("Wg", (8, 2, 128, 512))
        din("Wo", (8, 128, 1024)); din("gkw1", (128, 128)); din("gkw2e", (17, 512))
        din("W1", (16, 128, 1024)); din("W2", (8, 128, 2048))
        din("wgateT", (D, FI)); din("wupT", (D, FI)); din("wdownT", (FI, D))
        din("cpack", (128, CP_COLS), F32)
        din("rms_wb", (128, DV), F32)
        self.out = nc.dram_tensor("out", [T, D], F32, kind="ExternalOutput").ap()
        self.taps = {t: nc.dram_tensor("tap_" + t, [T, D], F32,
                                       kind="ExternalOutput").ap() for t in TAPS}
        self.x2_spill = nc.dram_tensor("x2_spill", [128, NCH * T], F32R).ap()
        self.cc_in = nc.dram_tensor("cc_in", [128, 1024], F32).ap()
        self.cc_out = nc.dram_tensor("cc_out", [128, 1024], F32).ap()
        self.qst_d = nc.dram_tensor("qst_d", [128, NCK * 512], BF16).ap()
        qk = "ExternalOutput" if DBGOUT else "Internal"
        self.qwg_t = nc.dram_tensor("qwg_t", [32, 128, 1024], BF16, kind=qk).ap()
        self.qwu_t = nc.dram_tensor("qwu_t", [32, 128, 1024], BF16, kind=qk).ap()
        self.qwd_t = nc.dram_tensor("qwd_t", [NCH, 128, FI], F32R, kind=qk).ap()
        if DBGOUT:
            self.dbg_swc = nc.dram_tensor("dbg_swc", [128, 16], F32, kind=qk).ap()
            self.dbg_mc = nc.dram_tensor("dbg_mc", [128, 16], F32, kind=qk).ap()
            self.dbg_tern = nc.dram_tensor("dbg_tern", [128, 2048], F32, kind=qk).ap()
            self.dbg_wt = nc.dram_tensor("dbg_wt", [128, 8192], F32, kind=qk).ap()

        self.ctx = ExitStack()
        self.sb = {}

    def sbuf(self, name, shape, dtype):
        t = self.ctx.enter_context(self.nc.sbuf_tensor(name, list(shape), dtype))
        self.sb[name] = t
        return t

    def sbuf_local(self, name, shape, dtype):
        t = self.phase_ctx.enter_context(self.nc.sbuf_tensor(name, list(shape), dtype))
        self.sb[name] = t
        return t

    def cpc(self, key, j=0):
        return self.CP[:, CPI[key] + j: CPI[key] + j + 1]

    def build(self):
        nc = self.nc
        with self.ctx, nc.allow_low_precision(
                "f32r (12-bit significand) stream accumulation is intentional; "
                "precision exceeds the sanctioned bf16+fp32 bar"):
            self.X = self.sbuf("X", [128, NCH * T], F32R)
            self.ACC = self.sbuf("ACC", [128, NCH * T], F32R)
            save_base = nc.sbuf_base
            nc.sbuf_base = save_base - NCH * T * 4
            self.ACCB = nc.sbuf_tensor("ACCB", [128, 2 * NCH * T], BF16).__enter__()
            assert nc.sbuf_base == save_base
            self.CP = self.sbuf("CP", [128, CP_COLS], F32)
            self.IDR = self.sbuf("IDR", [128, 128], F32R)
            self.IDB = self.sbuf("IDB", [128, 128], BF16)
            self.MASKB = self.sbuf("MASKB", [128, 128], BF16)
            self.ONECOL = self.sbuf("ONECOL", [128, 1], F32R)
            self.ONEROW = self.sbuf("ONEROW", [1, 128], F32R)
            self.ROWA = self.sbuf("ROWA", [1, T], F32R)
            self.ROWB = self.sbuf("ROWB", [1, T], F32R)
            self.XSQ = self.sbuf("XSQ", [128, 512], F32R)
            self.NTMP = self.sbuf("NTMP", [128, 512], F32R)
            self.PS = [self.ctx.enter_context(nc.psum_tensor(f"PS{i}", [128, 512], F32))
                       for i in range(8)]
            self.sem = {n: SemT(self.ctx, nc, n) for n in
                        ["sld", "sldB", "sld2", "sld2B", "gld", "gldB", "gq", "gqB",
                         "gk2", "guc", "srw", "sst", "mmA", "mmB", "vA", "vB", "scA",
                         "cc", "ginit"]}
            phases = [self.phase_init]
            if PHASES >= 1:
                phases += [self.phase_conv, lambda: self.tap("x1")]
            if PHASES >= 2:
                phases += [self.phase_moe, lambda: self.tap("x2")]
            if PHASES >= 3:
                phases += [self.phase_gla_d0, self.phase_gla_main,
                           self.phase_gla_post, lambda: self.gla_ctx.close(),
                           lambda: self.tap("xp")]
            if PHASES >= 4:
                phases += [self.phase_mlp, lambda: self.tap("x3")]
            if PHASES >= 5:
                phases += [self.phase_bitffn_prep, self.phase_bitffn,
                           lambda: self.bit_ctx.close(), lambda: self.tap("fin2")]
            if PHASES >= 6:
                phases += [self.phase_final]
            for ph in phases:
                with ExitStack() as pc:
                    self.phase_ctx = pc
                    name = getattr(ph, "__name__", "misc")
                    with nc.named_scope(name):
                        ph()
        return nc

    def phase_init(self):
        p = Prog(self.nc)
        s = self.sem
        IOTA = self.sbuf_local("IOTA", [128, 128], F32)
        p.ei("sync", s["sld"], lambda q: q.dma_start(out=self.CP[:, :], in_=self.dram["cpack"]), 16)

        gi = p.ei("gpsimd", s["ginit"], lambda q: q.iota(
            IOTA[:, :], [[1, 128]], base=0, channel_multiplier=-1,
            allow_small_or_imprecise_dtypes=True))
        p.wait("vector", s["ginit"], gi)
        p.ei("vector", s["vA"], lambda q: q.tensor_single_scalar(self.IDR[:, :], IOTA[:, :], 0.0, Al.is_equal))
        p.ei("vector", s["vA"], lambda q: q.tensor_single_scalar(self.IDB[:, :], IOTA[:, :], 0.0, Al.is_equal))
        p.ei("vector", s["vA"], lambda q: q.tensor_single_scalar(self.MASKB[:, :], IOTA[:, :], 0.0, Al.is_ge))
        p.ei("vector", s["vA"], lambda q: q.tensor_single_scalar(
            self.ONECOL[:, :], IOTA[:, 0:1], -1e30, Al.is_ge))
        p.ei("vector", s["vA"], lambda q: q.tensor_single_scalar(
            self.ONEROW[:, :], IOTA[0:1, 0:128], -1e30, Al.is_ge))
        p.run()

    def ln_stats2(self, p, buf, row_a, row_b, sq=None):
        s = self.sem
        sq = self.ACC if sq is None else sq
        mmB_prev = 0
        for t in range(8):
            tof = t * 256
            p.wait("tensor", s["vA"])
            def mksum(q, tof=tof):
                for c in range(NCH):
                    ins = q.matmul(self.PS[6][0:1, 0:256], self.ONECOL[:, :],
                                   buf[:, c * T + tof: c * T + tof + 256],
                                   start=(c == 0), stop=(c == NCH - 1))
                return ins
            mv = p.ei("tensor", s["mmA"], mksum)
            p.wait("vector", s["mmA"], mv)
            p.drain("vector")
            p.ei("vector", s["vA"], lambda q, tof=tof: q.tensor_copy(
                row_a[0:1, tof: tof + 256], self.PS[6][0:1, 0:256]))
            p.wait("scalar", s["mmB"], mmB_prev)
            p.drain("scalar")
            for c in range(NCH):
                scv = p.ei("scalar", s["scA"], lambda q, c=c, tof=tof: q.activation(
                    sq[:, c * 256:(c + 1) * 256],
                    buf[:, c * T + tof: c * T + tof + 256], AF.Square))
            p.wait("tensor", s["scA"], scv)
            def mksq(q):
                for c in range(NCH):
                    ins = q.matmul(self.PS[7][0:1, 0:256], self.ONECOL[:, :],
                                   sq[:, c * 256:(c + 1) * 256],
                                   start=(c == 0), stop=(c == NCH - 1))
                return ins
            mmB_prev = p.ei("tensor", s["mmB"], mksq)
            p.wait("vector", s["mmB"], mmB_prev)
            p.drain("vector")
            p.ei("vector", s["vA"], lambda q, tof=tof: q.tensor_copy(
                row_b[0:1, tof: tof + 256], self.PS[7][0:1, 0:256]))

    def ln_rows_finalize(self, p, row_a, row_b, n=D):
        s = self.sem
        p.drain("vector")
        p.ei("vector", s["vA"], lambda q: q.tensor_scalar(
            row_a[0:1, :], row_a[0:1, :], 1.0 / n, None, Al.mult))
        p.drain("vector")
        p.ei("vector", s["vA"], lambda q: q.tensor_scalar(
            row_b[0:1, :], row_b[0:1, :], 1.0 / n, EPS, Al.mult, Al.add))
        for seg in range(4):
            sl = slice(seg * 512, seg * 512 + 512)
            p.drain("vector")
            p.ei("vector", s["vA"], lambda q, sl=sl: q.scalar_tensor_tensor(
                self.XSQ[0:1, 0:512], row_a[0:1, sl], -1.0, row_a[0:1, sl],
                Al.mult, Al.mult))
            p.drain("vector")
            p.ei("vector", s["vA"], lambda q, sl=sl: q.tensor_tensor(
                row_b[0:1, sl], row_b[0:1, sl], self.XSQ[0:1, 0:512], Al.add))
        p.drain("vector")
        vv = p.ei("vector", s["vA"], lambda q: q.reciprocal(
            row_b[0:1, :], row_b[0:1, :]))
        p.wait("scalar", s["vA"], vv)
        p.drain("scalar")
        scv = p.ei("scalar", s["scA"], lambda q: q.activation(
            row_b[0:1, :], row_b[0:1, :], AF.Sqrt))
        p.wait("vector", s["scA"], scv)
        p.drain("vector")
        p.ei("vector", s["vA"], lambda q: q.scalar_tensor_tensor(
            row_a[0:1, :], row_a[0:1, :], -1.0, row_b[0:1, :], Al.mult, Al.mult))

    def ln_norm(self, p, src, dst, row_rstd, row_mrstd, affine=None, dst_dtype=None):
        s = self.sem
        p.wait("tensor", s["vA"], s["vA"].v)
        p.wait("tensor", s["scA"], s["scA"].v)
        for t in range(4):
            tof = t * 512
            def mkb(q, tof=tof):
                q.matmul(self.PS[4][:, :], self.ONEROW[0:1, :],
                         row_rstd[0:1, tof: tof + 512].bitcast(F32R), start=True, stop=True)
                return q.matmul(self.PS[5][:, :], self.ONEROW[0:1, :],
                                row_mrstd[0:1, tof: tof + 512].bitcast(F32R), start=True, stop=True)
            mv = p.ei("tensor", s["mmA"], mkb)
            p.wait("vector", s["mmA"], mv)
            for c in range(NCH):
                p.drain("vector")
                p.ei("vector", s["vA"], lambda q, c=c, tof=tof: q.tensor_tensor(
                    self.NTMP[:, :], src[:, c * T + tof: c * T + tof + 512],
                    self.PS[4][:, :], Al.mult))
                p.drain("vector")
                if affine is None:
                    vv = p.ei("vector", s["vA"], lambda q, c=c, tof=tof: q.tensor_tensor(
                        dst[:, c * T + tof: c * T + tof + 512], self.NTMP[:, :],
                        self.PS[5][:, :], Al.add))
                else:
                    wk, bk = affine
                    p.ei("vector", s["vA"], lambda q, c=c, tof=tof: q.tensor_tensor(
                        self.NTMP[:, :], self.NTMP[:, :], self.PS[5][:, :], Al.add))
                    p.drain("vector")
                    vv = p.ei("vector", s["vA"], lambda q, c=c, tof=tof, wk=wk, bk=bk: q.tensor_scalar(
                        dst[:, c * T + tof: c * T + tof + 512], self.NTMP[:, :],
                        self.cpc(wk, c), self.cpc(bk, c), Al.mult, Al.add))
            p.wait("tensor", s["vA"], s["vA"].v)

    def phase_conv(self):
        nc, s = self.nc, self.sem
        p = Prog(nc)
        RING = 18
        xr0 = self.sbuf_local("xre0", [128, D], F32R)
        xr1 = self.sbuf_local("xre1", [128, D], F32R)
        TOEP = self.sbuf_local("TOEP", [128, 17 * 128], F32R)
        xtm = self.sbuf_local("xtm_b", [128, D], F32R)
        stt = self.sbuf_local("stt_b", [128, 4], F32)
        sq1 = self.sbuf_local("sq1_b", [128, D], F32)
        cvE = SemT(self.ctx, nc, "cvE")

        def ring(j):
            sl = j % RING
            if sl < 16:
                return self.ACC[:, sl * D:(sl + 1) * D]
            return (xr0 if sl == 16 else xr1)[:, :]

        def emit_load(j):
            if j >= RING:
                p.wait("sync", cvE, j - RING + 1)
            p.ei("sync", s["sld"], lambda q, j=j: q.dma_start(
                out=ring(j), in_=self.dram["x_ext"][j * 128:(j + 1) * 128, :]), 16)

        p.ei("sync", s["sld"], lambda q: q.dma_start(
            out=TOEP[:, :], in_=self.dram["toep"]), 16)
        for j in range(17):
            emit_load(j)

        mmA = s["mmA"]
        for i in range(16):
            if i + 17 < 32:
                emit_load(i + 17)
            b0, b1 = (i % 2) * 2, (i % 2) * 2 + 1
            if i >= 2:
                p.wait("tensor", s["vA"], self.conv_ep_marks[i - 2])
            p.wait("tensor", s["sld"])
            def mkconv(q, i=i, b0=b0, b1=b1):
                for d in range(17):
                    st, sp = (d == 0), (d == 16)
                    lhsT = TOEP[:, d * 128:(d + 1) * 128]
                    src = ring(i + d)
                    q.matmul(self.PS[b0][:, :], lhsT, src[:, 0:512], start=st, stop=sp)
                    ins = q.matmul(self.PS[b1][:, :], lhsT, src[:, 512:1024], start=st, stop=sp)
                return ins
            mv = p.ei("tensor", mmA, mkconv)
            p.wait("vector", mmA, mv)
            p.drain("vector")
            p.ei("vector", s["vA"], lambda q, i=i, b0=b0: q.tensor_tensor(
                xtm[:, 0:512], ring(i)[:, 0:512], self.PS[b0][:, :], Al.add))
            p.drain("vector")
            vv = p.ei("vector", cvE, lambda q, i=i, b1=b1: q.tensor_tensor(
                xtm[:, 512:1024], ring(i)[:, 512:1024], self.PS[b1][:, :], Al.add))
            p.wait("scalar", cvE, vv)
            p.drain("scalar")
            scv = p.ei("scalar", s["scA"], lambda q: q.activation(sq1[:, :], xtm[:, :], AF.Square))
            p.wait("vector", s["scA"], scv)
            p.drain("vector")
            p.ei("vector", s["vA"], lambda q: q.reduce_sum(stt[:, 0:1], xtm[:, :], AX))
            p.drain("vector")
            p.ei("vector", s["vA"], lambda q: q.reduce_sum(stt[:, 1:2], sq1[:, :], AX))
            p.drain("vector")
            p.ei("vector", s["vA"], lambda q: q.tensor_scalar(
                stt[:, 0:1], stt[:, 0:1], 1.0 / D, None, Al.mult))
            p.drain("vector")
            p.ei("vector", s["vA"], lambda q: q.scalar_tensor_tensor(
                stt[:, 2:3], stt[:, 0:1], -1.0, stt[:, 0:1], Al.mult, Al.mult))
            p.drain("vector")
            p.ei("vector", s["vA"], lambda q: q.tensor_scalar(
                stt[:, 1:2], stt[:, 1:2], 1.0 / D, EPS, Al.mult, Al.add))
            p.drain("vector")
            p.ei("vector", s["vA"], lambda q: q.tensor_tensor(
                stt[:, 1:2], stt[:, 1:2], stt[:, 2:3], Al.add))
            p.drain("vector")
            vv = p.ei("vector", s["vA"], lambda q: q.reciprocal(stt[:, 2:3], stt[:, 1:2]))
            p.wait("scalar", s["vA"], vv)
            p.drain("scalar")
            scv = p.ei("scalar", s["scA"], lambda q: q.activation(
                stt[:, 3:4], stt[:, 2:3], AF.Sqrt))
            p.wait("vector", s["scA"], scv)
            p.drain("vector")
            vv = p.ei("vector", s["vA"], lambda q: q.tensor_scalar(
                xtm[:, :], xtm[:, :], stt[:, 0:1], stt[:, 3:4], Al.subtract, Al.mult))
            p.wait("tensor", s["vA"], vv)
            for c in range(NCH):
                bk = 4 + (c % 2)
                p.wait("tensor", s["vB"], s["vB"].v - 1)
                mv2 = p.ei("tensor", s["mmB"], lambda q, c=c, bk=bk: q.transpose(
                    self.PS[bk][:, 0:128].bitcast(F32R), xtm[:, c * 128:(c + 1) * 128],
                    self.IDR[:, :]))
                p.wait("vector", s["mmB"], mv2)
                p.drain("vector")
                p.ei("vector", s["vB"], lambda q, c=c, bk=bk, i=i: q.tensor_copy(
                    self.X[:, c * T + i * 128: c * T + (i + 1) * 128], self.PS[bk][:, 0:128]))
            self.conv_ep_marks[i] = s["vA"].v
        p.wait("tensor", s["vB"], s["vB"].v)
        p.run()

    def tap(self, name):
        if name not in self.taps:
            return
        nc, s = self.nc, self.sem
        p = Prog(nc)
        ttile = self.sbuf_local("tap_tm_" + name, [128, D], F32)
        for i in range(16):
            p.wait("vector", s["sst"])
            for c in range(NCH):
                bk = 6 + (c % 2)
                p.wait("tensor", s["vB"], s["vB"].v - 1)
                mv = p.ei("tensor", s["mmB"], lambda q, i=i, c=c, bk=bk: q.transpose(
                    self.PS[bk][:, 0:128].bitcast(F32R),
                    self.X[:, c * T + i * 128: c * T + (i + 1) * 128], self.IDR[:, :]))
                p.wait("vector", s["mmB"], mv)
                p.drain("vector")
                p.ei("vector", s["vB"], lambda q, c=c, bk=bk: q.tensor_copy(
                    ttile[:, c * 128:(c + 1) * 128], self.PS[bk][:, 0:128]))
            p.wait("scalar", s["vB"], s["vB"].v)
            p.ei("scalar", s["sst"], lambda q, i=i: q.dma_start(
                out=self.taps[name][i * 128:(i + 1) * 128, :], in_=ttile[:, :]), 16)
            p.wait("scalar", s["sst"], s["sst"].v)
        p.run()


    def phase_moe(self):
        nc, s = self.nc, self.sem
        p = Prog(nc)
        GWT = self.sbuf_local("GWT", [128, 64], F32R)
        RW = self.sbuf_local("RW", [8, T], BF16)
        RWSB = self.sbuf_local("RWSB", [128, 2 * 1024], BF16)
        RWROW = self.sbuf_local("RWROW", [1, 512], BF16)
        ONEROWB = self.sbuf_local("ONEROWB", [1, 128], BF16)
        WAB = self.sbuf_local("WAB", [128, 2048], F32R)
        W3B = self.sbuf_local("W3B", [128, 2048], F32R)
        W2B = self.sbuf_local("W2B", [128, 2 * 1408], BF16)
        HB = self.sbuf_local("HB", [128, HCH * 1024], BF16)
        VS1 = self.XSQ
        VS2 = self.NTMP

        p.ei("vector", s["vA"], lambda q: q.tensor_copy(
            ONEROWB[0:1, :], self.ONEROW[0:1, :]))
        for c in range(NCH):
            p.drain("vector")
            p.ei("vector", s["vA"], lambda q, c=c: q.tensor_copy(
                self.ACC[:, c * T:(c + 1) * T], self.X[:, c * T:(c + 1) * T]))

        p.wait("sync", s["mmA"], s["mmA"].v)
        gv = p.ei("sync", s["sld"], lambda q: q.dma_start(
            out=GWT[:, :], in_=self.dram["gate_wT"]), 16)
        if True:
            GROW = self.XSQ[0:8, 0:512].bitcast(F32R)
            GTM = self.NTMP[:, 0:40]
            p.wait("tensor", s["sld"], gv)
            for t in range(4):
                tof = t * 512
                def mkg(q, tof=tof):
                    for c in range(NCH):
                        ins = q.matmul(self.PS[6][0:8, :], GWT[:, c * 8:(c + 1) * 8],
                                       self.X[:, c * T + tof: c * T + tof + 512],
                                       start=(c == 0), stop=(c == NCH - 1))
                    return ins
                mv = p.ei("tensor", s["mmA"], mkg)
                p.wait("scalar", s["mmA"], mv)
                p.drain("scalar")
                scv = p.ei("scalar", s["scA"], lambda q: q.copy(GROW[0:8, :], self.PS[6][0:8, :]))
                p.wait("tensor", s["scA"], scv)
                for sc in range(4):
                    mv = p.ei("tensor", s["mmB"], lambda q, sc=sc: q.transpose(
                        self.PS[7][0:128, 0:8].bitcast(F32R), GROW[0:8, sc * 128:(sc + 1) * 128],
                        self.IDR[0:8, 0:8]))
                    p.wait("vector", s["mmB"], mv)
                    p.drain("vector")
                    p.ei("vector", s["vA"], lambda q: q.tensor_copy(
                        GTM[:, 0:8], self.PS[7][0:128, 0:8]))
                    p.drain("vector")
                    p.ei("vector", s["vA"], lambda q: q.reduce_max(
                        GTM[:, 32:33], GTM[:, 0:8], AX))
                    p.drain("vector")
                    p.ei("vector", s["vA"], lambda q: q.tensor_scalar(
                        GTM[:, 8:16], GTM[:, 0:8], GTM[:, 32:33].bitcast(F32), None, Al.subtract))
                    vv = s["vA"].v
                    p.wait("scalar", s["vA"], vv)
                    p.drain("scalar")
                    scv = p.ei("scalar", s["scA"], lambda q: q.activation(
                        GTM[:, 8:16], GTM[:, 8:16], AF.Exp))
                    p.wait("vector", s["scA"], scv)
                    p.drain("vector")
                    p.ei("vector", s["vA"], lambda q: q.reduce_sum(
                        GTM[:, 33:34], GTM[:, 8:16], AX))
                    p.drain("vector")
                    p.ei("vector", s["vA"], lambda q: q.reciprocal(
                        GTM[:, 34:35], GTM[:, 33:34]))
                    p.drain("vector")
                    p.ei("vector", s["vA"], lambda q: q.tensor_scalar(
                        GTM[:, 8:16], GTM[:, 8:16], GTM[:, 34:35].bitcast(F32), None, Al.mult))
                    p.drain("vector")
                    p.ei("vector", s["vA"], lambda q: q.tensor_tensor(
                        GTM[:, 0:8], GTM[:, 8:16],
                        self.CP[:, CPI["gbrow"]: CPI["gbrow"] + 8], Al.add))
                    p.drain("vector")
                    p.ei("vector", s["vA"], lambda q: q.reduce_max(
                        GTM[:, 32:33], GTM[:, 0:8], AX))
                    p.drain("vector")
                    p.ei("vector", s["vA"], lambda q: q.tensor_scalar(
                        GTM[:, 16:24], GTM[:, 0:8], GTM[:, 32:33].bitcast(F32), None, Al.is_equal))
                    p.drain("vector")
                    p.ei("vector", s["vA"], lambda q: q.scalar_tensor_tensor(
                        GTM[:, 16:24], GTM[:, 16:24], -1e30, GTM[:, 0:8], Al.mult, Al.add))
                    p.drain("vector")
                    p.ei("vector", s["vA"], lambda q: q.reduce_max(
                        GTM[:, 32:33], GTM[:, 16:24], AX))
                    p.drain("vector")
                    p.ei("vector", s["vA"], lambda q: q.tensor_scalar(
                        GTM[:, 16:24], GTM[:, 0:8], GTM[:, 32:33].bitcast(F32), None, Al.is_ge))
                    p.drain("vector")
                    vv = p.ei("vector", s["vA"], lambda q: q.tensor_tensor(
                        GTM[:, 24:32], GTM[:, 8:16], GTM[:, 16:24], Al.mult))
                    p.wait("tensor", s["vA"], vv)
                    mv = p.ei("tensor", s["mmB"], lambda q: q.transpose(
                        self.PS[7][0:8, 0:128].bitcast(F32R), GTM[:, 24:32].bitcast(F32R),
                        self.IDR[:, :]))
                    p.wait("vector", s["mmB"], mv)
                    p.drain("vector")
                    vv = p.ei("vector", s["vA"], lambda q, tof=tof, sc=sc: q.tensor_copy(
                        RW[0:8, tof + sc * 128: tof + (sc + 1) * 128], self.PS[7][0:8, 0:128]))
                    p.wait("tensor", s["vA"], vv)
                p.wait("tensor", s["scA"], s["scA"].v)

        mmA = s["mmA"]

        def w1_slice(e, mh):
            return self.dram["w1"][e * HCH + mh]

        def w3_slice(e, mh):
            return self.dram["w3"][e * HCH + mh]

        def w2_slice(e, md):
            return self.dram["w2"][e * 8 + md]

        mm0 = mmA.v
        a_marks = []
        b_marks = []
        ep_v = []
        ep_s = []
        bep_v = []
        rw_s = []
        rw_units = [(h, e) for h in range(2) for e in range(8)]
        rw_ptr = [0]

        def emit_rw(p):
            if rw_ptr[0] >= len(rw_units):
                return
            h_t, e_t = rw_units[rw_ptr[0]]
            eslot = rw_ptr[0] % 2
            rw_ptr[0] += 1
            tof_t = h_t * 1024
            if ep_v:
                p.wait("tensor", s["vA"], ep_v[-1])
            else:
                p.wait("tensor", s["vA"], s["vA"].v)
            for sub, bank in ((0, 6), (1, 7)):
                p.wait("scalar", s["mmB"], s["mmB"].v)
                rv = p.ei("scalar", s["srw"], lambda q, e_t=e_t, tof_t=tof_t, sub=sub:
                          q.dma_start(
                    out=RWROW[0:1, :],
                    in_=RW[e_t:e_t + 1, tof_t + sub * 512: tof_t + sub * 512 + 512]), 16)
                p.wait("tensor", s["srw"], rv)
                mv = p.ei("tensor", s["mmB"], lambda q, bank=bank: q.matmul(
                    self.PS[bank][:, :], ONEROWB[0:1, :],
                    RWROW[0:1, :], start=True, stop=True))
                p.wait("scalar", s["mmB"], mv)
                p.drain("scalar")
                scv = p.ei("scalar", s["scA"], lambda q, sub=sub, bank=bank, eslot=eslot:
                           q.copy(RWSB[:, eslot * 1024 + sub * 512:
                                       eslot * 1024 + sub * 512 + 512],
                                  self.PS[bank][:, :]))
                rw_s.append(scv)

        emit_rw(p)
        nrw = 0
        for half in range(2):
            tof = half * 1024
            for e in range(9):
                if e < 8:
                    cur_eslot = nrw % 2
                    nrw += 1
                for mh in range(HCH):
                    slot = len(a_marks) % 2
                    s1 = s["sld"] if slot == 0 else s["sldB"]
                    s3 = s["sld2"] if slot == 0 else s["sld2B"]
                    pace = a_marks[-2] if len(a_marks) >= 2 else mm0
                    p.wait("sync", mmA, pace)
                    lv1 = p.ei("sync", s1, lambda q, e=e, mh=mh, slot=slot: q.dma_start(
                        out=WAB[:, slot * 1024: slot * 1024 + 1024],
                        in_=w1_slice(e, mh)), 16)
                    lv3 = p.ei("sync", s3, lambda q, e=e, mh=mh, slot=slot: q.dma_start(
                        out=W3B[:, slot * 1024: slot * 1024 + 1024],
                        in_=w3_slice(e, mh)), 16)
                    p.wait("tensor", s1, lv1)
                    p.wait("tensor", s3, lv3)
                    bank_guard = max((ep_v[-2] if len(ep_v) >= 2 else 0),
                                     (bep_v[-1] if bep_v else 0))
                    p.wait("tensor", s["vA"], bank_guard)
                    if mh == 0 and rw_s:
                        p.wait("tensor", s["scA"], rw_s[-1])
                    pa0, pa1 = (0, 1) if slot == 0 else (2, 3)
                    pb0, pb1 = (4, 5) if slot == 0 else (6, 7)
                    def mka(q, slot=slot, tof=tof, pa0=pa0, pa1=pa1, pb0=pb0, pb1=pb1):
                        for k in range(NCH):
                            st, sp = (k == 0), (k == NCH - 1)
                            w1g = WAB[:, slot * 1024 + k * 128: slot * 1024 + (k + 1) * 128]
                            w3g = W3B[:, slot * 1024 + k * 128: slot * 1024 + (k + 1) * 128]
                            xs0 = self.X[:, k * T + tof: k * T + tof + 512]
                            xs1 = self.X[:, k * T + tof + 512: k * T + tof + 1024]
                            q.matmul(self.PS[pa0][:, :], w1g, xs0, start=st, stop=sp)
                            q.matmul(self.PS[pa1][:, :], w1g, xs1, start=st, stop=sp)
                            q.matmul(self.PS[pb0][:, :], w3g, xs0, start=st, stop=sp)
                            ins = q.matmul(self.PS[pb1][:, :], w3g, xs1, start=st, stop=sp)
                        return ins
                    amv = p.ei("tensor", mmA, mka)
                    a_marks.append(amv)
                    b1k, b3k = ("sb1", "sb3") if e == 8 else ("b1", "b3")
                    b1j = mh if e == 8 else e * HCH + mh
                    p.wait("vector", mmA, amv)
                    if mh == 0 and b_marks:
                        p.wait("scalar", mmA, b_marks[-1])
                    for sub, pa, pb in ((0, pa0, pb0), (1, pa1, pb1)):
                        p.drain("vector")
                        p.ei("vector", s["vA"], lambda q, pb=pb, b3k=b3k, b1j=b1j: q.tensor_scalar(
                            VS1[:, :], self.PS[pb][:, :], self.cpc(b3k, b1j), None, Al.add))
                        p.drain("vector")
                        p.wait("vector", s["scA"])
                        vv = p.ei("vector", s["vA"], lambda q, pa=pa, b1k=b1k, b1j=b1j: q.scalar_tensor_tensor(
                            VS2[:, :], self.PS[pa][:, :], self.cpc(b1k, b1j), VS1[:, :],
                            Al.add, Al.mult))
                        p.wait("scalar", s["vA"], vv)
                        p.drain("scalar")
                        scv = p.ei("scalar", s["scA"], lambda q, mh=mh, sub=sub: q.activation(
                            HB[:, mh * 1024 + sub * 512: mh * 1024 + sub * 512 + 512],
                            VS2[:, :], AF.Silu))
                    ep_v.append(vv)
                    ep_s.append(scv)
                for md in range(NCH):
                    mslot = len(b_marks) % 2
                    sg = s["gld"] if mslot == 0 else s["gldB"]
                    p.wait("gpsimd", mmA, b_marks[-2] if len(b_marks) >= 2 else mm0)
                    lv2 = p.ei("gpsimd", sg, lambda q, e=e, md=md, mslot=mslot: q.dma_start(
                        out=W2B[:, mslot * 1408: mslot * 1408 + 1408],
                        in_=w2_slice(e, md)), 16)
                    p.wait("tensor", sg, lv2)
                    if md == 0:
                        p.wait("tensor", s["scA"], ep_s[-1])
                    p.wait("tensor", s["vA"], max(
                        ep_v[-1], (bep_v[-2] if len(bep_v) >= 2 else 0)))
                    pc0, pc1 = (0, 1) if mslot == 0 else (2, 3)
                    def mkb2(q, pc0=pc0, pc1=pc1, mslot=mslot):
                        for kh in range(HCH):
                            st, sp = (kh == 0), (kh == HCH - 1)
                            w2g = W2B[:, mslot * 1408 + kh * 128: mslot * 1408 + (kh + 1) * 128]
                            q.matmul(self.PS[pc0][:, :], w2g,
                                     HB[:, kh * 1024: kh * 1024 + 512], start=st, stop=sp)
                            ins = q.matmul(self.PS[pc1][:, :], w2g,
                                           HB[:, kh * 1024 + 512: kh * 1024 + 1024], start=st, stop=sp)
                        return ins
                    bmv = p.ei("tensor", mmA, mkb2)
                    b_marks.append(bmv)
                    if md == 3 and e < 8:
                        emit_rw(p)
                    b2k = "sb2" if e == 8 else "b2"
                    b2j = md if e == 8 else e * NCH + md
                    p.wait("vector", mmA, bmv)
                    for sub, pc in ((0, pc0), (1, pc1)):
                        p.drain("vector")
                        if e < 8:
                            p.ei("vector", s["vA"], lambda q, pc=pc, b2k=b2k, b2j=b2j, sub=sub, cur_eslot=cur_eslot: q.scalar_tensor_tensor(
                                VS1[:, :], self.PS[pc][:, :], self.cpc(b2k, b2j),
                                RWSB[:, cur_eslot * 1024 + sub * 512:
                                     cur_eslot * 1024 + sub * 512 + 512],
                                Al.add, Al.mult))
                        else:
                            p.ei("vector", s["vA"], lambda q, pc=pc, b2k=b2k, b2j=b2j: q.tensor_scalar(
                                VS1[:, :], self.PS[pc][:, :], self.cpc(b2k, b2j), None, Al.add))
                        p.drain("vector")
                        vv = p.ei("vector", s["vA"], lambda q, md=md, tof=tof, sub=sub: q.tensor_tensor(
                            self.ACC[:, md * T + tof + sub * 512: md * T + tof + sub * 512 + 512],
                            self.ACC[:, md * T + tof + sub * 512: md * T + tof + sub * 512 + 512],
                            VS1[:, :], Al.add))
                    bep_v.append(vv)
        p.run()

        p = Prog(nc)
        self.ln_stats2(p, self.ACC, self.ROWA, self.ROWB, sq=self.X)
        self.ln_rows_finalize(p, self.ROWA, self.ROWB)
        self.ln_norm(p, self.ACC, self.X, self.ROWB, self.ROWA)
        p.run()

    def phase_gla_d0(self):
        nc, s = self.nc, self.sem
        gctx = ExitStack()
        self.gla_ctx = gctx
        self.HBF = gctx.enter_context(nc.sbuf_tensor("HBF", [128, NCH * T], BF16))
        self.RSTg = gctx.enter_context(nc.sbuf_tensor("RSTg", [128, NCK * 4], F32))
        self.ACSTg = gctx.enter_context(nc.sbuf_tensor("ACSTg", [128, NCK * 4], F32))
        self.FL2 = gctx.enter_context(nc.sbuf_tensor("FL2", [128, 1], F32))
        self.SCUR = gctx.enter_context(nc.sbuf_tensor("SCUR", [128, 1024], F32))

        p = Prog(nc)
        ROWA = self.sbuf_local("ROWg1", [1, T], F32R)
        ROWB = self.sbuf_local("ROWg2", [1, T], F32R)
        self.ln_stats2(p, self.X, ROWA, ROWB)
        self.ln_rows_finalize(p, ROWA, ROWB)
        self.ln_norm(p, self.X, self.HBF, ROWB, ROWA, affine=("lnw", "lnb"))
        p.drain("vector")
        p.ei("vector", s["vA"], lambda q: q.tensor_scalar(
            self.FL2[:, :], self.cpc("flag"), -1.0, 1.0, Al.mult, Al.add))
        p.drain("vector")
        p.ei("vector", s["vA"], lambda q: q.memset(self.SCUR[:, :], 0.0))
        for i in range(4):
            p.ei("scalar", s["sst"], lambda q, i=i: q.dma_start(
                out=self.x2_spill[:, i * 4096:(i + 1) * 4096],
                in_=self.X[:, i * 4096:(i + 1) * 4096]), 16)
        p.run()

    def phase_gla_main(self):
        nc, s = self.nc, self.sem
        HBF, SCUR = self.HBF, self.SCUR
        p = Prog(nc)
        UC = self.sbuf_local("UC", [17, 128], BF16)
        GKW1B = self.sbuf_local("GKW1B", [128, 8 * 16], BF16)
        GKW2B = self.sbuf_local("GKW2B", [17, 512], BF16)
        ACCB = self.ACCB
        WQ0, WK0, WV0 = 16384, 20480, 24576
        GTMc = self.sbuf_local("GTMc", [128, 512], BF16)
        FAC = self.sbuf_local("FACq", [128, 512], BF16)
        FACK = self.sbuf_local("FACk", [128, 512], BF16)
        KF = self.sbuf_local("KF", [128, 512], BF16)
        KT = self.sbuf_local("KT", [128, 512], BF16)
        VT = self.sbuf_local("VT", [128, 1024], BF16)
        PSB = self.sbuf_local("PSB", [128, 512], BF16)
        SOP = self.sbuf_local("SOPt", [128, 256], BF16)
        QSTc = self.sbuf_local("QSTc", [128, 512], BF16)
        AC = self.sbuf_local("ACc", [128, 20], F32)
        RACC = self.sbuf_local("RACC", [128, 4], F32)

        mm0 = s["mmA"].v
        p.wait("gpsimd", s["mmA"], mm0)
        p.ei("gpsimd", s["gq"], lambda q: q.dma_start(
            out=GKW1B[:, :], in_=self.dram["gkw1"]), 16)
        gk_ready = p.ei("gpsimd", s["gq"], lambda q: q.dma_start(
            out=GKW2B[:, :], in_=self.dram["gkw2e"]), 16)
        p.drain("vector")
        p.ei("vector", s["vA"], lambda q: q.memset(RACC[:, :], 0.0))
        p.wait("gpsimd", s["mmA"], mm0)
        uc1 = p.ei("gpsimd", s["guc"], lambda q: q.dma_start(
            out=UC[16:17, :], in_=self.ONEROW[0:1, :]), 16)
        for g in range(4):
            wq_rdy = p.ei("gpsimd", s["gqB"], lambda q, g=g: q.dma_start(
                out=ACCB[:, WQ0 + g * 1024: WQ0 + (g + 1) * 1024],
                in_=self.dram["Wq"][g]), 16)
            wk_rdy = p.ei("gpsimd", s["gk2"], lambda q, g=g: q.dma_start(
                out=ACCB[:, WK0 + g * 1024: WK0 + (g + 1) * 1024],
                in_=self.dram["Wk"][g]), 16)
        for k in range(NCH):
            for n in range(2):
                wv_rdy = p.ei("gpsimd", s["gld"], lambda q, k=k, n=n: q.dma_start(
                    out=ACCB[:, WV0 + (k * 2 + n) * 512: WV0 + (k * 2 + n + 1) * 512],
                    in_=self.dram["Wv"][k, n]), 16)

        wq_marks = []
        vt_marks = []
        v_kmarks = []
        for c in range(NCK):
            p.wait("tensor", s["gq"], gk_ready)
            p.wait("tensor", s["vA"])
            p.wait("tensor", s["vB"])
            def mku(q, c=c):
                for k in range(NCH):
                    ins = q.matmul(self.PS[0][0:16, 0:128], GKW1B[:, k * 16:(k + 1) * 16],
                                   HBF[:, k * T + c * 128: k * T + (c + 1) * 128],
                                   start=(k == 0), stop=(k == NCH - 1))
                return ins
            mv = p.ei("tensor", s["mmA"], mku)
            p.wait("scalar", s["mmA"], mv)
            p.drain("scalar")
            scv = p.ei("scalar", s["scA"], lambda q: q.copy(UC[0:16, :], self.PS[0][0:16, 0:128]))
            p.wait("tensor", s["scA"], scv)
            p.wait("tensor", s["guc"], uc1)
            mv = p.ei("tensor", s["mmA"], lambda q: q.matmul(
                self.PS[1][:, :], UC[:, :], GKW2B[:, :], start=True, stop=True))
            p.wait("scalar", s["mmA"], mv)
            p.wait("scalar", s["vA"])
            p.drain("scalar")
            p.ei("scalar", s["scA"], lambda q: q.activation(
                self.XSQ[:, :], self.PS[1][:, :], AF.Abs))
            p.drain("scalar")
            p.ei("scalar", s["scA"], lambda q: q.activation(
                self.NTMP[:, :], self.XSQ[:, :], AF.Exp, scale=-1.0))
            p.drain("scalar")
            p.ei("scalar", s["scA"], lambda q: q.activation(
                self.NTMP[:, :], self.NTMP[:, :], AF.Ln, bias=1.0))
            p.drain("scalar")
            scv = p.ei("scalar", s["scA"], lambda q: q.activation(
                self.XSQ[:, :], self.PS[1][:, :], AF.Relu, scale=-1.0))
            p.wait("vector", s["scA"], scv)
            p.drain("vector")
            vv = p.ei("vector", s["vA"], lambda q: q.tensor_tensor(
                GTMc[:, :], self.XSQ[:, :], self.NTMP[:, :], Al.add))
            p.wait("tensor", s["vA"], vv)
            def mkcs(q):
                for g in range(4):
                    ins = q.matmul(self.PS[2][:, g * 128:(g + 1) * 128],
                                   GTMc[:, g * 128:(g + 1) * 128], self.MASKB[:, :],
                                   start=True, stop=True)
                return ins
            mv_cs = p.ei("tensor", s["mmA"], mkcs)
            p.wait("vector", s["mmA"], mv_cs)
            p.drain("vector")
            p.ei("vector", s["vA"], lambda q: q.tensor_copy(
                AC[:, 0:4], self.PS[2][:, 127::128]))
            p.drain("vector")
            p.ei("vector", s["vA"], lambda q, c=c: q.tensor_copy(
                self.ACSTg[:, c * 4:(c + 1) * 4], AC[:, 0:4]))
            p.drain("vector")
            p.ei("vector", s["vA"], lambda q, c=c: q.tensor_copy(
                self.RSTg[:, c * 4:(c + 1) * 4], RACC[:, :]))
            p.drain("vector")
            p.ei("vector", s["vA"], lambda q: q.tensor_tensor(
                RACC[:, :], RACC[:, :], AC[:, 0:4], Al.add))
            p.drain("vector")
            p.ei("vector", s["vA"], lambda q: q.tensor_scalar(
                AC[:, 12:16], AC[:, 0:4], 1.0 / 32.0, None, Al.mult))
            p.drain("vector")
            vv = p.ei("vector", s["vA"], lambda q: q.tensor_scalar(
                AC[:, 16:20], AC[:, 0:4], -1.0 / 32.0, LN_DKS, Al.mult, Al.add))
            p.wait("scalar", s["vA"], vv)
            p.drain("scalar")
            p.ei("scalar", s["scA"], lambda q: q.activation(
                AC[:, 4:8], AC[:, 0:4], AF.Exp, scale=-1.0 / 32.0))
            p.drain("scalar")
            p.ei("scalar", s["scA"], lambda q: q.activation(
                AC[:, 8:12], AC[:, 0:4], AF.Exp, scale=-1.0 / 16.0))
            for g in range(4):
                p.drain("scalar")
                p.ei("scalar", s["scA"], lambda q, g=g: q.activation(
                    FAC[:, g * 128:(g + 1) * 128], self.PS[2][:, g * 128:(g + 1) * 128],
                    AF.Exp, scale=-1.0 / 16.0, bias=AC[:, 12 + g:13 + g]))
                p.drain("scalar")
                scv = p.ei("scalar", s["scA"], lambda q, g=g: q.activation(
                    FACK[:, g * 128:(g + 1) * 128], self.PS[2][:, g * 128:(g + 1) * 128],
                    AF.Exp, scale=1.0 / 16.0, bias=AC[:, 16 + g:17 + g]))
            fac_done = scv
            p.wait("vector", s["scA"], fac_done)
            p.wait("vector", s["srw"])
            if c == 0:
                p.wait("tensor", s["gqB"], wq_rdy)
                p.wait("tensor", s["gk2"], wk_rdy)
            qk_marks = []
            for g in range(4):
                qb = 3 if g % 2 == 0 else 4
                kb = 0 if g % 2 == 0 else 6
                if g == 0:
                    p.wait("tensor", s["vA"])
                elif g >= 2:
                    p.wait("tensor", s["vA"], qk_marks[g - 2])
                def mkq(q, c=c, g=g, qb=qb):
                    for k in range(NCH):
                        ins = q.matmul(self.PS[qb][:, 0:128],
                                       ACCB[:, WQ0 + g * 1024 + k * 128: WQ0 + g * 1024 + (k + 1) * 128],
                                       HBF[:, k * T + c * 128: k * T + (c + 1) * 128],
                                       start=(k == 0), stop=(k == NCH - 1))
                    return ins
                mvq = p.ei("tensor", s["mmA"], mkq)
                p.wait("vector", s["mmA"], mvq)
                p.drain("vector")
                p.ei("vector", s["vA"], lambda q, g=g, qb=qb: q.tensor_tensor(
                    QSTc[:, g * 128:(g + 1) * 128], self.PS[qb][:, 0:128],
                    FAC[:, g * 128:(g + 1) * 128], Al.mult))
                def mkk(q, c=c, g=g, kb=kb):
                    for k in range(NCH):
                        ins = q.matmul(self.PS[kb][:, 0:128],
                                       ACCB[:, WK0 + g * 1024 + k * 128: WK0 + g * 1024 + (k + 1) * 128],
                                       HBF[:, k * T + c * 128: k * T + (c + 1) * 128],
                                       start=(k == 0), stop=(k == NCH - 1))
                    return ins
                mvk = p.ei("tensor", s["mmA"], mkk)
                wq_marks.append(mvk)
                p.wait("vector", s["mmA"], mvk)
                p.drain("vector")
                qk_marks.append(p.ei("vector", s["vA"], lambda q, g=g, kb=kb: q.tensor_tensor(
                    KF[:, g * 128:(g + 1) * 128], self.PS[kb][:, 0:128],
                    FACK[:, g * 128:(g + 1) * 128], Al.mult)))
            vq = s["vA"].v
            p.wait("sync", s["vA"], vq)
            p.ei("sync", s["srw"], lambda q, c=c: q.dma_start(
                out=self.qst_d[:, c * 512:(c + 1) * 512], in_=QSTc[:, :]), 16)
            if c == 0:
                p.wait("tensor", s["gld"], wv_rdy)
            for n in range(2):
                vb = 5 if n == 0 else 7
                if len(vt_marks) >= 2:
                    p.wait("tensor", s["scA"], vt_marks[-2])
                def mkv(q, c=c, n=n, vb=vb):
                    for k in range(NCH):
                        ins = q.matmul(self.PS[vb][:, :],
                                       HBF[:, k * T + c * 128: k * T + (c + 1) * 128],
                                       ACCB[:, WV0 + (k * 2 + n) * 512: WV0 + (k * 2 + n + 1) * 512],
                                       start=(k == 0), stop=(k == NCH - 1))
                    return ins
                mvv = p.ei("tensor", s["mmA"], mkv)
                v_kmarks.append(mvv)
                p.wait("scalar", s["mmA"], v_kmarks[-1])
                p.wait("scalar", s["mmB"])
                p.drain("scalar")
                vt_marks.append(p.ei("scalar", s["scA"], lambda q, n=n, vb=vb: q.copy(
                    VT[:, n * 512:(n + 1) * 512], self.PS[vb][:, :])))
            vk = s["vA"].v
            p.wait("tensor", s["vA"], vk)
            for g in range(4):
                bk2 = 3 if g % 2 == 0 else 0
                p.wait("tensor", s["vB"], s["vB"].v - 1)
                mvt = p.ei("tensor", s["mmB"], lambda q, g=g, bk2=bk2: q.transpose(
                    self.PS[bk2][:, :].bitcast(BF16)[:, 0:128], KF[:, g * 128:(g + 1) * 128],
                    self.IDB[:, :]))
                p.wait("vector", s["mmB"], mvt)
                p.drain("vector")
                p.ei("vector", s["vB"], lambda q, g=g, bk2=bk2: q.tensor_copy(
                    KT[:, g * 128:(g + 1) * 128], self.PS[bk2][:, :].bitcast(BF16)[:, 0:128]))
            scv_v = s["scA"].v
            p.wait("tensor", s["scA"], scv_v)
            p.wait("tensor", s["vB"])
            def mkU(q):
                for g in range(4):
                    bank, half = (1, 0) if g == 0 else (1, 1) if g == 1 else (2, 0) if g == 2 else (2, 1)
                    ins = q.matmul(self.PS[bank][:, half * 256: half * 256 + 256],
                                   KT[:, g * 128:(g + 1) * 128],
                                   VT[:, g * 256:(g + 1) * 256], start=True, stop=True)
                return ins
            mvU = p.ei("tensor", s["mmA"], mkU)
            p.wait("vector", s["mmA"], mvU)
            for g in range(4):
                bank, half = (1, 0) if g == 0 else (1, 1) if g == 1 else (2, 0) if g == 2 else (2, 1)
                p.drain("vector")
                p.ei("vector", s["vA"], lambda q, g=g: q.tensor_scalar(
                    SOP[:, :], SCUR[:, g * 256:(g + 1) * 256], AC[:, 4 + g:5 + g],
                    None, Al.mult))
                p.drain("vector")
                p.ei("vector", s["vA"], lambda q, g=g, bank=bank, half=half: q.tensor_scalar(
                    self.NTMP[:, 0:256], self.PS[bank][:, half * 256: half * 256 + 256],
                    AC[:, 4 + g:5 + g], None, Al.mult))
                p.drain("vector")
                p.ei("vector", s["vA"], lambda q, g=g: q.scalar_tensor_tensor(
                    SCUR[:, g * 256:(g + 1) * 256], SCUR[:, g * 256:(g + 1) * 256],
                    AC[:, 8 + g:9 + g], self.NTMP[:, 0:256], Al.mult, Al.add))
                vv = s["vA"].v
                p.wait("tensor", s["vA"], vv)
                def mkP(q, g=g):
                    for hh in range(4):
                        tp = (96, 0) if hh == 3 else None
                        ins = q.matmul(self.PS[4 + hh][:, 0:128],
                                       KF[32 * hh:32 * hh + 32, g * 128:(g + 1) * 128],
                                       QSTc[32 * hh:32 * hh + 32, g * 128:(g + 1) * 128],
                                       start=True, stop=True, tile_position=tp)
                    return ins
                mvP = p.ei("tensor", s["mmB"], mkP)
                p.wait("vector", s["mmB"], mvP)
                for hh in range(4):
                    p.drain("vector")
                    p.ei("vector", s["vA"], lambda q, hh=hh: q.tensor_tensor(
                        PSB[:, hh * 128:(hh + 1) * 128], self.PS[4 + hh][:, 0:128],
                        self.MASKB[:, :], Al.mult))
                pv = s["vA"].v
                p.wait("tensor", s["vA"], pv)
                def mkO(q, g=g):
                    for hh in range(4):
                        tp = (96, 0) if hh == 3 else None
                        q.matmul(self.PS[4 + hh][:, 128:192], PSB[:, hh * 128:(hh + 1) * 128],
                                 VT[:, g * 256 + hh * 64: g * 256 + (hh + 1) * 64],
                                 start=True, stop=False)
                        ins = q.matmul(self.PS[4 + hh][:, 128:192],
                                       QSTc[32 * hh:32 * hh + 32, g * 128:(g + 1) * 128],
                                       SOP[32 * hh:32 * hh + 32, hh * 64:(hh + 1) * 64],
                                       start=False, stop=True, tile_position=tp)
                    return ins
                mvO = p.ei("tensor", s["mmB"], mkO)
                p.wait("vector", s["mmB"], mvO)
                for hh in range(4):
                    p.drain("vector")
                    p.ei("vector", s["vA"], lambda q, g=g, hh=hh, c=c: q.tensor_copy(
                        ACCB[:, c * 1024 + (g * 4 + hh) * 64: c * 1024 + (g * 4 + hh) * 64 + 64],
                        self.PS[4 + hh][:, 128:192]))
        p.drain("vector")
        p.ei("vector", s["vA"], lambda q: q.tensor_scalar(
            self.XSQ[:, :], SCUR[:, 0:512], self.FL2[:, 0:1], None, Al.mult))
        p.drain("vector")
        vv = p.ei("vector", s["vA"], lambda q: q.tensor_scalar(
            self.NTMP[:, :], SCUR[:, 512:1024], self.FL2[:, 0:1], None, Al.mult))
        p.wait("gpsimd", s["vA"], vv)
        p.ei("gpsimd", s["gld"], lambda q: q.dma_start(
            out=self.cc_in[:, 0:512], in_=self.XSQ[:, :]), 16)
        p.wait("gpsimd", s["gld"])
        p.ei("gpsimd", s["gld"], lambda q: q.dma_start(
            out=self.cc_in[:, 512:1024], in_=self.NTMP[:, :]), 16)
        p.wait("gpsimd", s["gld"])
        p.ei("gpsimd", s["cc"], lambda q: q.collective_compute(
            "AllReduce", Al.add,
            replica_groups=[[0, 1], [2, 3], [4, 5], [6, 7]],
            ins=[self.cc_in], outs=[self.cc_out]))
        p.run()

    def phase_gla_post(self):
        nc, s = self.nc, self.sem
        HBF, SCUR = self.HBF, self.SCUR
        p = Prog(nc)
        SRECV = self.sbuf_local("SRECV", [128, 1024], BF16)
        SCO = self.sbuf_local("SCO", [128, 1024], BF16)
        RMSWB = self.sbuf_local("RMSWBl", [128, DV], BF16)
        GSW = self.sbuf_local("GSW", [128, 1024], BF16)
        OF = self.sbuf_local("OF", [128, 1024], F32R)
        OGT = self.sbuf_local("OGT", [128, 1024], BF16)
        QSTc = self.sbuf_local("QSTc2", [128, 512], BF16)
        RSTD = self.sbuf_local("RSTD", [128, 16], F32)
        ECC = self.sbuf_local("ECC", [128, 12], F32)
        ACCB = self.ACCB
        WG0, WO0 = 16384, 24576

        mm0 = s["mmA"].v
        p.wait("gpsimd", s["mmA"], mm0)
        rmw = p.ei("gpsimd", s["guc"], lambda q: q.dma_start(
            out=RMSWB[:, :], in_=self.dram["rms_wb"]), 16)
        p.wait("vector", s["guc"], rmw)
        for k in range(NCH):
            for n in range(2):
                wg_rdy = p.ei("gpsimd", s["gld"], lambda q, k=k, n=n: q.dma_start(
                    out=ACCB[:, WG0 + (k * 2 + n) * 512: WG0 + (k * 2 + n + 1) * 512],
                    in_=self.dram["Wg"][k, n]), 16)
        for md in range(NCH):
            wo_rdy = p.ei("gpsimd", s["gldB"], lambda q, md=md: q.dma_start(
                out=ACCB[:, WO0 + md * 1024: WO0 + (md + 1) * 1024],
                in_=self.dram["Wo"][md]), 16)
        p.wait("gpsimd", s["cc"], s["cc"].v)
        rv = p.ei("gpsimd", s["gq"], lambda q: q.dma_start(
            out=SRECV[:, :], in_=self.cc_out), 16)
        p.wait("vector", s["gq"], rv)
        p.drain("vector")
        srecv_mark = p.ei("vector", s["vA"], lambda q: q.tensor_scalar(
            SRECV[:, :], SRECV[:, :], self.cpc("flag"), None, Al.mult))
        p.wait("gpsimd", s["vA"], srecv_mark)
        p.wait("vector", s["sst"])

        wg_marks = []
        wo_marks = []
        wo_eps = []
        corr_marks = []
        for c in range(NCK):
            p.wait("sync", s["mmA"], corr_marks[-1] if corr_marks else mm0)
            if not corr_marks:
                p.wait("sync", s["srw"])
            lq = p.ei("sync", s["srw"], lambda q, c=c: q.dma_start(
                out=QSTc[:, :], in_=self.qst_d[:, c * 512:(c + 1) * 512]), 16)
            p.drain("scalar")
            p.wait("scalar", s["vA"])
            p.ei("scalar", s["scA"], lambda q, c=c: q.activation(
                ECC[:, 0:4], self.ACSTg[:, c * 4:(c + 1) * 4], AF.Exp, scale=-1.0 / 32.0))
            p.drain("scalar")
            scv = p.ei("scalar", s["scA"], lambda q, c=c: q.activation(
                ECC[:, 4:8], self.RSTg[:, c * 4:(c + 1) * 4], AF.Exp, scale=-1.0 / 16.0))
            p.wait("vector", s["scA"], scv)
            p.wait("vector", s["mmA"], corr_marks[-1] if corr_marks else mm0)
            p.drain("vector")
            p.ei("vector", s["vA"], lambda q: q.tensor_tensor(
                ECC[:, 8:12], ECC[:, 0:4], ECC[:, 4:8], Al.mult))
            for g in range(4):
                p.drain("vector")
                p.ei("vector", s["vA"], lambda q, g=g: q.tensor_scalar(
                    SCO[:, g * 256:(g + 1) * 256], SRECV[:, g * 256:(g + 1) * 256],
                    ECC[:, 8 + g:9 + g], None, Al.mult))
            vv = s["vA"].v
            p.wait("tensor", s["vA"], vv)
            p.wait("tensor", s["srw"], lq)
            def mkcorr(q):
                for g in range(4):
                    for hh in range(4):
                        tp = (96, 0) if hh == 3 else None
                        ins = q.matmul(self.PS[4 + hh][:, g * 64:(g + 1) * 64],
                                       QSTc[32 * hh:32 * hh + 32, g * 128:(g + 1) * 128],
                                       SCO[32 * hh:32 * hh + 32, g * 256 + hh * 64: g * 256 + (hh + 1) * 64],
                                       start=True, stop=True, tile_position=tp)
                return ins
            mvc = p.ei("tensor", s["mmA"], mkcorr)
            corr_marks.append(mvc)
            p.wait("vector", s["mmA"], mvc)
            p.wait("vector", s["mmB"])
            for g in range(4):
                for hh in range(4):
                    p.drain("vector")
                    p.ei("vector", s["vA"], lambda q, g=g, hh=hh, c=c: q.tensor_tensor(
                        OF[:, (g * 4 + hh) * 64:(g * 4 + hh) * 64 + 64],
                        ACCB[:, c * 1024 + (g * 4 + hh) * 64: c * 1024 + (g * 4 + hh) * 64 + 64],
                        self.PS[4 + hh][:, g * 64:(g + 1) * 64], Al.add))
            vv = s["vA"].v
            p.wait("scalar", s["vA"], vv)
            p.wait("scalar", s["vA"])
            p.drain("scalar")
            p.ei("scalar", s["scA"], lambda q: q.activation(
                self.XSQ[:, :], OF[:, 0:512], AF.Square))
            p.drain("scalar")
            scv = p.ei("scalar", s["scA"], lambda q: q.activation(
                self.NTMP[:, :], OF[:, 512:1024], AF.Square))
            p.wait("vector", s["scA"], scv)
            p.drain("vector")
            p.ei("vector", s["vA"], lambda q: q.tensor_reduce(
                RSTD[:, 0:8], self.XSQ[:, :].rearrange(
                    "p (h v) -> p h v", h=8), AX, Al.add))
            p.drain("vector")
            p.ei("vector", s["vA"], lambda q: q.tensor_reduce(
                RSTD[:, 8:16], self.NTMP[:, :].rearrange(
                    "p (h v) -> p h v", h=8), AX, Al.add))
            p.drain("vector")
            p.ei("vector", s["vA"], lambda q: q.tensor_scalar(
                RSTD[:, :], RSTD[:, :], 1.0 / DV, EPS, Al.mult, Al.add))
            p.drain("vector")
            vv = p.ei("vector", s["vA"], lambda q: q.reciprocal(RSTD[:, :], RSTD[:, :]))
            p.wait("scalar", s["vA"], vv)
            p.drain("scalar")
            sc_rstd = p.ei("scalar", s["scA"], lambda q: q.activation(
                RSTD[:, :], RSTD[:, :], AF.Sqrt))
            if c == 0:
                p.wait("tensor", s["gld"], wg_rdy)
            for n in range(2):
                p.wait("tensor", s["scA"])
                p.wait("tensor", s["vA"])
                def mkg2(q, n=n, c=c):
                    for k in range(NCH):
                        ins = q.matmul(self.PS[n][:, :],
                                       HBF[:, k * T + c * 128: k * T + (c + 1) * 128],
                                       ACCB[:, WG0 + (k * 2 + n) * 512: WG0 + (k * 2 + n + 1) * 512],
                                       start=(k == 0), stop=(k == NCH - 1))
                    return ins
                mvg = p.ei("tensor", s["mmA"], mkg2)
                wg_marks.append(mvg)
                p.wait("scalar", s["mmA"], wg_marks[-1])
                p.drain("scalar")
                p.wait("scalar", s["vA"])
                scv = p.ei("scalar", s["scA"], lambda q, n=n: q.activation(
                    self.XSQ[:, :], self.PS[n][:, :], AF.Silu))
                p.wait("vector", s["scA"], scv)
                p.drain("vector")
                p.ei("vector", s["vA"], lambda q, n=n: q.tensor_tensor(
                    GSW[:, n * 512:(n + 1) * 512].rearrange("p (h v) -> p h v", h=8),
                    self.XSQ[:, :].rearrange("p (h v) -> p h v", h=8),
                    RMSWB[:, :].rearrange("p (o v) -> p o v", o=1).to_broadcast(
                        (128, 8, 64)), Al.mult))
            p.wait("vector", s["scA"], sc_rstd)
            p.drain("vector")
            p.ei("vector", s["vA"], lambda q: q.tensor_tensor(
                OF[:, :].rearrange("p (h v) -> p h v", h=16),
                OF[:, :].rearrange("p (h v) -> p h v", h=16),
                RSTD[:, :].rearrange("p (h o) -> p h o", o=1).to_broadcast((128, 16, 64)),
                Al.mult))
            p.drain("vector")
            vv = p.ei("vector", s["vA"], lambda q: q.tensor_tensor(
                OF[:, :], OF[:, :], GSW[:, :], Al.mult))
            p.wait("tensor", s["vA"], vv)
            p.wait("vector", s["mmA"])
            for t8 in range(8):
                bk2 = 2 + (t8 % 2)
                p.wait("tensor", s["vB"], s["vB"].v - 1)
                mvt = p.ei("tensor", s["mmB"], lambda q, t8=t8, bk2=bk2: q.transpose(
                    self.PS[bk2][:, 0:128].bitcast(F32R), OF[:, t8 * 128:(t8 + 1) * 128],
                    self.IDR[:, :]))
                p.wait("vector", s["mmB"], mvt)
                p.drain("vector")
                p.ei("vector", s["vB"], lambda q, t8=t8, bk2=bk2: q.tensor_copy(
                    OGT[:, t8 * 128:(t8 + 1) * 128], self.PS[bk2][:, 0:128]))
            p.wait("tensor", s["vB"])
            p.wait("tensor", s["scA"])
            if not wo_marks:
                p.wait("tensor", s["gldB"], wo_rdy)
            for wave in range(2):
                for mdl in range(4):
                    md = wave * 4 + mdl
                    if len(wo_eps) >= 4:
                        p.wait("tensor", s["vA"], wo_eps[-4])
                    def mkwo(q, md=md, mdl=mdl):
                        for k in range(NCH):
                            ins = q.matmul(self.PS[mdl][:, 0:128],
                                           ACCB[:, WO0 + md * 1024 + k * 128: WO0 + md * 1024 + (k + 1) * 128],
                                           OGT[:, k * 128:(k + 1) * 128],
                                           start=(k == 0), stop=(k == NCH - 1))
                        return ins
                    mvw = p.ei("tensor", s["mmA"], mkwo)
                    wo_marks.append(mvw)
                    p.wait("vector", s["mmA"], mvw)
                    p.drain("vector")
                    vv = p.ei("vector", s["vA"], lambda q, md=md, c=c, mdl=mdl: q.tensor_tensor(
                        self.X[:, md * T + c * 128: md * T + (c + 1) * 128],
                        self.X[:, md * T + c * 128: md * T + (c + 1) * 128],
                        self.PS[mdl][:, 0:128], Al.add))
                    wo_eps.append(vv)
        p.run()

    def phase_mlp(self):
        nc, s = self.nc, self.sem
        p = Prog(nc)
        ROWA = self.sbuf_local("ROWm1", [1, T], F32R)
        ROWB = self.sbuf_local("ROWm2", [1, T], F32R)
        W1B = self.sbuf_local("W1B", [128, 2 * 1024], F32R)
        W2B2 = self.sbuf_local("W2B2", [128, 2 * 2048], F32R)
        X2R = self.sbuf_local("X2R", [128, 4096], F32R)
        GEL = self.ACC[:, 0:8192]
        H2 = self.ACC[:, 8192:12288]

        self.ln_stats2(p, self.X, ROWA, ROWB)
        self.ln_rows_finalize(p, ROWA, ROWB)
        mm0 = s["mmA"].v
        w1_marks = []
        w2_marks = []
        gel_marks = []
        fin_marks = []
        for tt in range(4):
            tof = tt * 512
            p.wait("tensor", s["vA"])
            def mkb(q, tof=tof, ROWA=ROWA, ROWB=ROWB):
                q.matmul(self.PS[4][:, :], self.ONEROW[0:1, :],
                         ROWB[0:1, tof: tof + 512], start=True, stop=True)
                return q.matmul(self.PS[5][:, :], self.ONEROW[0:1, :],
                                ROWA[0:1, tof: tof + 512], start=True, stop=True)
            mv = p.ei("tensor", s["mmA"], mkb)
            p.wait("vector", s["mmA"], mv)
            for cch in range(NCH):
                p.drain("vector")
                p.ei("vector", s["vA"], lambda q, cch=cch, tof=tof: q.tensor_tensor(
                    self.NTMP[:, :], self.X[:, cch * T + tof: cch * T + tof + 512],
                    self.PS[4][:, :], Al.mult))
                p.drain("vector")
                p.ei("vector", s["vA"], lambda q: q.tensor_tensor(
                    self.NTMP[:, :], self.NTMP[:, :], self.PS[5][:, :], Al.add))
                p.drain("vector")
                p.ei("vector", s["vA"], lambda q, cch=cch: q.tensor_scalar(
                    H2[:, cch * 512:(cch + 1) * 512], self.NTMP[:, :],
                    self.cpc("lnw", cch), self.cpc("lnb", cch), Al.mult, Al.add))
            hv = s["vA"].v
            p.wait("tensor", s["vA"], hv)
            p.wait("sync", s["srw"])
            if fin_marks:
                p.wait("sync", s["vA"], fin_marks[-1])
            x2v = p.ei("sync", s["srw"], lambda q, tt=tt: q.dma_start(
                out=X2R[:, :].rearrange("p (c t) -> p c t", c=8),
                in_=self.x2_spill[:, :].rearrange("p (c f) -> p c f", c=8)[
                    :, :, tt * 512:(tt + 1) * 512]), 16)
            for im in range(16):
                slot = len(w1_marks) % 2
                sw = s["sld"] if slot == 0 else s["sldB"]
                p.wait("sync", s["mmA"], w1_marks[-2] if len(w1_marks) >= 2 else mm0)
                lw = p.ei("sync", sw, lambda q, im=im, slot=slot: q.dma_start(
                    out=W1B[:, slot * 1024: slot * 1024 + 1024],
                    in_=self.dram["W1"][im]), 16)
                p.wait("tensor", sw, lw)
                if len(gel_marks) >= 2:
                    p.wait("tensor", s["scA"], gel_marks[-2])
                pg = im % 2
                def mkw1(q, slot=slot, pg=pg):
                    for k in range(NCH):
                        ins = q.matmul(self.PS[pg][:, :],
                                       W1B[:, slot * 1024 + k * 128: slot * 1024 + (k + 1) * 128],
                                       H2[:, k * 512:(k + 1) * 512],
                                       start=(k == 0), stop=(k == NCH - 1))
                    return ins
                mv = p.ei("tensor", s["mmA"], mkw1)
                w1_marks.append(mv)
                p.wait("scalar", s["mmA"], mv)
                p.drain("scalar")
                gv = p.ei("scalar", s["scA"], lambda q, im=im, pg=pg: q.activation(
                    GEL[:, im * 512:(im + 1) * 512], self.PS[pg][:, :], AF.Gelu,
                    bias=self.cpc("l1b1", im)))
                gel_marks.append(gv)
            p.wait("tensor", s["scA"], gel_marks[-1])
            p.wait("vector", s["srw"], x2v)
            for md in range(NCH):
                slot = len(w2_marks) % 2
                sw = s["sld2"] if slot == 0 else s["sld2B"]
                p.wait("sync", s["mmA"], w2_marks[-2] if len(w2_marks) >= 2 else mm0)
                lw = p.ei("sync", sw, lambda q, md=md, slot=slot: q.dma_start(
                    out=W2B2[:, slot * 2048: slot * 2048 + 2048],
                    in_=self.dram["W2"][md]), 16)
                p.wait("tensor", sw, lw)
                if len(fin_marks) >= 2:
                    p.wait("tensor", s["vA"], fin_marks[-2])
                pg = 2 + md % 2
                def mkw2(q, slot=slot, pg=pg):
                    for kh in range(16):
                        ins = q.matmul(self.PS[pg][:, :],
                                       W2B2[:, slot * 2048 + kh * 128: slot * 2048 + (kh + 1) * 128],
                                       GEL[:, kh * 512:(kh + 1) * 512],
                                       start=(kh == 0), stop=(kh == 15))
                    return ins
                mv = p.ei("tensor", s["mmA"], mkw2)
                w2_marks.append(mv)
                p.wait("vector", s["mmA"], mv)
                p.drain("vector")
                p.ei("vector", s["vA"], lambda q, md=md, pg=pg, tof=tof: q.scalar_tensor_tensor(
                    self.XSQ[:, :], self.PS[pg][:, :], self.cpc("l1b2", md),
                    self.X[:, md * T + tof: md * T + tof + 512], Al.add, Al.add))
                p.drain("vector")
                fv = p.ei("vector", s["vA"], lambda q, md=md, tof=tof: q.tensor_tensor(
                    self.X[:, md * T + tof: md * T + tof + 512], self.XSQ[:, :],
                    X2R[:, md * 512:(md + 1) * 512], Al.add))
                fin_marks.append(fv)
        p.run()
        p = Prog(nc)
        self.ln_stats2(p, self.X, ROWA, ROWB)
        self.ln_rows_finalize(p, ROWA, ROWB)
        self.ln_norm(p, self.X, self.X, ROWB, ROWA)
        p.run()

    def phase_bitffn_prep(self):
        nc, s = self.nc, self.sem
        p = Prog(nc)
        self.bit_ctx = ExitStack()
        SWC = self.bit_ctx.enter_context(nc.sbuf_tensor("SWC", [128, 16], F32))
        self.SWC = SWC
        WT = self.sbuf_local("WT", [128, 2 * 4096], F32R)
        TERNB = self.sbuf_local("TERNB", [128, 2 * 4096], BF16)
        TERN = self.sbuf_local("TERN", [128, 2 * 1024], F32R)
        MC = self.sbuf_local("MC", [128, 16], F32R)
        mm0 = s["mmA"].v
        p.drain("vector")
        p.ei("vector", s["vA"], lambda q: q.tensor_single_scalar(
            MC[:, 3:4], self.ONECOL[:, 0:1], 1e30, Al.is_ge))
        mats = [("wgateT", 8, FI, 0), ("wupT", 8, FI, 1), ("wdownT", 32, D, 2)]
        ab_marks = []
        idx = 0
        for name, ntile, width, j in mats:
            p.drain("vector")
            p.ei("vector", s["vA"], lambda q, j=j: q.tensor_single_scalar(
                MC[:, j:j + 1], self.ONECOL[:, 0:1], 1e30, Al.is_ge))
            for m in range(ntile):
                slot = idx % 2
                ring = "sync" if slot == 0 else "scalar"
                sl = s["sld"] if slot == 0 else s["sld2"]
                p.wait(ring, s["scA"], ab_marks[-2] if len(ab_marks) >= 2 else 0)
                lw = p.ei(ring, sl, lambda q, name=name, m=m, width=width, slot=slot:
                          q.dma_start(out=WT[:, slot * 4096: slot * 4096 + width],
                                      in_=self.dram[name][m * 128:(m + 1) * 128, :]), 16)
                p.wait("scalar", sl, lw)
                p.wait("scalar", s["vA"])
                p.drain("scalar")
                sv = p.ei("scalar", s["scA"], lambda q, width=width, slot=slot: q.activation(
                    WT[:, slot * 4096: slot * 4096 + width],
                    WT[:, slot * 4096: slot * 4096 + width], AF.Abs,
                    accum_out=MC[:, 15:16]))
                ab_marks.append(sv)
                idx += 1
                p.wait("vector", s["scA"], sv)
                p.drain("vector")
                p.ei("vector", s["vA"], lambda q, j=j: q.tensor_tensor(
                    MC[:, j:j + 1], MC[:, j:j + 1], MC[:, 15:16], Al.add))
        vv = s["vA"].v
        p.wait("tensor", s["vA"], vv)
        mv = p.ei("tensor", s["mmA"], lambda q: q.matmul(
            self.PS[6][0:1, 0:4], self.ONECOL[:, :], MC[:, 0:4], start=True, stop=True))
        p.wait("vector", s["mmA"], mv)
        p.drain("vector")
        vv = p.ei("vector", s["vA"], lambda q: q.tensor_copy(
            MC[0:1, 4:8], self.PS[6][0:1, 0:4]))
        p.wait("tensor", s["vA"], vv)
        mv = p.ei("tensor", s["mmA"], lambda q: q.matmul(
            self.PS[6][:, 0:4], self.ONEROW[0:1, :], MC[0:1, 4:8], start=True, stop=True))
        p.wait("vector", s["mmA"], mv)
        p.drain("vector")
        p.ei("vector", s["vA"], lambda q: q.tensor_scalar(
            SWC[:, 4:6], self.PS[6][:, 0:2], 1.0 / (FI * D), EPS, Al.mult, Al.max))
        p.drain("vector")
        p.ei("vector", s["vA"], lambda q: q.tensor_scalar(
            SWC[:, 6:7], self.PS[6][:, 2:3], 1.0 / (FI * D), EPS, Al.mult, Al.max))
        p.drain("vector")
        p.ei("vector", s["vA"], lambda q: q.reciprocal(SWC[:, 0:3], SWC[:, 4:7]))
        p.drain("vector")
        p.ei("vector", s["vA"], lambda q: q.tensor_tensor(
            SWC[:, 7:8], SWC[:, 0:1], SWC[:, 0:1], Al.mult))
        p.drain("vector")
        vv = p.ei("vector", s["vA"], lambda q: q.tensor_tensor(
            SWC[:, 7:8], SWC[:, 7:8], SWC[:, 1:2], Al.mult))
        st_marks = [0, 0]
        q_marks = [vv]
        ts1_marks = []
        ab_last = ab_marks[-1]
        qmats = [("wgateT", self.qwg_t, 8, FI, 0, True), ("wupT", self.qwu_t, 8, FI, 1, True),
                 ("wdownT", self.qwd_t, 32, D, 2, False)]
        tiles = []
        for name, dst, ntile, width, j, isb in qmats:
            for m in range(ntile):
                tiles.append((name, dst, m, width, j, isb))
        for i, (name, dst, m, width, j, isb) in enumerate(tiles):
            slot = i % 2
            ring = "sync" if slot == 0 else "scalar"
            sl = s["sld"] if slot == 0 else s["sld2"]
            if i >= 2:
                p.wait(ring, s["vA"], q_marks[-2])
            else:
                p.wait(ring, s["scA"], ab_last)
            lw = p.ei(ring, sl, lambda q, name=name, m=m, width=width, slot=slot:
                      q.dma_start(out=WT[:, slot * 4096: slot * 4096 + width],
                                  in_=self.dram[name][m * 128:(m + 1) * 128, :]), 16)
            p.wait("scalar", sl, lw)
            p.wait("scalar", s["vA"], q_marks[0])
            p.drain("scalar")
            sv = p.ei("scalar", s["scA"], lambda q, width=width, slot=slot, j=j: q.activation(
                WT[:, :].bitcast(F32)[:, slot * 4096: slot * 4096 + width],
                WT[:, slot * 4096: slot * 4096 + width], AF.Copy,
                scale=SWC[:, j:j + 1], bias=MAGIC))
            ts1_marks.append(sv)
            p.wait("vector", s["scA"], sv)
            sstm = s["sst"] if slot == 0 else s["srw"]
            p.wait("vector", sstm, st_marks[-2])
            p.drain("vector")
            p.ei("vector", s["vA"], lambda q, width=width, slot=slot: q.tensor_scalar(
                WT[:, slot * 4096: slot * 4096 + width],
                WT[:, :].bitcast(F32)[:, slot * 4096: slot * 4096 + width],
                -MAGIC, -1.0, Al.add, Al.max))
            p.drain("vector")
            if isb:
                qv = p.ei("vector", s["vA"], lambda q, width=width, slot=slot: q.tensor_scalar(
                    TERNB[:, slot * 4096: slot * 4096 + width],
                    WT[:, slot * 4096: slot * 4096 + width], 1.0, None, Al.min))
            else:
                qv = p.ei("vector", s["vA"], lambda q, width=width, slot=slot: q.tensor_scalar(
                    TERN[:, slot * 1024: slot * 1024 + width],
                    WT[:, slot * 4096: slot * 4096 + width], 1.0, None, Al.min))
            q_marks.append(qv)
            p.wait("scalar", s["vA"], qv)
            if isb:
                sv2 = p.ei("scalar", sstm, lambda q, dst=dst, m=m, slot=slot: q.dma_start(
                    out=dst[:, :, m * 128:(m + 1) * 128].rearrange("a p f -> p a f"),
                    in_=TERNB[:, slot * 4096: slot * 4096 + 4096].rearrange(
                        "p (a f) -> p a f", f=128)), 16)
            else:
                quar, kk = m // 8, m % 8
                sv2 = p.ei("scalar", sstm, lambda q, dst=dst, quar=quar, kk=kk, slot=slot:
                           q.dma_start(
                    out=dst[:, :, quar * 1024 + kk * 128: quar * 1024 + kk * 128 + 128].rearrange(
                        "a p f -> p a f"),
                    in_=TERN[:, slot * 1024: slot * 1024 + 1024].rearrange(
                        "p (a f) -> p a f", f=128)), 16)
            st_marks.append(sv2)
        p.wait("scalar", sstm, st_marks[-1])
        p.wait("scalar", s["sst"] if (len(tiles) - 1) % 2 == 1 else s["srw"], st_marks[-2])

        if DBGOUT:
            p.wait("scalar", s["vA"])
            p.ei("scalar", s["sst"], lambda q: q.dma_start(
                out=self.dbg_swc, in_=SWC[:, :]), 16)
            p.ei("scalar", s["sst"], lambda q: q.dma_start(
                out=self.dbg_mc, in_=MC[:, :].bitcast(F32)), 16)
            p.ei("scalar", s["sst"], lambda q: q.dma_start(
                out=self.dbg_tern, in_=TERN[:, :].bitcast(F32)), 16)
            p.ei("scalar", s["sst"], lambda q: q.dma_start(
                out=self.dbg_wt, in_=WT[:, :].bitcast(F32)), 16)
            p.wait("scalar", s["sst"], s["sst"].v)
        p.run()

    def phase_bitffn(self):
        nc, s = self.nc, self.sem
        SWC = self.SWC
        p = Prog(nc)
        QXB = self.sbuf_local("QXB", [128, NCH * T], BF16)
        QWGT = self.sbuf_local("QWGT", [128, 2 * 1024], BF16)
        QWUT = self.sbuf_local("QWUT", [128, 2 * 1024], BF16)
        QWDT = self.sbuf_local("QWDT", [128, 2 * 1024], F32R)
        HMX = self.sbuf_local("HMX", [128, 512], BF16)
        HMN = self.sbuf_local("HMN", [128, 512], BF16)
        SXC = self.sbuf_local("SXC", [128, 48], F32R)
        SXROW = self.sbuf_local("SXROW", [1, 512], F32R)
        SHROW = self.sbuf_local("SHROW", [1, 512], F32R)
        OSCROW = self.sbuf_local("OSCROW", [1, 512], F32R)
        mm0 = s["mmA"].v

        AMAX = self.ACC[:, 0:2048]
        NMIN = self.ACC[:, 2048:4096]
        p.drain("vector")
        p.ei("vector", s["vA"], lambda q: q.tensor_tensor(
            AMAX, self.X[:, 0:T], self.X[:, T:2 * T], Al.max))
        p.drain("vector")
        p.ei("vector", s["vA"], lambda q: q.tensor_tensor(
            NMIN, self.X[:, 0:T], self.X[:, T:2 * T], Al.min))
        for cch in range(2, NCH):
            p.drain("vector")
            p.ei("vector", s["vA"], lambda q, cch=cch: q.tensor_tensor(
                AMAX, AMAX, self.X[:, cch * T:(cch + 1) * T], Al.max))
            p.drain("vector")
            p.ei("vector", s["vA"], lambda q, cch=cch: q.tensor_tensor(
                NMIN, NMIN, self.X[:, cch * T:(cch + 1) * T], Al.min))
        p.drain("vector")
        p.ei("vector", s["vA"], lambda q: q.tensor_scalar(
            NMIN, NMIN, -1.0, None, Al.mult))
        p.drain("vector")
        p.ei("vector", s["vA"], lambda q: q.tensor_tensor(AMAX, AMAX, NMIN, Al.max))
        vv = s["vA"].v
        p.wait("tensor", s["vA"], vv)
        for tc in range(16):
            bk2 = 6 + (tc % 2)
            p.wait("tensor", s["vB"], s["vB"].v - 1)
            mvt = p.ei("tensor", s["mmB"], lambda q, tc=tc, bk2=bk2: q.transpose(
                self.PS[bk2][:, 0:128].bitcast(F32R),
                self.ACC[:, tc * 128:(tc + 1) * 128], self.IDR[:, :]))
            p.wait("vector", s["mmB"], mvt)
            p.drain("vector")
            p.ei("vector", s["vB"], lambda q, tc=tc, bk2=bk2: q.tensor_reduce(
                SXC[:, tc:tc + 1], self.PS[bk2][:, 0:128], AX, Al.max))
        p.drain("vector")
        p.ei("vector", s["vB"], lambda q: q.tensor_scalar(
            SXC[:, 16:32], SXC[:, 0:16], EPS, None, Al.max))
        p.drain("vector")
        p.ei("vector", s["vB"], lambda q: q.reciprocal(SXC[:, 32:48], SXC[:, 16:32]))
        p.drain("vector")
        vv = p.ei("vector", s["vB"], lambda q: q.tensor_scalar(
            SXC[:, 32:48], SXC[:, 32:48], 127.0, None, Al.mult))
        p.wait("tensor", s["vB"], vv)
        for tt in range(4):
            tof = tt * 512
            for sc4 in range(4):
                tc = tt * 4 + sc4
                bk2 = 6 + (sc4 % 2)
                p.wait("tensor", s["vB"], s["vB"].v - 1)
                mvt = p.ei("tensor", s["mmB"], lambda q, tc=tc, bk2=bk2: q.transpose(
                    self.PS[bk2][0:1, 0:128].bitcast(F32R),
                    SXC[:, 32 + tc:33 + tc].bitcast(F32R), self.IDR[:, :]))
                p.wait("vector", s["mmB"], mvt)
                p.drain("vector")
                p.ei("vector", s["vB"], lambda q, sc4=sc4, bk2=bk2: q.tensor_copy(
                    SXROW[0:1, sc4 * 128:(sc4 + 1) * 128], self.PS[bk2][0:1, 0:128]))
            p.wait("tensor", s["vB"])
            p.wait("tensor", s["vA"])
            mv = p.ei("tensor", s["mmA"], lambda q: q.matmul(
                self.PS[4][:, :], self.ONEROW[0:1, :], SXROW[0:1, :],
                start=True, stop=True))
            p.wait("vector", s["mmA"], mv)
            for cch in range(NCH):
                p.drain("vector")
                p.ei("vector", s["vA"], lambda q, cch=cch, tof=tof: q.tensor_tensor(
                    self.NTMP[:, :], self.X[:, cch * T + tof: cch * T + tof + 512],
                    self.PS[4][:, :], Al.mult))
                p.drain("vector")
                p.ei("vector", s["vA"], lambda q, cch=cch, tof=tof: q.tensor_scalar(
                    QXB[:, cch * T + tof: cch * T + tof + 512], self.NTMP[:, :],
                    MAGIC, -MAGIC, Al.add, Al.add))
            p.wait("tensor", s["vA"], s["vA"].v)
        gq_marks = []
        dn_marks = []
        fin_marks = []
        for tt in range(4):
            tof = tt * 512
            hm_marks = []
            for m in range(32):
                slot = len(gq_marks) % 2
                sw1 = s["sld"] if slot == 0 else s["sldB"]
                sw2 = s["gld"] if slot == 0 else s["gldB"]
                pace = gq_marks[-2] if len(gq_marks) >= 2 else mm0
                p.wait("sync", s["mmA"], pace)
                p.wait("gpsimd", s["mmA"], pace)
                l1 = p.ei("sync", sw1, lambda q, m=m, slot=slot: q.dma_start(
                    out=QWGT[:, slot * 1024: slot * 1024 + 1024],
                    in_=self.qwg_t[m]), 16)
                l2 = p.ei("gpsimd", sw2, lambda q, m=m, slot=slot: q.dma_start(
                    out=QWUT[:, slot * 1024: slot * 1024 + 1024],
                    in_=self.qwu_t[m]), 16)
                p.wait("tensor", sw1, l1)
                p.wait("tensor", sw2, l2)
                if len(hm_marks) >= 2:
                    p.wait("tensor", s["vA"], hm_marks[-2])
                pg0, pg1 = (0, 1) if m % 2 == 0 else (2, 3)
                def mkgu(q, tof=tof, slot=slot, pg0=pg0, pg1=pg1):
                    for k in range(NCH):
                        st, sp = (k == 0), (k == NCH - 1)
                        q.matmul(self.PS[pg0][:, :],
                                 QWGT[:, slot * 1024 + k * 128: slot * 1024 + (k + 1) * 128],
                                 QXB[:, k * T + tof: k * T + tof + 512], start=st, stop=sp)
                        ins = q.matmul(self.PS[pg1][:, :],
                                       QWUT[:, slot * 1024 + k * 128: slot * 1024 + (k + 1) * 128],
                                       QXB[:, k * T + tof: k * T + tof + 512], start=st, stop=sp)
                    return ins
                if m == 0:
                    p.wait("tensor", s["scA"], s["scA"].v)
                    p.wait("tensor", s["vA"], s["vA"].v)
                mv = p.ei("tensor", s["mmA"], mkgu)
                gq_marks.append(mv)
                p.wait("vector", s["mmA"], mv)
                if m == 0:
                    p.drain("vector")
                p.ei("vector", s["vA"], lambda q, pg0=pg0: q.tensor_relu(
                    self.XSQ[:, :], self.PS[pg0][:, :]))
                vvr = s["vA"].v
                p.wait("scalar", s["vA"], vvr)
                p.drain("scalar")
                sv = p.ei("scalar", s["scA"], lambda q: q.activation(
                    self.NTMP[:, :], self.XSQ[:, :], AF.Square))
                p.wait("vector", s["scA"], sv)
                hv = p.ei("vector", s["vA"], lambda q, m=m, pg1=pg1: q.tensor_tensor(
                    self.ACC[:, m * 512:(m + 1) * 512], self.NTMP[:, :],
                    self.PS[pg1][:, :], Al.mult))
                hm_marks.append(hv)
                p.drain("vector")
                if m == 0:
                    p.ei("vector", s["vA"], lambda q: q.tensor_copy(
                        HMX[:, :], self.ACC[:, 0:512]))
                    p.ei("vector", s["vA"], lambda q: q.tensor_copy(
                        HMN[:, :], self.ACC[:, 0:512]))
                else:
                    p.ei("vector", s["vA"], lambda q, m=m: q.tensor_tensor(
                        HMX[:, :], HMX[:, :], self.ACC[:, m * 512:(m + 1) * 512], Al.max))
                    p.ei("vector", s["vA"], lambda q, m=m: q.tensor_tensor(
                        HMN[:, :], HMN[:, :], self.ACC[:, m * 512:(m + 1) * 512], Al.min))
            p.drain("vector")
            p.ei("vector", s["vA"], lambda q: q.tensor_scalar(
                HMN[:, :], HMN[:, :], -1.0, None, Al.mult))
            p.drain("vector")
            hmv = p.ei("vector", s["vA"], lambda q: q.tensor_tensor(
                HMX[:, :], HMX[:, :], HMN[:, :], Al.max))
            p.wait("tensor", s["vA"], hmv)
            for sc4 in range(4):
                bk2 = 6 + (sc4 % 2)
                p.wait("tensor", s["vB"], s["vB"].v - 1)
                mvt = p.ei("tensor", s["mmB"], lambda q, sc4=sc4, bk2=bk2: q.transpose(
                    self.PS[bk2][:, :].bitcast(BF16)[:, 0:128],
                    HMX[:, sc4 * 128:(sc4 + 1) * 128], self.IDB[:, :]))
                p.wait("vector", s["mmB"], mvt)
                p.drain("vector")
                p.ei("vector", s["vB"], lambda q, sc4=sc4, bk2=bk2: q.tensor_reduce(
                    SXC[:, sc4:sc4 + 1], self.PS[bk2][:, :].bitcast(BF16)[:, 0:128],
                    AX, Al.max))
            p.drain("vector")
            p.ei("vector", s["vB"], lambda q, tt=tt: q.tensor_tensor(
                SXC[:, 4:8], SXC[:, 32 + tt * 4: 36 + tt * 4],
                SXC[:, 32 + tt * 4: 36 + tt * 4], Al.mult))
            p.drain("vector")
            p.ei("vector", s["vB"], lambda q, tt=tt: q.tensor_tensor(
                SXC[:, 4:8], SXC[:, 4:8], SXC[:, 32 + tt * 4: 36 + tt * 4], Al.mult))
            p.drain("vector")
            p.ei("vector", s["vB"], lambda q: q.tensor_scalar(
                SXC[:, 4:8], SXC[:, 4:8], SWC[:, 7:8], None, Al.mult))
            p.drain("vector")
            p.ei("vector", s["vB"], lambda q: q.tensor_scalar(
                SXC[:, 8:12], SXC[:, 4:8], EPS, None, Al.mult))
            p.drain("vector")
            p.ei("vector", s["vB"], lambda q: q.tensor_tensor(
                SXC[:, 12:16], SXC[:, 8:12], SXC[:, 0:4], Al.max))
            p.drain("vector")
            p.ei("vector", s["vB"], lambda q: q.reciprocal(SXC[:, 0:4], SXC[:, 12:16]))
            p.drain("vector")
            p.ei("vector", s["vB"], lambda q: q.tensor_scalar(
                SXC[:, 0:4], SXC[:, 0:4], 127.0, None, Al.mult))
            p.drain("vector")
            p.ei("vector", s["vB"], lambda q: q.reciprocal(SXC[:, 8:12], SXC[:, 4:8]))
            p.drain("vector")
            p.ei("vector", s["vB"], lambda q: q.tensor_tensor(
                SXC[:, 16:20], SXC[:, 12:16], SXC[:, 8:12], Al.mult))
            p.drain("vector")
            p.ei("vector", s["vB"], lambda q: q.tensor_scalar(
                SXC[:, 16:20], SXC[:, 16:20], SWC[:, 6:7], 1.0 / 127.0,
                Al.mult, Al.mult))
            vv = s["vB"].v
            p.wait("tensor", s["vB"], vv)
            for sc4 in range(4):
                bk2 = 6 + (sc4 % 2)
                p.wait("tensor", s["vB"], s["vB"].v - 1)
                mvt = p.ei("tensor", s["mmB"], lambda q, sc4=sc4, bk2=bk2: q.transpose(
                    self.PS[bk2][0:1, 0:128].bitcast(F32R),
                    SXC[:, sc4:sc4 + 1].bitcast(F32R), self.IDR[:, :]))
                p.wait("vector", s["mmB"], mvt)
                p.drain("vector")
                p.ei("vector", s["vB"], lambda q, sc4=sc4, bk2=bk2: q.tensor_copy(
                    SHROW[0:1, sc4 * 128:(sc4 + 1) * 128], self.PS[bk2][0:1, 0:128]))
                p.wait("tensor", s["vB"])
                mvt = p.ei("tensor", s["mmB"], lambda q, sc4=sc4, bk2=bk2: q.transpose(
                    self.PS[bk2][0:1, 0:128].bitcast(F32R),
                    SXC[:, 16 + sc4:17 + sc4].bitcast(F32R), self.IDR[:, :]))
                p.wait("vector", s["mmB"], mvt)
                p.drain("vector")
                p.ei("vector", s["vB"], lambda q, sc4=sc4, bk2=bk2: q.tensor_copy(
                    OSCROW[0:1, sc4 * 128:(sc4 + 1) * 128], self.PS[bk2][0:1, 0:128]))
            vv = s["vB"].v
            p.wait("tensor", s["vB"], vv)
            mv = p.ei("tensor", s["mmA"], lambda q: q.matmul(
                self.PS[4][:, :], self.ONEROW[0:1, :], SHROW[0:1, :],
                start=True, stop=True))
            mv2 = p.ei("tensor", s["mmA"], lambda q: q.matmul(
                self.PS[5][:, :], self.ONEROW[0:1, :], OSCROW[0:1, :],
                start=True, stop=True))
            p.wait("vector", s["mmA"], mv2)
            p.wait("scalar", s["mmA"], mv2)
            osv = s["mmA"].v
            qh_marks = []
            for m in range(32):
                scr = self.NTMP if m % 2 == 0 else self.XSQ
                if m >= 2:
                    p.drain("vector")
                p.ei("vector", s["vA"], lambda q, m=m, scr=scr: q.tensor_tensor(
                    scr[:, :], self.ACC[:, m * 512:(m + 1) * 512],
                    self.PS[4][:, :], Al.mult))
                p.drain("vector")
                qh_marks.append(p.ei("vector", s["vA"], lambda q, m=m, scr=scr: q.tensor_scalar(
                    self.ACC[:, m * 512:(m + 1) * 512], scr[:, :],
                    MAGIC, -MAGIC, Al.add, Al.add)))
            for md in range(NCH):
                pg = 2 + (md % 2)
                p.wait("tensor", s["scA"])
                for quar in range(4):
                    dslot = len(dn_marks) % 2
                    sd = s["srw"] if dslot == 0 else s["sld2"]
                    p.wait("sync", s["mmA"], dn_marks[-2] if len(dn_marks) >= 2 else mm0)
                    lw = p.ei("sync", sd, lambda q, md=md, quar=quar, dslot=dslot: q.dma_start(
                        out=QWDT[:, dslot * 1024: dslot * 1024 + 1024],
                        in_=self.qwd_t[md][:, quar * 1024:(quar + 1) * 1024]), 16)
                    p.wait("tensor", sd, lw)
                    p.wait("tensor", s["vA"], qh_marks[quar * 8 + 7])
                    def mkdn(q, md=md, quar=quar, pg=pg, dslot=dslot):
                        for kk in range(8):
                            kh = quar * 8 + kk
                            ins = q.matmul(self.PS[pg][:, :],
                                           QWDT[:, dslot * 1024 + kk * 128: dslot * 1024 + (kk + 1) * 128],
                                           self.ACC[:, kh * 512:(kh + 1) * 512],
                                           start=(kh == 0), stop=(kh == 31))
                        return ins
                    mv = p.ei("tensor", s["mmA"], mkdn)
                    dn_marks.append(mv)
                if fin_marks:
                    p.wait("scalar", s["vA"], fin_marks[-1][0])
                p.wait("scalar", s["mmA"], dn_marks[-1])
                p.drain("scalar")
                scv = p.ei("scalar", s["scA"], lambda q, pg=pg: q.copy(
                    self.XSQ[:, :], self.PS[pg][:, :]))
                p.wait("vector", s["scA"], scv)
                p.drain("vector")
                p.ei("vector", s["vA"], lambda q: q.tensor_tensor(
                    self.XSQ[:, :], self.XSQ[:, :], self.PS[5][:, :], Al.mult))
                p.drain("vector")
                fv = p.ei("vector", s["vA"], lambda q, md=md, tof=tof: q.tensor_tensor(
                    self.X[:, md * T + tof: md * T + tof + 512],
                    self.X[:, md * T + tof: md * T + tof + 512],
                    self.XSQ[:, :], Al.add))
                fin_marks.append((fv, scv))
        p.run()

    def phase_final(self):
        nc, s = self.nc, self.sem
        p = Prog(nc)
        ROWA = self.sbuf_local("ROWf1", [1, T], F32R)
        ROWB = self.sbuf_local("ROWf2", [1, T], F32R)
        OTILE = self.sbuf_local("OTILE", [128, D], F32)
        self.ln_stats2(p, self.X, ROWA, ROWB)
        self.ln_rows_finalize(p, ROWA, ROWB)
        self.ln_norm(p, self.X, self.X, ROWB, ROWA)
        for i in range(16):
            p.wait("vector", s["sst"])
            for cch in range(NCH):
                bk = 6 + (cch % 2)
                p.wait("tensor", s["vB"], s["vB"].v - 1)
                p.wait("tensor", s["vA"])
                mv = p.ei("tensor", s["mmB"], lambda q, i=i, cch=cch, bk=bk: q.transpose(
                    self.PS[bk][:, 0:128].bitcast(F32R),
                    self.X[:, cch * T + i * 128: cch * T + (i + 1) * 128], self.IDR[:, :]))
                p.wait("vector", s["mmB"], mv)
                p.drain("vector")
                p.ei("vector", s["vB"], lambda q, cch=cch, bk=bk: q.tensor_copy(
                    OTILE[:, cch * 128:(cch + 1) * 128], self.PS[bk][:, 0:128]))
            p.wait("scalar", s["vB"], s["vB"].v)
            p.ei("scalar", s["sst"], lambda q, i=i: q.dma_start(
                out=self.out[i * 128:(i + 1) * 128, :], in_=OTILE[:, :]), 16)
        p.wait("scalar", s["sst"], s["sst"].v)
        p.run()


def build_nc():
    b = Builder()
    b.conv_ep_marks = {}
    return b.build()


def kernel(**inputs):
    in_maps = _host_layout(inputs)
    nc = build_nc()
    res = run_bass_kernel_spmd(nc, in_maps, core_ids=list(range(8)))
    full = np.zeros((4, 4096, D), np.float32)
    for core in range(8):
        b, half = core // 2, core % 2
        full[b, half * 2048:(half + 1) * 2048] = res.results[core]["out"]
    return full
